# version 05
# baseline (speedup 1.0000x reference)
import numpy as np
import concourse.bass as bass
import concourse.mybir as mybir
from concourse.bass_utils import run_bass_kernel_spmd

F32 = mybir.dt.float32
I32 = mybir.dt.int32
BF16 = mybir.dt.bfloat16
AF = mybir.ActivationFunctionType
ALU = mybir.AluOpType
AX = mybir.AxisListType.X
AP = bass.AP

B, TFULL, D = 2, 8192, 1024
T = 2048
HALO = 31
TE = T + HALO
NCH = 8
NH = 8
HD = 128
NE = 8
CH = 512
NCHUNK = T // CH


def hilo(v):
    v = np.asarray(v, dtype=np.float64)
    hi = v.astype(np.float32)
    lo = (v - hi.astype(np.float64)).astype(np.float32)
    return hi, lo


class Prog:
    P9MODE = "full"

    def __init__(self, debug=(), phases=99):
        self.nc = bass.Bass("TRN2", target_bir_lowering=False)
        self.debug = dict(debug)
        self.phases = phases
        self._keep = []

    def sbuf(self, name, shape, dtype):
        cm = self.nc.sbuf_tensor(name, list(shape), dtype)
        t = cm.__enter__()
        self._keep.append(cm)
        return t

    def psum(self, name, shape, dtype=F32):
        cm = self.nc.psum_tensor(name, list(shape), dtype)
        t = cm.__enter__()
        self._keep.append(cm)
        return t

    def sem(self, name):
        cm = self.nc.semaphore(name)
        h = cm.__enter__()
        self._keep.append(cm)
        return h

    def dst(self, eng, sname, out, in_, **kw):
        nc = self.nc
        n0 = nc.n_instructions()
        eng.dma_start(out, in_, **kw).then_inc(getattr(self, sname), 16)
        cnt = 0
        for nm in nc.inst_names()[n0:]:
            inst = nc.find_inst_by_name(nm)
            if inst is not None and type(inst).__name__ == "InstDMACopy":
                cnt += 1
        assert cnt >= 1
        setattr(self, sname + "_n", getattr(self, sname + "_n") + 16 * cnt)
        return getattr(self, sname + "_n")

    class _Scope:
        def __init__(self, prog):
            self.prog = prog
            self.cms = []

        def sbuf(self, name, shape, dtype):
            cm = self.prog.nc.sbuf_tensor(name, list(shape), dtype)
            t = cm.__enter__()
            self.cms.append(cm)
            return t

        def __enter__(self):
            return self

        def __exit__(self, *a):
            for cm in reversed(self.cms):
                cm.__exit__(None, None, None)
            return False

    def scope(self):
        return Prog._Scope(self)

    def declare_io(self):
        nc = self.nc
        I = lambda n, s, dt=F32: nc.dram_tensor(n, list(s), dt, kind="ExternalInput")
        O = lambda n, s, dt=F32: nc.dram_tensor(n, list(s), dt, kind="ExternalOutput")
        N = lambda n, s, dt=F32: nc.dram_tensor(n, list(s), dt)
        self.x_in = I("x_fm", [NCH, 128, TE])
        self.encq = I("encq", [NH, NCH, 128, HD])
        self.encv = I("encv", [NCH, 128, NH * HD])
        self.dec = I("dec", [NCH, 128, D])
        self.hebw = I("heb", [NH, HD, HD])
        self.mw1 = I("mw1", [NCH, 128, 512])
        self.mw2 = I("mw2", [4, 128, D])
        self.mb1 = I("mb1", [128, 4])
        self.mb2 = I("mb2", [128, 8])
        self.bmet = I("bmet", [128, NH])
        self.w0T = I("w0T", [NCH, 128, D])
        self.b0c = I("b0c", [128, NCH])
        self.wqkvT = I("wqkvT", [D, 3 * D])
        self.bq_c = I("bq_c", [128, NCH])
        self.bk_c = I("bk_c", [128, NCH])
        self.bkv_r = I("bkv_r", [1, 2 * D])
        self.gwc = I("gw", [NCH, 128, NE])
        self.gbc = I("gb", [NE, 1])
        self.w1 = I("w1", [NE, NCH, 128, D])
        self.w2 = I("w2", [NE, NCH, 128, D])
        self.w3 = I("w3", [NE, NCH, 128, D])
        self.b1c = I("b1c", [NE, 128, NCH])
        self.b2c = I("b2c", [NE, 128, NCH])
        self.b3r = I("b3r", [NE, 1, D])
        self.id128 = I("id128", [128, 128], BF16)
        self.idf1 = I("idf1", [1, 1])
        self.ones_col = I("ones_col", [128, 1], BF16)
        self.maskbigA = I("maskbigA", [128, 128])
        self.maskbigB = I("maskbigB", [31, 128])
        self.tri_in = I("tri", [128, 128], BF16)
        self.dk_hi = I("dk_hi", [128, 16]); self.dk_lo = I("dk_lo", [128, 16])
        self.di_hi = I("di_hi", [128, 16]); self.di_lo = I("di_lo", [128, 16])
        self.drm_hi = I("drm_hi", [128, 4]); self.drm_lo = I("drm_lo", [128, 4])
        self.drm_mk = I("drm_mk", [128, 4])
        self.bsel_in = I("bsel", [8, 1])
        self.b2_in = I("b2", [2, 1])
        self.ccol_in = I("ccol", [128, 8])
        self.ones128_in = I("ones128", [128, 128], BF16)
        self.idf8_in = I("idf8", [8, 8])
        self.idb8_in = I("idb8", [8, 8], BF16)
        self.idf128_in = I("idf128", [128, 128])
        self.onesrf_in = I("ones_rf", [1, 128])
        self.bsel2_in = I("bsel2", [128, 2])
        self.strided_off = 31

        self.cc_sig_i = N("cc_sig_i", [2, D]); self.cc_sig_o = N("cc_sig_o", [2, D])
        self.cc_gm_i = N("cc_gm_i", [2, NE]); self.cc_gm_o = N("cc_gm_o", [2, NE])
        self.cc_gs_i = N("cc_gs_i", [2, NE]); self.cc_gs_o = N("cc_gs_o", [2, NE])
        self.cc_ut_i = N("cc_ut_i", [128, NCH * D], BF16)
        self.cc_ut_o = N("cc_ut_o", [4 * 128, NCH * D], BF16)
        self.cc_vp_i = N("cc_vp_i", [NE, D]); self.cc_vp_o = N("cc_vp_o", [NE, D])
        self.x1_park = N("x1_park", [NCH, 128, T])
        self.a1_park = N("a1_park", [16, 128, D], BF16)
        self.out = O("out", [NCH, 128, T])
        self.dbg = {}
        for name, (shape, dt) in self.debug.items():
            self.dbg[name] = O(name, list(shape), dt)

    def build(self):
        self.declare_io()
        nc = self.nc
        self.sD = self.sem("sD"); self.sD_n = 0
        self.sG = self.sem("sG"); self.sG_n = 0
        self.sP = self.sem("sP"); self.sP_n = 0
        self.sS = self.sem("sS"); self.sS_n = 0
        self.sV = self.sem("sV"); self.sV_n = 0
        self.sC = self.sem("sC"); self.sC_n = 0
        self.sO = self.sem("sO"); self.sO_n = 0
        self.sW = self.sem("sW"); self.sW_n = 0
        self.sX = self.sem("sX"); self.sX_n = 0
        self.sY = self.sem("sY"); self.sY_n = 0
        self.sA = self.sem("sA"); self.sA_n = 0
        self.sB = self.sem("sB"); self.sB_n = 0
        self.sE = self.sem("sE"); self.sE_n = 0
        self.sF = self.sem("sF"); self.sF_n = 0
        self.sZ = self.sem("sZ"); self.sZ_n = 0
        self.sZ2 = self.sem("sZ2"); self.sZ2_n = 0

        self.pb = [self.psum(f"pb{i}", [128, 512]) for i in range(6)]
        self.pt = self.psum("pt", [128, 512], BF16)
        self.pf = self.psum("pf", [128, 512], F32)

        self.xnsb = self.sbuf("xnsb", [128, NCH * TE], BF16)
        self.scr = self.sbuf("scr", [128, TE], F32)
        self.cstb = self.sbuf("cstb", [128, 400], BF16)
        self.cstf = self.sbuf("cstf", [128, 672], F32)

        self.emit()
        nc.finalize()
        return nc

    def ID(self):  return self.cstb[:, 0:128]
    def OC(self):  return self.cstb[:, 128:129]
    def ONES128(self): return self.cstb[:, 136:264]
    def IDB8(self): return self.cstb[0:8, 264:272]
    def TRI(self): return self.cstb[:, 272:400]
    def MBA(self): return self.cstf[:, 0:128]
    def MBB(self): return self.cstf[0:31, 128:256]
    def DKM(self): return self.cstf[:, 256:272]
    def DIC(self): return self.cstf[:, 272:288]
    def DRM(self): return self.cstf[:, 288:292]
    def BSEL(self): return self.cstf[0:8, 292:293]
    def B2(self):  return self.cstf[0:2, 293:294]
    def IDF(self): return self.cstf[0:1, 294:295]
    def IDF8(self): return self.cstf[0:8, 296:304]

    def CCOL(self, i, parts=128):
        return self.cstf[0:parts, 304 + i: 305 + i]

    def BSEL2(self, parts=128):
        return self.cstf[0:parts, 312:314]

    def IDF128(self):
        return self.cstf[:, 384:512]

    def IDF32(self):
        return self.cstf[0:32, 384:416]

    def IDF2(self):
        return self.cstf[0:2, 384:386]

    def ONESRF(self):
        return self.cstf[0:1, 512:640]

    def xc(self, c, lo=0, n=TE):
        return self.xsb[:, c * TE + lo: c * TE + lo + n]

    def xn(self, c, lo=0, n=TE):
        return self.xnsb[:, c * TE + lo: c * TE + lo + n]

    def dbg_write(self, s, name, src_ap):
        if name in self.dbg:
            self.dst(s, "sO", self.dbg[name].ap(), src_ap)
            s.wait_ge(self.sO, self.sO_n)

    def bcast_row(self, row_fn, nblk, dst_fn, banks=(4, 5)):
        nc = self.nc
        with nc.Block() as blk:
            p0 = self.sP_n
            v0 = self.sV_n

            @blk.tensor
            def _(t):
                for i in range(nblk):
                    bank = self.pb[banks[i % 2]]
                    if i >= 2:
                        t.wait_ge(self.sV, v0 + (i - 2) + 1)
                    t.matmul(bank[:, 0:row_fn(i).ap[-1][1]], self.ONESRF(),
                             row_fn(i), start=True, stop=True).then_inc(self.sP, 1)
            self.sP_n += nblk

            @blk.vector
            def _(v):
                for i in range(nblk):
                    bank = self.pb[banks[i % 2]]
                    n = row_fn(i).ap[-1][1]
                    v.wait_ge(self.sP, p0 + i + 1)
                    v.tensor_copy(dst_fn(i), bank[:, 0:n]).then_inc(self.sV, 1)
                    v.drain()
            self.sV_n += nblk

    def rowcol(self, row_ap_fn, n, out_tile):
        nc = self.nc
        with nc.Block() as blk:
            @blk.tensor
            def _(t):
                for i in range(n):
                    tr = t.transpose(self.pf[:, i:i + 1], row_ap_fn(i), self.IDF())
                tr.then_inc(self.sP, 1)
            self.sP_n += 1

            @blk.vector
            def _(v):
                v.wait_ge(self.sP, self.sP_n)
                v.tensor_copy(out_tile, self.pf[:, 0:n]); v.drain()

    def emit(self):
        ph = self.phases
        with self.scope() as sc0:
            self.xsb = sc0.sbuf("xsb", [128, NCH * TE], F32)
            self.p0_load()
            with self.scope() as bsc:
                self.mbc = bsc.sbuf("mbc01", [128, TE], F32)
                self.rbc = bsc.sbuf("rbc01", [128, TE], F32)
                if ph >= 1: self.p1_ln("0")
                if ph >= 2: self.p2_ffn0()
            if ph >= 3: self.p3_attn0()
            if ph >= 4:
                with self.scope() as bsc:
                    self.mbc = bsc.sbuf("mbc4", [128, TE], F32)
                    self.rbc = bsc.sbuf("rbc4", [128, TE], F32)
                    self.p4_ln1()
            if ph < 9:
                self.p_final(parked=(ph >= 4))
        if ph >= 5:
            with self.scope() as scG:
                self.wcolsb = scG.sbuf("wcolsb", [128, 16 * NE], BF16)
                self.indb = scG.sbuf("indb", [NE, T], BF16)
                if ph >= 6: self.p6_gate()
                self.p5_qkv()
                if ph >= 7: self.p7_attn1()
                if ph >= 8: self.p8_moe()
                if ph >= 9: self.p9_assemble()

    def p0_load(self):
        nc = self.nc
        with nc.Block() as blk:
            @blk.sync
            def _(s):
                def ld(dst, src_ap):
                    self.dst(s, "sD", dst, src_ap)
                ld(self.xsb[:, :], AP(self.x_in, 0, [[TE, 128], [128 * TE, NCH], [1, TE]]))
                ld(self.ID(), self.id128[:, :])
                ld(self.OC(), self.ones_col[:, :])
                ld(self.MBA(), self.maskbigA[:, :])
                ld(self.MBB(), self.maskbigB[:, :])
                ld(self.TRI(), self.tri_in[:, :])
                ld(self.BSEL(), self.bsel_in[:, :])
                ld(self.B2(), self.b2_in[:, :])
                ld(self.IDF(), self.idf1[:, :])
                ld(self.cstf[:, 304:312], self.ccol_in[:, :])
                ld(self.ONES128(), self.ones128_in[:, :])
                ld(self.cstf[:, 384:400], self.dk_hi[:, :])
                ld(self.cstf[:, 400:416], self.di_hi[:, :])
                ld(self.cstf[:, 416:432], self.dk_lo[:, :])
                ld(self.cstf[:, 432:448], self.di_lo[:, :])
                ld(self.cstf[:, 448:452], self.drm_hi[:, :])
                ld(self.cstf[:, 452:456], self.drm_lo[:, :])
                ld(self.cstf[:, 456:460], self.drm_mk[:, :])
                s.wait_ge(self.sD, self.sD_n)
        with nc.Block() as blk:
            @blk.vector
            def _(v):
                v.tensor_add(self.DKM(), self.cstf[:, 384:400], self.cstf[:, 416:432]); v.drain()
                v.tensor_add(self.DIC(), self.cstf[:, 400:416], self.cstf[:, 432:448]); v.drain()
                v.tensor_add(self.DRM(), self.cstf[:, 448:452], self.cstf[:, 452:456]); v.drain()
        with nc.Block() as blk:
            @blk.scalar
            def _(sc):
                sc.activation(self.DKM(), self.DKM(), AF.Exp); sc.drain()
                sc.activation(self.DIC(), self.DIC(), AF.Exp); sc.drain()
                sc.activation(self.DRM(), self.DRM(), AF.Exp); sc.drain()
        with nc.Block() as blk:
            @blk.vector
            def _(v):
                v.tensor_scalar(self.DKM(), self.DKM(), 0.03125, None, op0=ALU.mult); v.drain()
                v.tensor_mul(self.DRM(), self.DRM(), self.cstf[:, 456:460]); v.drain()
        with nc.Block() as blk:
            @blk.sync
            def _(s):
                self.dst(s, "sD", self.IDF8(), self.idf8_in[:, :])
                self.dst(s, "sD", self.IDB8(), self.idb8_in[:, :])
                self.dst(s, "sD", self.IDF128(), self.idf128_in[:, :])
                self.dst(s, "sD", self.ONESRF(), self.onesrf_in[:, :])
                self.dst(s, "sD", self.cstf[:, 312:314], self.bsel2_in[:, :])
                s.wait_ge(self.sD, self.sD_n)

    def p1_ln(self, tag, ntok=TE, lo=0):
        nc = self.nc
        nbl = (ntok + 511) // 512
        lsc = self.scope(); lsc.__enter__()
        self.scrb = lsc.sbuf(f"scrb{tag}", [128, 2 * TE], BF16)
        self.rows = lsc.sbuf(f"rows{tag}", [1, 2 * TE], F32)
        with nc.Block() as blk:
            @blk.vector
            def _(v):
                for c in range(NCH):
                    v.tensor_copy(self.xn(c, lo, ntok), self.xc(c, lo, ntok)); v.drain()
        with nc.Block() as blk:
            @blk.tensor
            def _(t):
                for c in range(NCH):
                    for bi in range(nbl):
                        n = min(512, ntok - bi * 512)
                        t.matmul(self.pb[bi][0:1, 0:n], self.OC(),
                                 self.xn(c, lo + bi * 512, n),
                                 start=(c == 0), stop=(c == NCH - 1))
        with nc.Block() as blk:
            @blk.vector
            def _(v):
                for bi in range(nbl):
                    n = min(512, ntok - bi * 512)
                    v.tensor_scalar(self.rows[0:1, bi * 512: bi * 512 + n],
                                    self.pb[bi][0:1, 0:n], 1.0 / 1024.0, None,
                                    op0=ALU.mult); v.drain()
        with nc.Block() as blk:
            s_base = self.sS_n
            p_base = self.sP_n

            @blk.scalar
            def _(sc):
                for c in range(NCH):
                    if c >= 2:
                        sc.wait_ge(self.sP, p_base + c - 1)
                    half = (c % 2) * TE
                    sc.activation(self.scrb[:, half: half + ntok],
                                  self.xn(c, lo, ntok), AF.Square
                                  ).then_inc(self.sS, 1); sc.drain()
            self.sS_n += NCH

            @blk.tensor
            def _(t):
                for c in range(NCH):
                    t.wait_ge(self.sS, s_base + c + 1)
                    half = (c % 2) * TE
                    for bi in range(nbl):
                        n = min(512, ntok - bi * 512)
                        mm = t.matmul(self.pb[bi][0:1, 0:n], self.OC(),
                                      self.scrb[:, half + bi * 512: half + bi * 512 + n],
                                      start=(c == 0), stop=(c == NCH - 1))
                    mm.then_inc(self.sP, 1)
            self.sP_n += NCH
        with nc.Block() as blk:
            @blk.vector
            def _(v):
                for bi in range(nbl):
                    n = min(512, ntok - bi * 512)
                    v.tensor_scalar(self.rows[0:1, TE + bi * 512: TE + bi * 512 + n],
                                    self.pb[bi][0:1, 0:n], 1.0 / 1024.0, None,
                                    op0=ALU.mult); v.drain()
                v.tensor_mul(self.scr[0:1, 0:ntok],
                             self.rows[0:1, 0:ntok], self.rows[0:1, 0:ntok]); v.drain()
                v.tensor_sub(self.rows[0:1, TE: TE + ntok],
                             self.rows[0:1, TE: TE + ntok],
                             self.scr[0:1, 0:ntok]); v.drain()
        with nc.Block() as blk:
            @blk.scalar
            def _(sc):
                sc.activation(self.rows[0:1, TE: TE + ntok],
                              self.rows[0:1, TE: TE + ntok], AF.Sqrt,
                              bias=self.CCOL(0, 1))
                sc.drain()
        with nc.Block() as blk:
            @blk.vector
            def _(v):
                v.reciprocal(self.rows[0:1, TE: TE + ntok],
                             self.rows[0:1, TE: TE + ntok]); v.drain()
        nb = (ntok + 511) // 512
        self.bcast_row(lambda i: self.rows[0:1, i * 512: i * 512 + min(512, ntok - i * 512)],
                       nb, lambda i: self.mbc[:, i * 512: i * 512 + min(512, ntok - i * 512)])
        self.bcast_row(lambda i: self.rows[0:1, TE + i * 512: TE + i * 512 + min(512, ntok - i * 512)],
                       nb, lambda i: self.rbc[:, i * 512: i * 512 + min(512, ntok - i * 512)])
        with nc.Block() as blk:
            @blk.vector
            def _(v):
                for c in range(NCH):
                    v.tensor_sub(self.scr[:, 0:ntok], self.xc(c, lo, ntok),
                                 self.mbc[:, 0:ntok]); v.drain()
                    v.tensor_mul(self.xn(c, lo, ntok), self.scr[:, 0:ntok],
                                 self.rbc[:, 0:ntok]); v.drain()
        with nc.Block() as blk:
            @blk.sync
            def _(s):
                self.dbg_write(s, f"dbg_xn{tag}",
                               AP(self.xnsb, 0, [[NCH * TE, 128], [1, NCH * TE]]))
        lsc.__exit__(None, None, None)

    def p2_ffn0(self):
        nc = self.nc
        sc_ = self.scope(); sc_.__enter__()
        wq = sc_.sbuf("wq", [128, NCH * D], BF16)
        xq = sc_.sbuf("xq", [128, NCH * T], BF16)
        wamx = sc_.sbuf("wamx", [128, D], F32)
        tmp = sc_.sbuf("tmp_p2", [128, 2048], F32)
        itmp = sc_.sbuf("itmp_p2", [128, 2048], I32)
        wraw = sc_.sbuf("wraw_p2", [128, D], F32)
        swc = sc_.sbuf("swc", [128, NCH], F32)
        b0s = sc_.sbuf("b0s", [128, NCH], F32)

        for c in range(NCH):
            with nc.Block() as blk:
                @blk.sync
                def _(s):
                    self.dst(s, "sD", wraw[:, :], self.w0T[c])
                    if c == 0:
                        self.dst(s, "sD", b0s[:, :], self.b0c[:, :])
                    s.wait_ge(self.sD, self.sD_n)
            with nc.Block() as blk:
                cc = c

                @blk.vector
                def _(v):
                    v.tensor_scalar(tmp[:, 0:D], wraw[:, :], -1.0, None,
                                    op0=ALU.mult); v.drain()
                    v.tensor_tensor(tmp[:, 0:D], tmp[:, 0:D], wraw[:, :],
                                    op=ALU.max); v.drain()
                    if cc == 0:
                        v.tensor_copy(wamx[:, :], tmp[:, 0:D])
                    else:
                        v.tensor_tensor(wamx[:, :], wamx[:, :], tmp[:, 0:D],
                                        op=ALU.max)
                    v.drain()
        for g2_ in range(2):
            with nc.Block() as blk:
                gg = g2_
                p0 = self.sP_n

                @blk.tensor
                def _(t):
                    for j in range(4):
                        ocx = gg * 4 + j
                        tr = t.transpose(self.pf[:, j * 128:(j + 1) * 128],
                                         wamx[:, ocx * 128:(ocx + 1) * 128],
                                         self.IDF128())
                    tr.then_inc(self.sP, 1)
                self.sP_n += 1

                @blk.vector
                def _(v):
                    v.wait_ge(self.sP, self.sP_n)
                    for j in range(4):
                        ocx = gg * 4 + j
                        v.tensor_reduce(swc[:, ocx:ocx + 1],
                                        self.pf[:, j * 128:(j + 1) * 128], AX,
                                        ALU.max); v.drain()
        with nc.Block() as blk:
            @blk.vector
            def _(v):
                v.tensor_scalar(swc[:, :], swc[:, :], 1e-8, None, op0=ALU.max)
                v.drain()
        for half in range(2):
            with nc.Block() as blk:
                hh = half
                p0 = self.sP_n

                @blk.tensor
                def _(t):
                    for ocx in range(4):
                        tr = t.transpose(self.pf[0:1, ocx * 128:(ocx + 1) * 128],
                                         swc[:, hh * 4 + ocx: hh * 4 + ocx + 1],
                                         self.IDF128())
                    tr.then_inc(self.sP, 1)
                self.sP_n += 1

                @blk.vector
                def _(v):
                    v.wait_ge(self.sP, self.sP_n)
                    v.tensor_copy(self.scr[0:1, hh * 512: hh * 512 + 512],
                                  self.pf[0:1, 0:512]); v.drain()
        with nc.Block() as blk:
            @blk.vector
            def _(v):
                v.reciprocal(self.scr[0:1, 0:D], self.scr[0:1, 0:D]); v.drain()
        self.bcast_row(lambda i: self.scr[0:1, i * 512:(i + 1) * 512], 2,
                       lambda i: wamx[:, i * 512:(i + 1) * 512])
        with nc.Block() as blk:
            @blk.vector
            def _(v):
                v.tensor_scalar(swc[:, :], swc[:, :], 1.0 / 127.0, None,
                                op0=ALU.mult); v.drain()
        for c in range(NCH):
            with nc.Block() as blk:
                @blk.sync
                def _(s):
                    self.dst(s, "sD", wraw[:, :], self.w0T[c])
                    s.wait_ge(self.sD, self.sD_n)
            with nc.Block() as blk:
                cc = c

                @blk.vector
                def _(v):
                    cs = slice(cc * D, (cc + 1) * D)
                    v.tensor_mul(self.scr[:, 0:D], wraw[:, :], wamx[:, :]); v.drain()
                    v.tensor_scalar(self.scr[:, 0:D], self.scr[:, 0:D],
                                    127.0, 0.5, op0=ALU.mult, op1=ALU.add); v.drain()
                    v.tensor_copy(itmp[:, 0:D], self.scr[:, 0:D]); v.drain()
                    v.tensor_copy(tmp[:, 0:D], itmp[:, 0:D]); v.drain()
                    v.tensor_tensor(self.scr[:, 0:D], tmp[:, 0:D],
                                    self.scr[:, 0:D], op=ALU.is_gt); v.drain()
                    v.tensor_sub(wq[:, cs], tmp[:, 0:D],
                                 self.scr[:, 0:D]); v.drain()

        with nc.Block() as blk:
            @blk.vector
            def _(v):
                v.tensor_scalar(self.scr[:, 0:T], self.xn(0, HALO, T), -1.0,
                                None, op0=ALU.mult); v.drain()
                v.tensor_tensor(self.scr[:, 0:T], self.scr[:, 0:T],
                                self.xn(0, HALO, T), op=ALU.max); v.drain()
                for c in range(1, NCH):
                    v.tensor_scalar(tmp[:, 0:T], self.xn(c, HALO, T), -1.0,
                                    None, op0=ALU.mult); v.drain()
                    v.tensor_tensor(tmp[:, 0:T], tmp[:, 0:T],
                                    self.xn(c, HALO, T), op=ALU.max); v.drain()
                    v.tensor_tensor(self.scr[:, 0:T], self.scr[:, 0:T],
                                    tmp[:, 0:T], op=ALU.max); v.drain()
        with nc.Block() as blk:
            p0 = self.sP_n
            v0 = self.sV_n

            @blk.tensor
            def _(t):
                for g4 in range(4):
                    if g4 >= 1:
                        t.wait_ge(self.sV, v0 + g4)
                    for j in range(4):
                        tc = g4 * 4 + j
                        tr = t.transpose(self.pf[:, j * 128:(j + 1) * 128],
                                         self.scr[:, tc * 128:(tc + 1) * 128],
                                         self.IDF128())
                    tr.then_inc(self.sP, 1)
            self.sP_n += 4

            @blk.vector
            def _(v):
                for g4 in range(4):
                    v.wait_ge(self.sP, p0 + g4 + 1)
                    for j in range(4):
                        tc = g4 * 4 + j
                        v.tensor_reduce(tmp[:, tc:tc + 1],
                                        self.pf[:, j * 128:(j + 1) * 128],
                                        AX, ALU.max); v.drain()
                    v.nop().then_inc(self.sV, 1)
            self.sV_n += 4
        for grp4 in range(4):
            with nc.Block() as blk:
                gg = grp4
                p0 = self.sP_n

                @blk.tensor
                def _(t):
                    for j in range(4):
                        tr = t.transpose(self.pf[0:1, j * 128:(j + 1) * 128],
                                         tmp[:, gg * 4 + j: gg * 4 + j + 1],
                                         self.IDF128())
                    tr.then_inc(self.sP, 1)
                self.sP_n += 1

                @blk.vector
                def _(v):
                    v.wait_ge(self.sP, self.sP_n)
                    v.tensor_copy(self.scr[0:1, gg * 512: gg * 512 + 512],
                                  self.pf[0:1, 0:512]); v.drain()
        with nc.Block() as blk:
            @blk.vector
            def _(v):
                v.tensor_scalar(self.scr[0:1, 0:T], self.scr[0:1, 0:T], 1e-8,
                                None, op0=ALU.max); v.drain()
                v.tensor_scalar(self.scr[1:2, 0:T] if False else
                                self.mbc[0:1, 0:T], self.scr[0:1, 0:T],
                                1.0 / 127.0, None, op0=ALU.mult); v.drain()
                v.reciprocal(self.scr[0:1, 0:T], self.scr[0:1, 0:T]); v.drain()
        self.bcast_row(lambda i: self.mbc[0:1, i * 512:(i + 1) * 512], 4,
                       lambda i: self.rbc[:, i * 512:(i + 1) * 512])
        self.bcast_row(lambda i: self.scr[0:1, i * 512:(i + 1) * 512], 4,
                       lambda i: self.mbc[:, i * 512:(i + 1) * 512])
        with nc.Block() as blk:
            @blk.vector
            def _(v):
                for c in range(NCH):
                    cs = slice(c * T, (c + 1) * T)
                    v.tensor_tensor(self.scr[:, 0:T], self.xn(c, HALO, T),
                                    self.mbc[:, 0:T], op=ALU.mult); v.drain()
                    v.tensor_scalar(self.scr[:, 0:T], self.scr[:, 0:T],
                                    127.0, 0.5, op0=ALU.mult, op1=ALU.add); v.drain()
                    v.tensor_copy(itmp[:, 0:T], self.scr[:, 0:T]); v.drain()
                    v.tensor_copy(tmp[:, 0:T], itmp[:, 0:T]); v.drain()
                    v.tensor_tensor(self.scr[:, 0:T], tmp[:, 0:T],
                                    self.scr[:, 0:T], op=ALU.is_gt); v.drain()
                    v.tensor_sub(xq[:, cs], tmp[:, 0:T],
                                 self.scr[:, 0:T]); v.drain()
        with nc.Block() as blk:
            @blk.sync
            def _(s):
                self.dbg_write(s, "dbg_xq", AP(xq, 0, [[NCH * T, 128], [1, NCH * T]]))
                self.dbg_write(s, "dbg_wq", AP(wq, 0, [[NCH * D, 128], [1, NCH * D]]))

        with nc.Block() as blk:
            @blk.vector
            def _(v):
                for oc in range(NCH):
                    v.tensor_scalar(self.xc(oc, HALO, T), self.xc(oc, HALO, T),
                                    b0s[:, oc:oc + 1], None, op0=ALU.add); v.drain()
        tiles = [(oc, tb) for oc in range(NCH) for tb in range(T // 512)]
        with nc.Block() as blk:
            p_base = self.sP_n
            v_base = self.sV_n

            @blk.tensor
            def _(t):
                for i, (oc, tb) in enumerate(tiles):
                    bank = self.pb[i % 4]
                    if i >= 4:
                        t.wait_ge(self.sV, v_base + (i - 4) + 1)
                    for dc in range(NCH):
                        mm = t.matmul(bank[:, 0:512],
                                      wq[:, dc * D + oc * 128: dc * D + (oc + 1) * 128],
                                      xq[:, dc * T + tb * 512: dc * T + (tb + 1) * 512],
                                      start=(dc == 0), stop=(dc == NCH - 1))
                    mm.then_inc(self.sP, 1)
            self.sP_n += len(tiles)

            @blk.vector
            def _(v):
                for i, (oc, tb) in enumerate(tiles):
                    bank = self.pb[i % 4]
                    v.wait_ge(self.sP, p_base + i + 1)
                    v.tensor_tensor(self.mbc[:, 0:512], bank[:, 0:512],
                                    self.rbc[:, tb * 512:(tb + 1) * 512],
                                    op=ALU.mult).then_inc(self.sV, 1); v.drain()
                    v.scalar_tensor_tensor(self.xc(oc, HALO + tb * 512, 512),
                                           self.mbc[:, 0:512], swc[:, oc:oc + 1],
                                           self.xc(oc, HALO + tb * 512, 512),
                                           op0=ALU.mult, op1=ALU.add); v.drain()
            self.sV_n += len(tiles)
        with nc.Block() as blk:
            @blk.sync
            def _(s):
                self.dbg_write(s, "dbg_xffn",
                               AP(self.xsb, 0, [[NCH * TE, 128], [1, NCH * TE]]))
        sc_.__exit__(None, None, None)

    def emit_qh(self, qh, enqh):
        nc = self.nc
        nbl = (TE + 511) // 512
        with nc.Block() as blk:
            p0 = self.sP_n
            v0 = self.sV_n

            @blk.tensor
            def _(t):
                for bi in range(nbl):
                    n = min(512, TE - bi * 512)
                    if bi >= 2:
                        t.wait_ge(self.sV, v0 + (bi - 2) + 1)
                    for dc in range(NCH):
                        mm = t.matmul(self.pb[bi % 2][:, 0:n],
                                      enqh[:, dc * HD: (dc + 1) * HD],
                                      self.xn(dc, bi * 512, n),
                                      start=(dc == 0), stop=(dc == NCH - 1))
                    mm.then_inc(self.sP, 1)
            self.sP_n += nbl

            @blk.vector
            def _(v):
                for bi in range(nbl):
                    n = min(512, TE - bi * 512)
                    v.wait_ge(self.sP, p0 + bi + 1)
                    v.tensor_scalar(qh[:, bi * 512: bi * 512 + n],
                                    self.pb[bi % 2][:, 0:n], 0.0, None,
                                    op0=ALU.max).then_inc(self.sV, 1); v.drain()
            self.sV_n += nbl

    def p3_attn0(self):
        nc = self.nc
        sc_ = self.scope(); sc_.__enter__()
        mcol = sc_.sbuf("mcol", [128, 28], F32)
        scA = self.scope(); scA.__enter__()
        qh = scA.sbuf("qhA", [128, TE], BF16)
        enqh = scA.sbuf("enqhA", [128, NCH * HD], BF16)
        sigp = scA.sbuf("sigp", [128, 12], F32)
        bmsb = scA.sbuf("bmsb", [128, 24], F32)
        ehb = scA.sbuf("ehbA", [128, 16], BF16)

        for h in range(NH):
            with nc.Block() as blk:
                hh = h

                @blk.gpsimd
                def _(g):
                    self.dst(g, "sG", enqh[:, :],
                             AP(self.encq, hh * NCH * 128 * HD,
                                [[HD, 128], [128 * HD, NCH], [1, HD]]))
                    g.wait_ge(self.sG, self.sG_n)
            self.emit_qh(qh, enqh)
            with nc.Block() as blk:
                hh = h

                @blk.vector
                def _(v):
                    v.tensor_reduce(sigp[:, 8:9], qh[:, HALO: HALO + T], AX,
                                    ALU.add); v.drain()
                    v.tensor_reduce(sigp[:, 9:10],
                                    AP(qh, HALO, [[TE, 128], [8, 256]]), AX,
                                    ALU.add); v.drain()
                    v.tensor_scalar(sigp[:, 8:9], sigp[:, 8:9], 1.0 / 8192.0,
                                    None, op0=ALU.mult); v.drain()
                    v.scalar_tensor_tensor(sigp[:, hh:hh + 1], sigp[:, 9:10],
                                           0.5 / 1024.0, sigp[:, 8:9],
                                           op0=ALU.mult, op1=ALU.add); v.drain()
        for grp4 in range(2):
            with nc.Block() as blk:
                gg = grp4
                p0 = self.sP_n

                @blk.tensor
                def _(t):
                    for j in range(4):
                        tr = t.transpose(self.pf[0:1, j * 128:(j + 1) * 128],
                                         sigp[:, gg * 4 + j: gg * 4 + j + 1],
                                         self.IDF128())
                    tr.then_inc(self.sP, 1)
                self.sP_n += 1

                @blk.vector
                def _(v):
                    v.wait_ge(self.sP, self.sP_n)
                    v.tensor_copy(self.scr[0:1, gg * 512: gg * 512 + 512],
                                  self.pf[0:1, 0:512]); v.drain()
        sig2 = scA.sbuf("sig2", [2, D], F32)
        with nc.Block() as blk:
            @blk.sync
            def _(s):
                self.dst(s, "sD", sig2[0:1, :], self.scr[0:1, 0:D])
                self.dst(s, "sD", sig2[1:2, :], self.scr[0:1, 0:D])
                s.wait_ge(self.sD, self.sD_n)
        with nc.Block() as blk:
            @blk.vector
            def _(v):
                v.tensor_scalar(sig2[:, :], sig2[:, :], self.B2(), None,
                                op0=ALU.mult); v.drain()
        with nc.Block() as blk:
            @blk.sync
            def _(s):
                self.dst(s, "sD", self.cc_sig_i[:, :], sig2[:, :])
                s.wait_ge(self.sD, self.sD_n)
        with nc.Block() as blk:
            @blk.gpsimd
            def _(g):
                self.sC_n += 1
                g.collective_compute(
                    "AllReduce", ALU.add, replica_groups=[list(range(8))],
                    ins=[self.cc_sig_i.ap().opt()], outs=[self.cc_sig_o.ap().opt()],
                ).then_inc(self.sC)
                g.wait_ge(self.sC, self.sC_n)
        with nc.Block() as blk:
            @blk.sync
            def _(s):
                self.dst(s, "sD", sig2[:, :], self.cc_sig_o[:, :])
                s.wait_ge(self.sD, self.sD_n)
        with nc.Block() as blk:
            p0 = self.sP_n

            @blk.tensor
            def _(t):
                for h in range(NH):
                    tr = t.transpose(self.pf[:, h * 2:(h + 1) * 2],
                                     sig2[:, h * 128:(h + 1) * 128], self.IDF2())
                tr.then_inc(self.sP, 1)
            self.sP_n += 1

            @blk.vector
            def _(v):
                v.wait_ge(self.sP, self.sP_n)
                for h in range(NH):
                    v.tensor_tensor(self.pf[:, h * 2:(h + 1) * 2],
                                    self.pf[:, h * 2:(h + 1) * 2],
                                    self.BSEL2(), op=ALU.mult); v.drain()
                    v.tensor_reduce(mcol[:, h:h + 1], self.pf[:, h * 2:(h + 1) * 2],
                                    AX, ALU.add); v.drain()

        with self.scope() as msc:
            mwsb = msc.sbuf("mwsb", [128, NCH * 512], BF16)
            mw2sb = msc.sbuf("mw2sb", [128, 4 * D], BF16)
            with nc.Block() as blk:
                @blk.gpsimd
                def _(g):
                    self.dst(g, "sG", mwsb[:, :],
                             AP(self.mw1, 0, [[512, 128], [128 * 512, NCH], [1, 512]]))
                    self.dst(g, "sG", mw2sb[:, :],
                             AP(self.mw2, 0, [[D, 128], [128 * D, 4], [1, D]]))
                    g.wait_ge(self.sG, self.sG_n)

                @blk.sync
                def _(s):
                    self.dst(s, "sD", bmsb[:, 0:8], self.bmet[:, :])
                    self.dst(s, "sD", bmsb[:, 8:12], self.mb1[:, :])
                    self.dst(s, "sD", bmsb[:, 12:20], self.mb2[:, :])
                    s.wait_ge(self.sD, self.sD_n)
            with nc.Block() as blk:
                @blk.vector
                def _(v):
                    v.tensor_copy(ehb[:, 0:8], mcol[:, 0:8]); v.drain()
            with nc.Block() as blk:
                @blk.tensor
                def _(t):
                    for m in range(4):
                        for dc in range(NCH):
                            mm = t.matmul(self.pf[:, m:m + 1],
                                          mwsb[:, dc * 512 + m * 128: dc * 512 + (m + 1) * 128],
                                          ehb[:, dc:dc + 1],
                                          start=(dc == 0), stop=(dc == NCH - 1))
                    mm.then_inc(self.sP, 1)
                self.sP_n += 1

                @blk.scalar
                def _(sc):
                    sc.wait_ge(self.sP, self.sP_n)
                    for m in range(4):
                        sc.activation(mcol[:, 8 + m:9 + m], self.pf[:, m:m + 1],
                                      AF.Gelu, bias=bmsb[:, 8 + m:9 + m]); sc.drain()
            with nc.Block() as blk:
                @blk.vector
                def _(v):
                    v.tensor_copy(ehb[:, 8:12], mcol[:, 8:12]); v.drain()
            with nc.Block() as blk:
                @blk.tensor
                def _(t):
                    for oc in range(NCH):
                        for m in range(4):
                            mm = t.matmul(self.pf[:, 8 + oc:9 + oc],
                                          mw2sb[:, m * D + oc * 128: m * D + (oc + 1) * 128],
                                          ehb[:, 8 + m:9 + m],
                                          start=(m == 0), stop=(m == 3))
                    mm.then_inc(self.sP, 1)
                self.sP_n += 1

                @blk.vector
                def _(v):
                    v.wait_ge(self.sP, self.sP_n)
                    v.tensor_tensor(mcol[:, 12:20], self.pf[:, 8:16],
                                    bmsb[:, 12:20], op=ALU.add); v.drain()
                    v.tensor_scalar(mcol[:, 12:20], mcol[:, 12:20], 0.1, None,
                                    op0=ALU.mult); v.drain()
                    v.tensor_tensor(mcol[:, 12:20], mcol[:, 12:20], bmsb[:, 0:8],
                                    op=ALU.add); v.drain()
            with nc.Block() as blk:
                @blk.scalar
                def _(sc):
                    sc.activation(mcol[:, 20:28], mcol[:, 12:20], AF.Abs); sc.drain()
                    sc.activation(mcol[:, 20:28], mcol[:, 20:28], AF.Exp,
                                  scale=self.CCOL(4)); sc.drain()
                    sc.activation(mcol[:, 20:28], mcol[:, 20:28], AF.Ln,
                                  bias=self.CCOL(3)); sc.drain()
            with nc.Block() as blk:
                @blk.vector
                def _(v):
                    v.tensor_scalar(mcol[:, 12:20], mcol[:, 12:20], 0.0, None,
                                    op0=ALU.max); v.drain()
                    v.tensor_tensor(mcol[:, 20:28], mcol[:, 20:28], mcol[:, 12:20],
                                    op=ALU.add); v.drain()
                    v.tensor_scalar(mcol[:, 20:28], mcol[:, 20:28], 1e-6, None,
                                    op0=ALU.add); v.drain()
        with nc.Block() as blk:
            @blk.sync
            def _(s):
                self.dbg_write(s, "dbg_sig", mcol[:, 0:8])
                self.dbg_write(s, "dbg_mdiag", mcol[:, 20:28])

        scA.__exit__(None, None, None)
        vtm = sc_.sbuf("vtm", [128, 17 * D], BF16)

        with self.scope() as vsc:
            encvsb = vsc.sbuf("encvsb", [128, NCH * D], BF16)
            with nc.Block() as blk:
                @blk.gpsimd
                def _(g):
                    self.dst(g, "sG", encvsb[:, :],
                             AP(self.encv, 0, [[D, 128], [128 * D, NCH], [1, D]]))
                    g.wait_ge(self.sG, self.sG_n)
            with nc.Block() as blk:
                p_base = self.sP_n
                v_base = self.sV_n

                @blk.tensor
                def _(t):
                    for tc in range(17):
                        ntok = 128 if tc < 16 else 31
                        bk = (tc % 2) * 2
                        if tc >= 2:
                            t.wait_ge(self.sV, v_base + (tc - 2) + 1)
                        for dc in range(NCH):
                            for half in range(2):
                                mm = t.matmul(
                                    self.pb[bk + half][0:ntok, 0:512],
                                    self.xn(dc, tc * 128, ntok),
                                    encvsb[:, dc * D + half * 512: dc * D + (half + 1) * 512],
                                    start=(dc == 0), stop=(dc == NCH - 1))
                        mm.then_inc(self.sP, 1)
                self.sP_n += 17

                @blk.vector
                def _(v):
                    for tc in range(17):
                        ntok = 128 if tc < 16 else 31
                        bk = (tc % 2) * 2
                        v.wait_ge(self.sP, p_base + tc + 1)
                        for half in range(2):
                            v.tensor_scalar(
                                vtm[0:ntok, tc * D + half * 512: tc * D + (half + 1) * 512],
                                self.pb[bk + half][0:ntok, 0:512], 0.0,
                                None, op0=ALU.max); v.drain()
                        v.nop().then_inc(self.sV, 1)
                self.sV_n += 17

        aall = sc_.sbuf("aall", [128, 4 * T], BF16)
        qh = sc_.sbuf("qh", [128, TE], BF16)
        mq = sc_.sbuf("mq", [128, TE], BF16)
        sqm = sc_.sbuf("sqm", [128, TE], BF16)
        m2q = sc_.sbuf("m2q", [128, TE], BF16)
        ehb = sc_.sbuf("ehb", [128, 2 * 256], BF16)
        enqh = sc_.sbuf("enqh", [128, NCH * HD], BF16)
        hebh = sc_.sbuf("hebh", [128, HD], BF16)
        rDbc = sc_.sbuf("rDbc", [128, T], BF16)
        rNbc = sc_.sbuf("rNbc", [128, T], BF16)
        ctxs = sc_.sbuf("ctxs", [128, T], BF16)
        hebs = sc_.sbuf("hebs", [128, T], BF16)
        decsb = sc_.sbuf("decsb", [128, 4 * D], BF16)

        with nc.Block() as blk:
            @blk.vector
            def _(v):
                v.memset(self.pb[2][:, 0:512], 0.0); v.drain()
                v.memset(self.pb[3][:, 0:512], 0.0); v.drain()

        for grp in range(2):
            for hi in range(4):
                h = grp * 4 + hi
                with nc.Block() as blk:
                    hh = h

                    @blk.gpsimd
                    def _(g):
                        self.dst(g, "sG", enqh[:, :],
                                 AP(self.encq, hh * NCH * 128 * HD,
                                    [[HD, 128], [128 * HD, NCH], [1, HD]]))
                        self.dst(g, "sG", hebh[:, :],
                                 AP(self.hebw, hh * HD * HD, [[HD, 128], [1, HD]]))
                        g.wait_ge(self.sG, self.sG_n)
                self.emit_qh(qh, enqh)
                with nc.Block() as blk:
                    hh = h

                    @blk.vector
                    def _(v):
                        v.tensor_scalar(mq[:, :], qh[:, :], mcol[:, 20 + hh:21 + hh],
                                        None, op0=ALU.mult); v.drain()
                        v.tensor_tensor(sqm[:, :], qh[:, :], mq[:, :],
                                        op=ALU.mult); v.drain()
                        v.tensor_scalar(m2q[:, :], mq[:, :], -2.0, None,
                                        op0=ALU.mult); v.drain()

                with nc.Block() as blk:
                    pA = self.sP_n; pB = self.sW_n
                    vR = self.sV_n; sE = self.sS_n
                    vS = self.sX_n; vD = self.sY_n
                    hh = h

                    @blk.tensor
                    def _(t):
                        for qc in range(16):
                            bank = self.pb[qc % 2]
                            if qc >= 2:
                                t.wait_ge(self.sV, vR + (qc - 2) + 1)
                            kA = qc * 128
                            kB = qc * 128 + 128
                            qq = HALO + qc * 128
                            ops = [
                                (bank[:, 0:128], sqm[:, kA:kA + 128], self.ONES128()),
                                (bank[0:31, 128:256], sqm[:, kB:kB + 31], self.ONES128()),
                                (bank[0:31, 128:256], self.ONES128()[:, 0:31],
                                 sqm[:, qq:qq + 128]),
                                (bank[0:31, 128:256], qh[:, kB:kB + 31],
                                 m2q[:, qq:qq + 128]),
                                (bank[:, 0:128], self.ONES128(), sqm[:, qq:qq + 128]),
                                (bank[:, 0:128], qh[:, kA:kA + 128], m2q[:, qq:qq + 128]),
                            ]
                            for i, (o, l, r) in enumerate(ops):
                                mm = t.matmul(o, l, r, start=(i == 0), stop=(i == 5))
                            mm.then_inc(self.sP, 1)
                            t.wait_ge(self.sS, sE + qc + 1)
                            if qc >= 2:
                                t.wait_ge(self.sX, vS + (qc - 2) + 1)
                            ebuf = ehb[:, (qc % 2) * 256:(qc % 2) * 256 + 256]
                            g4 = qc // 4
                            c0 = (qc % 4) * 128
                            if qc % 4 == 0 and g4 > 0:
                                t.wait_ge(self.sY, vD + g4)
                            t.matmul(self.pf[0:1, c0:c0 + 128], self.OC(),
                                     ebuf[:, 0:128], start=True, stop=False)
                            t.matmul(self.pf[0:1, c0:c0 + 128], self.OC()[0:31, :],
                                     ebuf[0:31, 128:256], start=False, stop=True)
                            cb = self.pb[4 + qc % 2]
                            t.matmul(cb[:, 0:128],
                                     vtm[:, qc * D + hh * 128: qc * D + (hh + 1) * 128],
                                     ebuf[:, 0:128], start=True, stop=False)
                            t.matmul(cb[:, 0:128],
                                     vtm[0:31, (qc + 1) * D + hh * 128: (qc + 1) * D + (hh + 1) * 128],
                                     ebuf[0:31, 128:256], start=False, stop=True)
                            mm = t.matmul(cb[:, 256:384], hebh[:, :],
                                          qh[:, qq:qq + 128], start=True, stop=True)
                            mm.then_inc(self.sW, 1)
                    self.sP_n += 16
                    self.sW_n += 16

                    @blk.vector
                    def _(v):
                        for qc in range(16):
                            bank = self.pb[qc % 2]
                            ebank = self.pb[2 + qc % 2]
                            v.wait_ge(self.sP, pA + qc + 1)
                            if qc >= 2:
                                v.wait_ge(self.sS, sE + (qc - 2) + 1)
                            v.scalar_tensor_tensor(ebank[:, 0:128], bank[:, 0:128],
                                                   0.0, self.MBA(), op0=ALU.max,
                                                   op1=ALU.add); v.drain()
                            v.scalar_tensor_tensor(ebank[0:31, 128:256],
                                                   bank[0:31, 128:256], 0.0,
                                                   self.MBB(), op0=ALU.max,
                                                   op1=ALU.add).then_inc(self.sV, 1)
                            v.drain()
                            v.wait_ge(self.sW, pB + qc + 1)
                            cb = self.pb[4 + qc % 2]
                            v.tensor_copy(ctxs[:, qc * 128:(qc + 1) * 128],
                                          cb[:, 0:128]); v.drain()
                            v.tensor_copy(hebs[:, qc * 128:(qc + 1) * 128],
                                          cb[:, 256:384]).then_inc(self.sX, 1)
                            v.drain()
                            if qc % 4 == 3:
                                v.tensor_copy(self.scr[0:1, (qc // 4) * 512:(qc // 4) * 512 + 512],
                                              self.pf[0:1, 0:512]
                                              ).then_inc(self.sY, 1); v.drain()
                    self.sV_n += 16
                    self.sX_n += 16
                    self.sY_n += 4

                    @blk.scalar
                    def _(sc):
                        for qc in range(16):
                            ebank = self.pb[2 + qc % 2]
                            sc.wait_ge(self.sV, vR + qc + 1)
                            if qc >= 2:
                                sc.wait_ge(self.sW, pB + (qc - 2) + 1)
                            sc.activation(ebank[:, 0:256], ebank[:, 0:256], AF.Sqrt,
                                          bias=self.CCOL(1)); sc.drain()
                            sc.activation(ehb[:, (qc % 2) * 256:(qc % 2) * 256 + 256],
                                          ebank[:, 0:256],
                                          AF.Exp, scale=self.CCOL(4)
                                          ).then_inc(self.sS, 1); sc.drain()
                    self.sS_n += 16

                with nc.Block() as blk:
                    @blk.vector
                    def _(v):
                        v.tensor_scalar(self.scr[0:1, 0:T], self.scr[0:1, 0:T],
                                        1e-8, None, op0=ALU.add); v.drain()
                        v.reciprocal(self.scr[0:1, 0:T], self.scr[0:1, 0:T]); v.drain()
                self.bcast_row(lambda i: self.scr[0:1, i * 512:(i + 1) * 512], 4,
                               lambda i: rDbc[:, i * 512:(i + 1) * 512])
                with nc.Block() as blk:
                    @blk.vector
                    def _(v):
                        v.tensor_tensor(mq[:, HALO:HALO + T], qh[:, HALO:HALO + T],
                                        qh[:, HALO:HALO + T], op=ALU.mult); v.drain()
                with nc.Block() as blk:
                    pN = self.sP_n
                    vN = self.sV_n

                    @blk.tensor
                    def _(t):
                        for g4 in range(4):
                            if g4 > 0:
                                t.wait_ge(self.sV, vN + g4)
                            mm = t.matmul(self.pb[5][0:1, 0:512], self.OC(),
                                          mq[:, HALO + g4 * 512: HALO + (g4 + 1) * 512],
                                          start=True, stop=True)
                            mm.then_inc(self.sP, 1)
                    self.sP_n += 4

                    @blk.vector
                    def _(v):
                        for g4 in range(4):
                            v.wait_ge(self.sP, pN + g4 + 1)
                            v.tensor_copy(self.scr[0:1, g4 * 512:(g4 + 1) * 512],
                                          self.pb[5][0:1, 0:512]
                                          ).then_inc(self.sV, 1); v.drain()
                    self.sV_n += 4
                with nc.Block() as blk:
                    @blk.scalar
                    def _(sc):
                        sc.activation(self.scr[0:1, 0:T], self.scr[0:1, 0:T],
                                      AF.Sqrt); sc.drain()
                with nc.Block() as blk:
                    @blk.vector
                    def _(v):
                        v.tensor_scalar(self.scr[0:1, 0:T], self.scr[0:1, 0:T],
                                        1e-12, None, op0=ALU.max); v.drain()
                        v.reciprocal(self.scr[0:1, 0:T], self.scr[0:1, 0:T]); v.drain()
                self.bcast_row(lambda i: self.scr[0:1, i * 512:(i + 1) * 512], 4,
                               lambda i: rNbc[:, i * 512:(i + 1) * 512])
                with nc.Block() as blk:
                    hh = hi

                    @blk.vector
                    def _(v):
                        v.tensor_tensor(ctxs[:, :], ctxs[:, :], rDbc[:, :],
                                        op=ALU.mult); v.drain()
                        v.tensor_tensor(hebs[:, :], hebs[:, :], rNbc[:, :],
                                        op=ALU.mult); v.drain()
                        v.tensor_tensor(ctxs[:, :], ctxs[:, :], hebs[:, :],
                                        op=ALU.add); v.drain()
                        v.tensor_tensor(aall[:, hh * T:(hh + 1) * T], ctxs[:, :],
                                        qh[:, HALO:HALO + T], op=ALU.mult); v.drain()

            with nc.Block() as blk:
                gg = grp

                @blk.gpsimd
                def _(g):
                    self.dst(g, "sG", decsb[:, :],
                             AP(self.dec, gg * 4 * 128 * D,
                                [[D, 128], [128 * D, 4], [1, D]]))
                    g.wait_ge(self.sG, self.sG_n)
            tiles = [(oc, tb) for oc in range(NCH) for tb in range(T // 512)]
            with nc.Block() as blk:
                p_base = self.sP_n
                v_base = self.sV_n

                @blk.tensor
                def _(t):
                    for i, (oc, tb) in enumerate(tiles):
                        bank = self.pb[i % 4]
                        if i >= 4:
                            t.wait_ge(self.sV, v_base + (i - 4) + 1)
                        for hc in range(4):
                            mm = t.matmul(bank[:, 0:512],
                                          decsb[:, hc * D + oc * 128: hc * D + (oc + 1) * 128],
                                          aall[:, hc * T + tb * 512: hc * T + (tb + 1) * 512],
                                          start=(hc == 0), stop=(hc == 3))
                        mm.then_inc(self.sP, 1)
                self.sP_n += len(tiles)

                @blk.vector
                def _(v):
                    for i, (oc, tb) in enumerate(tiles):
                        bank = self.pb[i % 4]
                        v.wait_ge(self.sP, p_base + i + 1)
                        v.tensor_add(self.xc(oc, HALO + tb * 512, 512),
                                     self.xc(oc, HALO + tb * 512, 512),
                                     bank[:, 0:512]).then_inc(self.sV, 1); v.drain()
                self.sV_n += len(tiles)
            with nc.Block() as blk:
                @blk.vector
                def _(v):
                    v.memset(self.pb[2][:, 0:512], 0.0); v.drain()
                    v.memset(self.pb[3][:, 0:512], 0.0); v.drain()
        with nc.Block() as blk:
            @blk.sync
            def _(s):
                self.dbg_write(s, "dbg_x1",
                               AP(self.xsb, 0, [[NCH * TE, 128], [1, NCH * TE]]))
        sc_.__exit__(None, None, None)

    def p4_ln1(self):
        nc = self.nc
        self.p1_ln("1", ntok=T, lo=HALO)
        with nc.Block() as blk:
            @blk.sync
            def _(s):
                self.dst(s, "sD",
                         AP(self.x1_park, 0, [[T, 128], [128 * T, NCH], [1, T]]),
                         AP(self.xsb, HALO, [[NCH * TE, 128], [TE, NCH], [1, T]]))
                s.wait_ge(self.sD, self.sD_n)

    def p6_gate(self):
        nc = self.nc
        sc_ = self.scope(); sc_.__enter__()
        glog = sc_.sbuf("glog", [8, T], F32)
        pm = sc_.sbuf("pm", [8, T], F32)
        gwsb = sc_.sbuf("gwsb", [128, NCH * NE], BF16)
        gcol = sc_.sbuf("gcol", [8, 8], F32)
        g2 = sc_.sbuf("g2", [2, NE], F32)
        wtb = sc_.sbuf("wtb", [8, T], BF16)
        with nc.Block() as blk:
            @blk.gpsimd
            def _(g):
                self.dst(g, "sG", gwsb[:, :],
                         AP(self.gwc, 0, [[NE, 128], [128 * NE, NCH], [1, NE]]))
                g.wait_ge(self.sG, self.sG_n)

            @blk.sync
            def _(s):
                self.dst(s, "sD", gcol[:, 0:1], self.gbc[:, :])
                s.wait_ge(self.sD, self.sD_n)
        with nc.Block() as blk:
            @blk.tensor
            def _(t):
                for tb in range(4):
                    for dc in range(NCH):
                        mm = t.matmul(self.pb[tb][0:8, 0:512],
                                      gwsb[:, dc * NE:(dc + 1) * NE],
                                      self.xn(dc, HALO + tb * 512, 512),
                                      start=(dc == 0), stop=(dc == NCH - 1))
                mm.then_inc(self.sP, 1)
            self.sP_n += 1

            @blk.vector
            def _(v):
                v.wait_ge(self.sP, self.sP_n)
                for tb in range(4):
                    v.tensor_scalar(glog[:, tb * 512:(tb + 1) * 512],
                                    self.pb[tb][0:8, 0:512], gcol[:, 0:1], None,
                                    op0=ALU.add); v.drain()

        def stat_allreduce(src_i, cc_i, cc_o, is_max, out_i):
            with nc.Block() as blk:
                @blk.tensor
                def _(t):
                    t.transpose(self.pf[0:1, 0:8], gcol[:, src_i:src_i + 1],
                                self.IDF8()).then_inc(self.sP, 1)
                self.sP_n += 1

                @blk.vector
                def _(v):
                    v.wait_ge(self.sP, self.sP_n)
                    v.tensor_copy(self.scr[0:1, 0:8], self.pf[0:1, 0:8]); v.drain()
            with nc.Block() as blk:
                @blk.sync
                def _(s):
                    self.dst(s, "sD", g2[0:1, :], self.scr[0:1, 0:8])
                    self.dst(s, "sD", g2[1:2, :], self.scr[0:1, 0:8])
                    s.wait_ge(self.sD, self.sD_n)
            with nc.Block() as blk:
                @blk.vector
                def _(v):
                    v.tensor_scalar(g2[:, :], g2[:, :], self.B2(), None,
                                    op0=ALU.mult); v.drain()
                    if is_max:
                        v.tensor_scalar(self.scr[0:2, 8:9], self.B2(), -1.0,
                                        3.0e38, op0=ALU.add, op1=ALU.mult); v.drain()
                        v.tensor_scalar(g2[:, :], g2[:, :], self.scr[0:2, 8:9],
                                        None, op0=ALU.add); v.drain()
            with nc.Block() as blk:
                @blk.sync
                def _(s):
                    self.dst(s, "sD", cc_i[:, :], g2[:, :])
                    s.wait_ge(self.sD, self.sD_n)
            with nc.Block() as blk:
                @blk.gpsimd
                def _(g):
                    self.sC_n += 1
                    g.collective_compute(
                        "AllReduce", ALU.max if is_max else ALU.add,
                        replica_groups=[list(range(8))],
                        ins=[cc_i.ap().opt()], outs=[cc_o.ap().opt()],
                    ).then_inc(self.sC)
                    g.wait_ge(self.sC, self.sC_n)
            with nc.Block() as blk:
                @blk.sync
                def _(s):
                    self.dst(s, "sD", g2[:, :], cc_o[:, :])
                    s.wait_ge(self.sD, self.sD_n)
            with nc.Block() as blk:
                @blk.tensor
                def _(t):
                    t.transpose(self.pf[0:8, 0:2], g2[:, :],
                                self.IDF2()).then_inc(self.sP, 1)
                self.sP_n += 1

                @blk.vector
                def _(v):
                    v.wait_ge(self.sP, self.sP_n)
                    v.tensor_tensor(self.pf[0:8, 0:2], self.pf[0:8, 0:2],
                                    self.BSEL2(8), op=ALU.mult); v.drain()
                    v.tensor_reduce(gcol[:, out_i:out_i + 1], self.pf[0:8, 0:2],
                                    AX, ALU.add); v.drain()

        with nc.Block() as blk:
            @blk.vector
            def _(v):
                v.tensor_reduce(gcol[:, 1:2], glog[:, :], AX, ALU.max); v.drain()
        stat_allreduce(1, self.cc_gm_i, self.cc_gm_o, True, 2)
        with nc.Block() as blk:
            @blk.vector
            def _(v):
                v.tensor_scalar(gcol[:, 3:4], gcol[:, 2:3], -1.0, None,
                                op0=ALU.mult); v.drain()
        with nc.Block() as blk:
            @blk.scalar
            def _(sc):
                sc.activation(glog[:, :], glog[:, :], AF.Exp,
                              bias=gcol[0:8, 3:4]); sc.drain()
        with nc.Block() as blk:
            @blk.vector
            def _(v):
                v.tensor_reduce(gcol[:, 4:5], glog[:, :], AX, ALU.add); v.drain()
        stat_allreduce(4, self.cc_gs_i, self.cc_gs_o, False, 5)
        with nc.Block() as blk:
            @blk.vector
            def _(v):
                v.reciprocal(gcol[:, 6:7], gcol[:, 5:6]); v.drain()
                v.tensor_scalar(glog[:, :], glog[:, :], gcol[:, 6:7], None,
                                op0=ALU.mult); v.drain()
        ptm = sc_.sbuf("ptm", [128, 16 * NE], F32)
        m12 = sc_.sbuf("m12", [128, 2 * 16], F32)
        with nc.Block() as blk:
            p0 = self.sP_n
            v0 = self.sV_n

            @blk.tensor
            def _(t):
                for g4 in range(4):
                    if g4 >= 1:
                        t.wait_ge(self.sV, v0 + g4)
                    for j in range(4):
                        tc = g4 * 4 + j
                        tr = t.transpose(self.pf[0:128, j * 8: j * 8 + 8],
                                         glog[:, tc * 128:(tc + 1) * 128],
                                         self.IDF8())
                    tr.then_inc(self.sP, 1)
            self.sP_n += 4

            @blk.vector
            def _(v):
                for g4 in range(4):
                    v.wait_ge(self.sP, p0 + g4 + 1)
                    for j in range(4):
                        tc = g4 * 4 + j
                        v.tensor_copy(ptm[:, tc * 8:(tc + 1) * 8],
                                      self.pf[0:128, j * 8: j * 8 + 8]); v.drain()
                    v.nop().then_inc(self.sV, 1)
            self.sV_n += 4
        with nc.Block() as blk:
            @blk.vector
            def _(v):
                for tc in range(16):
                    sl_ = ptm[:, tc * 8:(tc + 1) * 8]
                    v.tensor_reduce(m12[:, tc:tc + 1], sl_, AX, ALU.max); v.drain()
                    v.tensor_scalar(self.scr[:, 0:8], sl_, m12[:, tc:tc + 1],
                                    None, op0=ALU.is_ge); v.drain()
                    v.scalar_tensor_tensor(self.scr[:, 0:8], self.scr[:, 0:8],
                                           -1.0e9, sl_, op0=ALU.mult,
                                           op1=ALU.add); v.drain()
                    v.tensor_reduce(m12[:, 16 + tc: 17 + tc], self.scr[:, 0:8],
                                    AX, ALU.max); v.drain()
                    v.tensor_scalar(self.scr[:, 0:8], sl_,
                                    m12[:, 16 + tc: 17 + tc], None,
                                    op0=ALU.is_ge); v.drain()
                    v.tensor_tensor(self.scr[:, 8:16], sl_, self.scr[:, 0:8],
                                    op=ALU.mult); v.drain()
                    v.tensor_copy(self.wcolsb[:, tc * 8:(tc + 1) * 8],
                                  self.scr[:, 8:16]); v.drain()
                    v.tensor_copy(ptm[:, tc * 8:(tc + 1) * 8],
                                  self.scr[:, 0:8]); v.drain()
        with nc.Block() as blk:
            p0 = self.sP_n
            v0 = self.sV_n

            @blk.tensor
            def _(t):
                for g4 in range(4):
                    if g4 >= 1:
                        t.wait_ge(self.sV, v0 + g4)
                    for j in range(4):
                        tc = g4 * 4 + j
                        tr = t.transpose(self.pf[0:8, j * 128: j * 128 + 128],
                                         ptm[:, tc * 8:(tc + 1) * 8], self.IDF128())
                    tr.then_inc(self.sP, 1)
            self.sP_n += 4

            @blk.vector
            def _(v):
                for g4 in range(4):
                    v.wait_ge(self.sP, p0 + g4 + 1)
                    for j in range(4):
                        tc = g4 * 4 + j
                        v.tensor_copy(self.indb[:, tc * 128:(tc + 1) * 128],
                                      self.pf[0:8, j * 128: j * 128 + 128]); v.drain()
                    v.nop().then_inc(self.sV, 1)
            self.sV_n += 4
        with nc.Block() as blk:
            @blk.sync
            def _(s):
                self.dbg_write(s, "dbg_wtok", self.wcolsb[:, :])
                self.dbg_write(s, "dbg_ind", self.indb[:, :])
        sc_.__exit__(None, None, None)

    def p5_qkv(self):
        nc = self.nc
        scL = self.scope(); scL.__enter__()
        self.scL = scL
        q1 = scL.sbuf("q1", [128, NCH * T], BF16)
        vtm1 = scL.sbuf("vtm1", [128, 16 * D], BF16)
        a1b = scL.sbuf("a1b", [128, D], BF16)
        ssb = scL.sbuf("ssb", [128, 10 * 128], BF16)
        ktc = scL.sbuf("ktc", [128, 3 * 512], BF16)
        slb = ktc
        bqk = scL.sbuf("bqk", [128, 16], F32)
        st32 = scL.sbuf("st32", [128, NCH * D], F32)
        st16 = scL.sbuf("st16", [128, NCH * D], BF16)
        kfm = scL.sbuf("kfm", [128, NCH * T], BF16)
        wdb = scL.sbuf("wdb", [128, NCH * 512], BF16)
        self._l1 = dict(q1=q1, vtm1=vtm1, st32=st32, st16=st16, a1b=a1b,
                        slb=slb, ssb=ssb, ktc=ktc, kfm=kfm)

        with nc.Block() as blk:
            @blk.sync
            def _(s):
                self.dst(s, "sD", bqk[:, 0:8], self.bq_c[:, :])
                self.dst(s, "sD", bqk[:, 8:16], self.bk_c[:, :])
                s.wait_ge(self.sD, self.sD_n)

            @blk.sync
            def _(s2):
                self.dst(s2, "sD", self.scr[0:1, 0:D],
                         AP(self.bkv_r, D, [[1, 1], [1, D]]))
                s2.wait_ge(self.sD, self.sD_n)
        self.bcast_row(lambda i: self.scr[0:1, i * 512:(i + 1) * 512], 2,
                       lambda i: self.scr[:, i * 512:(i + 1) * 512])

        with nc.Block() as blk:
            p_base = self.sP_n
            v_base = self.sV_n
            gmarks = []

            @blk.gpsimd
            def _(g):
                for f in range(16):
                    if f >= 2:
                        g.wait_ge(self.sP, p_base + 4 * (f - 1))
                    gmarks.append(self.dst(
                        g, ["sA", "sB"][f % 2],
                        wdb[:, (f % 2) * 1024:(f % 2) * 1024 + 1024],
                        AP(self.wqkvT, f * 128,
                           [[3 * D, 128], [128 * 3 * D, NCH], [1, 128]])))

            @blk.tensor
            def _(t):
                for f in range(16):
                    t.wait_ge([self.sA, self.sB][f % 2], gmarks[f])
                    if f >= 1:
                        t.wait_ge(self.sV, v_base + 4 * f)
                    for tb in range(4):
                        for dc in range(NCH):
                            mm = t.matmul(
                                self.pb[tb][:, 0:512],
                                wdb[:, (f % 2) * 1024 + dc * 128:(f % 2) * 1024 + (dc + 1) * 128],
                                self.xn(dc, HALO + tb * 512, 512),
                                start=(dc == 0), stop=(dc == NCH - 1))
                        mm.then_inc(self.sP, 1)
            self.sP_n += 64

            @blk.vector
            def _(v):
                for f in range(16):
                    dst = q1 if f < 8 else kfm
                    fc = f % 8
                    for tb in range(4):
                        v.wait_ge(self.sP, p_base + 4 * f + tb + 1)
                        v.tensor_scalar(dst[:, fc * T + tb * 512: fc * T + (tb + 1) * 512],
                                        self.pb[tb][:, 0:512], bqk[:, f:f + 1],
                                        None, op0=ALU.add).then_inc(self.sV, 1)
                        v.drain()
            self.sV_n += 64

        for half in range(2):
            with nc.Block() as blk:
                hh = half

                @blk.gpsimd
                def _(g):
                    self.dst(g, "sG", wdb[:, :],
                             AP(self.wqkvT, 2 * D + hh * 512,
                                [[3 * D, 128], [128 * 3 * D, NCH], [1, 512]]))
                    g.wait_ge(self.sG, self.sG_n)
            with nc.Block() as blk:
                p_base = self.sP_n
                v_base = self.sV_n
                hh = half

                @blk.tensor
                def _(t):
                    for tc in range(16):
                        bank = self.pb[tc % 4]
                        if tc >= 4:
                            t.wait_ge(self.sV, v_base + (tc - 4) + 1)
                        for dc in range(NCH):
                            mm = t.matmul(bank[:, 0:512],
                                          self.xn(dc, HALO + tc * 128, 128),
                                          wdb[:, dc * 512:(dc + 1) * 512],
                                          start=(dc == 0), stop=(dc == NCH - 1))
                        mm.then_inc(self.sP, 1)
                self.sP_n += 16

                @blk.vector
                def _(v):
                    for tc in range(16):
                        bank = self.pb[tc % 4]
                        v.wait_ge(self.sP, p_base + tc + 1)
                        v.tensor_tensor(vtm1[:, tc * D + hh * 512: tc * D + hh * 512 + 512],
                                        bank[:, 0:512],
                                        self.scr[:, hh * 512: hh * 512 + 512],
                                        op=ALU.add).then_inc(self.sV, 1); v.drain()
                self.sV_n += 16

    def p7_attn1(self):
        nc = self.nc
        L = self._l1
        q1, vtm1, st32, st16 = L["q1"], L["vtm1"], L["st32"], L["st16"]
        a1b, slb, ssb, ktc, kfm = L["a1b"], L["slb"], L["ssb"], L["ktc"], L["kfm"]

        with nc.Block() as blk:
            @blk.vector
            def _(v):
                for dc in range(NCH):
                    v.memset(st32[:, dc * D:(dc + 1) * D], 0.0); v.drain()
                    v.memset(st16[:, dc * D:(dc + 1) * D], 0.0); v.drain()

        for c in range(NCHUNK):
            pairs = [(p_, kb) for p_ in range(4) for kb in range(p_ + 1)]
            with nc.Block() as blk:
                p0 = self.sP_n
                v0 = self.sV_n
                cc = c

                @blk.tensor
                def _(t):
                    for i, (p_, kb) in enumerate(pairs):
                        bank = self.pb[i % 2]
                        if i >= 2:
                            t.wait_ge(self.sV, v0 + (i - 2) + 1)
                        kt = cc * 4 + kb
                        qt = cc * 4 + p_
                        for dc in range(NCH):
                            mm = t.matmul(
                                bank[:, 0:128],
                                kfm[:, dc * T + kt * 128:dc * T + (kt + 1) * 128],
                                q1[:, dc * T + qt * 128:dc * T + (qt + 1) * 128],
                                start=(dc == 0), stop=(dc == NCH - 1))
                        mm.then_inc(self.sP, 1)
                self.sP_n += len(pairs)

                @blk.vector
                def _(v):
                    for i, (p_, kb) in enumerate(pairs):
                        bank = self.pb[i % 2]
                        kt = cc * 4 + kb
                        v.wait_ge(self.sP, p0 + i + 1)
                        if p_ == kb:
                            v.scalar_tensor_tensor(ssb[:, i * 128:(i + 1) * 128],
                                                   bank[:, 0:128],
                                                   self.DKM()[:, kt:kt + 1],
                                                   self.TRI(), op0=ALU.mult,
                                                   op1=ALU.mult
                                                   ).then_inc(self.sV, 1); v.drain()
                        else:
                            v.tensor_scalar(ssb[:, i * 128:(i + 1) * 128],
                                            bank[:, 0:128],
                                            self.DKM()[:, kt:kt + 1], None,
                                            op0=ALU.mult).then_inc(self.sV, 1)
                            v.drain()
                self.sV_n += len(pairs)
            with nc.Block() as blk:
                p0 = self.sP_n
                v0 = self.sV_n
                d0 = self.sD_n
                cc = c

                @blk.tensor
                def _(t):
                    for p_ in range(4):
                        qt = cc * 4 + p_
                        bk = (p_ % 2) * 2
                        if p_ >= 2:
                            t.wait_ge(self.sV, v0 + (p_ - 2) + 1)
                        for kb in range(p_ + 1):
                            i = p_ * (p_ + 1) // 2 + kb
                            kt = cc * 4 + kb
                            for hf in range(2):
                                t.matmul(self.pb[bk + hf][:, 0:512],
                                         ssb[:, i * 128:(i + 1) * 128],
                                         vtm1[:, kt * D + hf * 512: kt * D + (hf + 1) * 512],
                                         start=(kb == 0), stop=False)
                        for dc in range(NCH):
                            for hf in range(2):
                                mm = t.matmul(
                                    self.pb[bk + hf][:, 0:512],
                                    q1[:, dc * T + qt * 128:dc * T + (qt + 1) * 128],
                                    st16[:, dc * D + hf * 512:dc * D + (hf + 1) * 512],
                                    start=False, stop=(dc == NCH - 1))
                        mm.then_inc(self.sP, 1)
                self.sP_n += 4

                dmarks = []

                @blk.sync
                def _(s):
                    for p_ in range(4):
                        qt = cc * 4 + p_
                        s.wait_ge(self.sV, v0 + p_ + 1)
                        dmarks.append(self.dst(
                            s, "sD",
                            AP(self.a1_park, qt * 128 * D, [[D, 128], [1, D]]),
                            a1b[:, :]))
                    s.wait_ge(self.sD, self.sD_n)

                @blk.vector
                def _(v):
                    for p_ in range(4):
                        qt = cc * 4 + p_
                        bk = (p_ % 2) * 2
                        v.wait_ge(self.sP, p0 + p_ + 1)
                        if p_ >= 1:
                            v.wait_ge(self.sD, dmarks[p_ - 1])
                        for hf in range(2):
                            v.tensor_scalar(a1b[:, hf * 512:(hf + 1) * 512],
                                            self.pb[bk + hf][:, 0:512],
                                            self.DIC()[:, qt:qt + 1], None,
                                            op0=ALU.mult); v.drain()
                        v.nop().then_inc(self.sV, 1)
                self.sV_n += 4
            with nc.Block() as blk:
                pT = self.sP_n
                pU = self.sW_n
                vK = self.sV_n
                vS = self.sX_n
                cc = c

                @blk.tensor
                def _(t):
                    for dc in range(NCH):
                        for tsc in range(4):
                            kt = cc * 4 + tsc
                            tr = t.transpose(
                                self.pt[:, tsc * 128:(tsc + 1) * 128],
                                kfm[:, dc * T + kt * 128:dc * T + (kt + 1) * 128],
                                self.ID())
                        tr.then_inc(self.sP, 1)
                        t.wait_ge(self.sV, vK + dc + 1)
                        bk = (dc % 2) * 2
                        if dc >= 2:
                            t.wait_ge(self.sX, vS + (dc - 2) + 1)
                        for tsc in range(4):
                            for hf in range(2):
                                mm = t.matmul(
                                    self.pb[bk + hf][:, 0:512],
                                    ktc[:, tsc * 128:(tsc + 1) * 128],
                                    vtm1[:, (cc * 4 + tsc) * D + hf * 512: (cc * 4 + tsc) * D + (hf + 1) * 512],
                                    start=(tsc == 0), stop=(tsc == 3))
                        mm.then_inc(self.sW, 1)
                self.sP_n += NCH
                self.sW_n += NCH

                @blk.vector
                def _(v):
                    for dc in range(NCH):
                        v.wait_ge(self.sP, pT + dc + 1)
                        for tsc in range(4):
                            kt = cc * 4 + tsc
                            v.tensor_scalar(ktc[:, tsc * 128:(tsc + 1) * 128],
                                            self.pt[:, tsc * 128:(tsc + 1) * 128],
                                            self.DKM()[:, kt:kt + 1], None,
                                            op0=ALU.mult); v.drain()
                        v.nop().then_inc(self.sV, 1)
                        v.wait_ge(self.sW, pU + dc + 1)
                        bk = (dc % 2) * 2
                        for hf in range(2):
                            hs = slice(dc * D + hf * 512, dc * D + (hf + 1) * 512)
                            v.tensor_tensor(st32[:, hs], st32[:, hs],
                                            self.pb[bk + hf][:, 0:512],
                                            op=ALU.add); v.drain()
                            v.tensor_copy(st16[:, hs], st32[:, hs]); v.drain()
                        v.nop().then_inc(self.sX, 1)
                self.sV_n += NCH
                self.sX_n += NCH

        with nc.Block() as blk:
            @blk.sync
            def _(s):
                self.dst(s, "sD", self.cc_ut_i.ap(), st16[:, :])
                s.wait_ge(self.sD, self.sD_n)
        with nc.Block() as blk:
            @blk.gpsimd
            def _(g):
                self.sC_n += 1
                g.collective_compute(
                    "AllGather", ALU.bypass,
                    replica_groups=[[0, 1, 2, 3], [4, 5, 6, 7]],
                    ins=[self.cc_ut_i.ap().opt()], outs=[self.cc_ut_o.ap().opt()],
                ).then_inc(self.sC)
                g.wait_ge(self.sC, self.sC_n)
        for dc in range(NCH):
            for hf in range(2):
                with nc.Block() as blk:
                    dd, hh = dc, hf

                    @blk.sync
                    def _(s):
                        for qs in range(3):
                            self.dst(s, "sD", slb[:, qs * 512:(qs + 1) * 512],
                                     AP(self.cc_ut_o,
                                        qs * 128 * NCH * D + dd * D + hh * 512,
                                        [[NCH * D, 128], [1, 512]]))
                        s.wait_ge(self.sD, self.sD_n)
                with nc.Block() as blk:
                    dd, hh = dc, hf

                    @blk.vector
                    def _(v):
                        hs = slice(dd * D + hh * 512, dd * D + (hh + 1) * 512)
                        v.tensor_scalar(self.scr[:, 0:512], slb[:, 0:512],
                                        self.DRM()[:, 0:1], None,
                                        op0=ALU.mult); v.drain()
                        v.scalar_tensor_tensor(self.scr[:, 0:512],
                                               slb[:, 512:1024],
                                               self.DRM()[:, 1:2],
                                               self.scr[:, 0:512],
                                               op0=ALU.mult, op1=ALU.add); v.drain()
                        v.scalar_tensor_tensor(self.scr[:, 0:512],
                                               slb[:, 1024:1536],
                                               self.DRM()[:, 2:3],
                                               self.scr[:, 0:512],
                                               op0=ALU.mult, op1=ALU.add); v.drain()
                        v.tensor_copy(st16[:, hs], self.scr[:, 0:512]); v.drain()
        with nc.Block() as blk:
            p0 = self.sP_n
            v0 = self.sV_n
            g0 = self.sG_n

            @blk.tensor
            def _(t):
                for tc in range(16):
                    bk = (tc % 2) * 2
                    if tc >= 2:
                        t.wait_ge(self.sV, v0 + (tc - 2) + 1)
                    for dc in range(NCH):
                        for hf in range(2):
                            mm = t.matmul(
                                self.pb[bk + hf][:, 0:512],
                                q1[:, dc * T + tc * 128:dc * T + (tc + 1) * 128],
                                st16[:, dc * D + hf * 512:dc * D + (hf + 1) * 512],
                                start=(dc == 0), stop=(dc == NCH - 1))
                    mm.then_inc(self.sP, 1)
            self.sP_n += 16

            rmarks = []

            @blk.gpsimd
            def _(g):
                for tc in range(16):
                    g.wait_ge(self.sV, v0 + tc + 1)
                    rmarks.append(self.dst(
                        g, "sG",
                        AP(self.a1_park, tc * 128 * D, [[D, 128], [1, D]]),
                        a1b[:, :], accum_op=ALU.add))
                g.wait_ge(self.sG, self.sG_n)

            @blk.vector
            def _(v):
                for tc in range(16):
                    bk = (tc % 2) * 2
                    v.wait_ge(self.sP, p0 + tc + 1)
                    if tc >= 1:
                        v.wait_ge(self.sG, rmarks[tc - 1])
                    for hf in range(2):
                        v.tensor_scalar(a1b[:, hf * 512:(hf + 1) * 512],
                                        self.pb[bk + hf][:, 0:512],
                                        self.DIC()[:, tc:tc + 1], None,
                                        op0=ALU.mult); v.drain()
                    v.nop().then_inc(self.sV, 1)
            self.sV_n += 16
        with nc.Block() as blk:
            @blk.sync
            def _(s):
                self.dbg_write(s, "dbg_a1", self.a1_park.ap())
        self.scL.__exit__(None, None, None)

    def p8_moe(self):
        nc = self.nc
        scM = self.scope(); scM.__enter__()
        h1 = scM.sbuf("h1", [128, NCH * T], BF16)
        h2 = scM.sbuf("h2", [128, NCH * T], BF16)
        ytm = scM.sbuf("ytm", [128, 16 * D], BF16)
        vps = scM.sbuf("vps", [1, D], F32)
        bcl = scM.sbuf("bcl", [128, 2 * NCH], F32)
        b3s = scM.sbuf("b3s", [1, D], BF16)
        wsb = scM.sbuf("wsb", [128, 2 * NCH * D], BF16)

        wmarks = {}

        def load_w(g, which, e, slot, si):
            wmarks[si] = (slot, self.dst(
                g, ["sA", "sB"][slot],
                wsb[:, slot * NCH * D:(slot + 1) * NCH * D],
                AP(which, e * NCH * 128 * D,
                   [[D, 128], [128 * D, NCH], [1, D]])))

        with nc.Block() as blk:
            @blk.gpsimd
            def _(g):
                load_w(g, self.w1, 0, 0, 0)
                g.wait_ge(self.sA, self.sA_n)
        steps = [(e, l) for e in range(NE) for l in range(3)]
        for si, (e, l) in enumerate(steps):
            slot = si % 2
            nslot = (si + 1) % 2
            with nc.Block() as blk:
                ee, ll = e, l
                p0 = self.sP_n
                v0 = self.sV_n
                g0w = self.sG_n

                @blk.gpsimd
                def _(g):
                    if ll == 2:
                        self.dst(g, "sG", b3s[:, :],
                                 AP(self.b3r, ee * D, [[1, 1], [1, D]]))
                    if si + 1 < len(steps):
                        ne_, nl_ = steps[si + 1]
                        wt = [self.w1, self.w2, self.w3][nl_]
                        load_w(g, wt, ne_, nslot, si + 1)

                bmark = [None]

                @blk.sync
                def _(s):
                    if ll == 0:
                        self.dst(s, "sD", bcl[:, 0:8],
                                 AP(self.b1c, ee * 128 * NCH, [[NCH, 128], [1, NCH]]))
                        bmark[0] = self.dst(s, "sD", bcl[:, 8:16],
                                            AP(self.b2c, ee * 128 * NCH,
                                               [[NCH, 128], [1, NCH]]))
                        s.wait_ge(self.sD, self.sD_n)

                @blk.tensor
                def _(t):
                    wslot, wmark = wmarks[si]
                    t.wait_ge([self.sA, self.sB][wslot], wmark)
                    if ll < 2:
                        src = self.xnsb if ll == 0 else h1
                        soff = HALO if ll == 0 else 0
                        swid = TE if ll == 0 else T
                        for i in range(32):
                            oc, tb = i // 4, i % 4
                            bank = self.pb[i % 4]
                            if i >= 4:
                                t.wait_ge(self.sV, v0 + (i - 4) + 1)
                            for dc in range(NCH):
                                mm = t.matmul(
                                    bank[:, 0:512],
                                    wsb[:, slot * NCH * D + dc * D + oc * 128: slot * NCH * D + dc * D + (oc + 1) * 128],
                                    src[:, dc * swid + soff + tb * 512: dc * swid + soff + tb * 512 + 512],
                                    start=(dc == 0), stop=(dc == NCH - 1))
                            mm.then_inc(self.sP, 1)
                    else:
                        t.wait_ge(self.sG, self.sG_n)
                        for tc in range(16):
                            bk = (tc % 2) * 2
                            if tc >= 2:
                                t.wait_ge(self.sV, v0 + (tc - 2) + 1)
                            for oc in range(NCH):
                                for hf in range(2):
                                    t.matmul(
                                        self.pb[bk + hf][:, 0:512],
                                        h2[:, oc * T + tc * 128:oc * T + (tc + 1) * 128],
                                        wsb[:, slot * NCH * D + oc * D + hf * 512: slot * NCH * D + oc * D + hf * 512 + 512],
                                        start=(oc == 0), stop=False)
                            for hf in range(2):
                                mm = t.matmul(self.pb[bk + hf][:, 0:512],
                                              self.ONES128()[0:1, :],
                                              AP(b3s, hf * 512, [[D, 1], [1, 512]]),
                                              start=False, stop=True)
                            mm.then_inc(self.sP, 1)
                self.sP_n += 32 if l < 2 else 16

                @blk.vector
                def _(v):
                    if bmark[0] is not None:
                        v.wait_ge(self.sD, bmark[0])
                    if ll < 2:
                        dst = h1 if ll == 0 else h2
                        bo = 0 if ll == 0 else 8
                        for i in range(32):
                            oc, tb = i // 4, i % 4
                            bank = self.pb[i % 4]
                            v.wait_ge(self.sP, p0 + i + 1)
                            v.tensor_scalar(dst[:, oc * T + tb * 512: oc * T + tb * 512 + 512],
                                            bank[:, 0:512], bcl[:, bo + oc:bo + oc + 1],
                                            0.0, op0=ALU.add, op1=ALU.max
                                            ).then_inc(self.sV, 1); v.drain()
                    else:
                        for tc in range(16):
                            bk = (tc % 2) * 2
                            v.wait_ge(self.sP, p0 + tc + 1)
                            for hf in range(2):
                                v.tensor_scalar(ytm[:, tc * D + hf * 512: tc * D + hf * 512 + 512],
                                                self.pb[bk + hf][:, 0:512], 0.0,
                                                None, op0=ALU.max); v.drain()
                            v.nop().then_inc(self.sV, 1)
                self.sV_n += 32 if l < 2 else 16
            if l == 2:
                with nc.Block() as blk:
                    ee = e
                    p0 = self.sP_n

                    @blk.tensor
                    def _(t):
                        for hf in range(2):
                            bnk = self.pf if hf == 0 else self.pb[4]
                            for tc in range(16):
                                mm = t.matmul(
                                    bnk[0:1, 0:512],
                                    self.wcolsb[:, tc * 8 + ee: tc * 8 + ee + 1],
                                    ytm[:, tc * D + hf * 512: tc * D + hf * 512 + 512],
                                    start=(tc == 0), stop=(tc == 15))
                        mm.then_inc(self.sP, 1)
                    self.sP_n += 1

                    @blk.vector
                    def _(v):
                        v.wait_ge(self.sP, self.sP_n)
                        v.tensor_copy(vps[0:1, 0:512],
                                      self.pf[0:1, 0:512]); v.drain()
                        v.tensor_copy(vps[0:1, 512:1024],
                                      self.pb[4][0:1, 0:512]).then_inc(self.sV, 1)
                        v.drain()
                    self.sV_n += 1

                    @blk.sync
                    def _(s):
                        s.wait_ge(self.sV, self.sV_n)
                        self.dst(s, "sD", AP(self.cc_vp_i, ee * D, [[1, 1], [1, D]]),
                                 vps[0:1, 0:D])
                        s.wait_ge(self.sD, self.sD_n)
        with nc.Block() as blk:
            @blk.gpsimd
            def _(g):
                self.sC_n += 1
                g.collective_compute(
                    "AllReduce", ALU.add, replica_groups=[list(range(8))],
                    ins=[self.cc_vp_i.ap().opt()], outs=[self.cc_vp_o.ap().opt()],
                ).then_inc(self.sC)
                g.wait_ge(self.sC, self.sC_n)
        with nc.Block() as blk:
            @blk.sync
            def _(s):
                self.dbg_write(s, "dbg_v", self.cc_vp_o.ap())
                self.dbg_write(s, "dbg_vp", self.cc_vp_i.ap())
        scM.__exit__(None, None, None)

    def p9_assemble(self):
        nc = self.nc
        sc_ = self.scope(); sc_.__enter__()
        vsb = sc_.sbuf("vsb", [8, D], BF16)
        x1t = sc_.sbuf("x1t", [128, 2 * 512], F32)
        a1t = sc_.sbuf("a1t", [128, 2 * 512], BF16)
        ot = sc_.sbuf("ot", [128, 2 * 512], F32)
        with nc.Block() as blk:
            @blk.gpsimd
            def _(g):
                self.dst(g, "sG", vsb[:, :], self.cc_vp_o.ap())
                g.wait_ge(self.sG, self.sG_n)
        tiles = [(oc, tb) for oc in range(NCH) for tb in range(4)]
        with nc.Block() as blk:
            p0 = self.sP_n
            v0 = self.sV_n
            o0 = self.sO_n
            lmarks = []
            omarks = []

            @blk.sync
            def _(s):
                for i, (oc, tb) in enumerate(tiles):
                    sl_ = ["sE", "sF"][i % 2]
                    if i >= 2:
                        s.wait_ge(self.sV, v0 + (i - 2) + 1)
                    self.dst(s, sl_, x1t[:, (i % 2) * 512:(i % 2) * 512 + 512],
                             AP(self.x1_park, oc * 128 * T + tb * 512,
                                [[T, 128], [1, 512]]))
                    lmarks.append(self.dst(
                        s, sl_,
                        AP(a1t, (i % 2) * 512, [[2 * 512, 128], [128, 4], [1, 128]]),
                        AP(self.a1_park, tb * 4 * 128 * D + oc * 128,
                           [[D, 128], [128 * D, 4], [1, 128]])))

            @blk.tensor
            def _(t):
                for i, (oc, tb) in enumerate(tiles):
                    bk = i % 2
                    if i >= 1:
                        t.wait_ge(self.sV, v0 + i)
                    t.wait_ge([self.sE, self.sF][i % 2], lmarks[i])
                    t.matmul(self.pb[bk][:, 0:512],
                             vsb[:, oc * 128:(oc + 1) * 128],
                             self.indb[:, tb * 512:(tb + 1) * 512],
                             start=True, stop=True)
                    for j in range(4):
                        tr = t.transpose(
                            self.pt[:, j * 128:(j + 1) * 128],
                            a1t[:, (i % 2) * 512 + j * 128:(i % 2) * 512 + (j + 1) * 128],
                            self.ID())
                    tr.then_inc(self.sP, 1)
            self.sP_n += len(tiles)

            @blk.scalar
            def _(sc):
                for i, (oc, tb) in enumerate(tiles):
                    bk = i % 2
                    sc.wait_ge(self.sV, v0 + i + 1)
                    omarks.append(self.dst(
                        sc, ["sZ", "sZ2"][i % 2],
                        AP(self.out, oc * 128 * T + tb * 512, [[T, 128], [1, 512]]),
                        ot[:, bk * 512:bk * 512 + 512]))
                sc.wait_ge(self.sZ, self.sZ_n)
                sc.wait_ge(self.sZ2, self.sZ2_n)

            @blk.vector
            def _(v):
                for i, (oc, tb) in enumerate(tiles):
                    bk = i % 2
                    v.wait_ge(self.sP, p0 + i + 1)
                    if i >= 2:
                        v.wait_ge([self.sZ, self.sZ2][i % 2], omarks[i - 2])
                    if Prog.P9MODE == "full":
                        v.scalar_tensor_tensor(ot[:, bk * 512:bk * 512 + 512],
                                               self.pb[bk][:, 0:512], 0.0,
                                               x1t[:, (i % 2) * 512:(i % 2) * 512 + 512],
                                               op0=ALU.add, op1=ALU.add); v.drain()
                        v.tensor_tensor(ot[:, bk * 512:bk * 512 + 512],
                                        ot[:, bk * 512:bk * 512 + 512],
                                        self.pt[:, 0:512], op=ALU.add
                                        ).then_inc(self.sV, 1); v.drain()
                    elif Prog.P9MODE == "a1":
                        v.tensor_copy(ot[:, bk * 512:bk * 512 + 512],
                                      self.pt[:, 0:512]).then_inc(self.sV, 1); v.drain()
                    elif Prog.P9MODE == "moe":
                        v.tensor_copy(ot[:, bk * 512:bk * 512 + 512],
                                      self.pb[bk][:, 0:512]).then_inc(self.sV, 1); v.drain()
                    else:
                        v.tensor_copy(ot[:, bk * 512:bk * 512 + 512],
                                      x1t[:, (i % 2) * 512:(i % 2) * 512 + 512]
                                      ).then_inc(self.sV, 1); v.drain()
            self.sV_n += len(tiles)
        sc_.__exit__(None, None, None)

    def p_final(self, parked=False):
        nc = self.nc
        with nc.Block() as blk:
            pk = parked

            @blk.sync
            def _(s):
                if pk:
                    self.dst(s, "sO", self.out.ap(), self.x1_park.ap())
                else:
                    self.dst(s, "sO",
                             AP(self.out, 0, [[T, 128], [128 * T, NCH], [1, T]]),
                             AP(self.xsb, HALO, [[NCH * TE, 128], [TE, NCH], [1, T]]))
                s.wait_ge(self.sO, self.sO_n)




import ml_dtypes

LN_ALPHA = np.log(np.float64(0.99))


def stage_consts():
    import ml_dtypes
    c = {}
    c["id128"] = np.eye(128, dtype=ml_dtypes.bfloat16)
    c["idf1"] = np.ones((1, 1), dtype=np.float32)
    cc = np.zeros((128, 8), dtype=np.float32)
    cc[:, 0] = 1e-5; cc[:, 1] = 1e-8; cc[:, 2] = 0.5; cc[:, 3] = 1.0; cc[:, 4] = -1.0
    c["ccol"] = cc
    c["ones_col"] = np.ones((128, 1), dtype=ml_dtypes.bfloat16)
    c["ones128"] = np.ones((128, 128), dtype=ml_dtypes.bfloat16)
    c["idf8"] = np.eye(8, dtype=np.float32)
    c["idb8"] = np.eye(8, dtype=ml_dtypes.bfloat16)
    c["idf128"] = np.eye(128, dtype=np.float32)
    c["ones_rf"] = np.ones((1, 128), dtype=np.float32)
    kk, qq = np.indices((128, 128))
    band = (kk - qq >= 0) & (kk - qq <= 31)
    c["maskbigA"] = np.where(band, 0.0, 1e4).astype(np.float32)
    kk2, qq2 = np.indices((31, 128))
    band2 = ((kk2 + 128) - qq2 >= 0) & ((kk2 + 128) - qq2 <= 31)
    c["maskbigB"] = np.where(band2, 0.0, 1e4).astype(np.float32)
    kk3, qq3 = np.indices((128, 128))
    c["tri"] = (qq3 >= kk3).astype(ml_dtypes.bfloat16)
    p_, tc_ = np.indices((128, 16))
    jl = (tc_ * 128 + p_).astype(np.float64)
    hi, lo = hilo(-jl * LN_ALPHA)
    c["dk_hi"], c["dk_lo"] = hi, lo
    hi, lo = hilo(jl * LN_ALPHA)
    c["di_hi"], c["di_lo"] = hi, lo
    return c


def stage_core_consts(core):
    b, p = core // 4, core % 4
    out = {}
    delta = np.zeros((128, 4), dtype=np.float64)
    mask = np.zeros((128, 4), dtype=np.float32)
    for q in range(4):
        d = p - q
        if d >= 1:
            delta[:, q] = 2048.0 * d * LN_ALPHA
            mask[:, q] = 1.0
    hi, lo = hilo(delta)
    out["drm_hi"], out["drm_lo"], out["drm_mk"] = hi, lo, mask
    bs = np.zeros((8, 1), dtype=np.float32)
    bs[4 * b: 4 * b + 4, 0] = 1.0
    out["bsel"] = bs
    b2 = np.zeros((2, 1), dtype=np.float32)
    b2[b, 0] = 1.0
    out["b2"] = b2
    bs2 = np.zeros((128, 2), dtype=np.float32)
    bs2[:, b] = 1.0
    out["bsel2"] = bs2
    return out


def stage_weights(inp):
    w = {}
    eq = np.ascontiguousarray(inp["L0_attn_enc_q"])
    w["encq"] = np.ascontiguousarray(eq.reshape(NH, NCH, 128, HD))
    ev = np.ascontiguousarray(inp["L0_attn_enc_v"])
    w["encv"] = np.ascontiguousarray(
        ev.transpose(1, 0, 2).reshape(NCH, 128, NH * HD))
    w["dec"] = np.ascontiguousarray(
        np.asarray(inp["L0_attn_decoder"]).reshape(NCH, 128, D))
    w["heb"] = np.ascontiguousarray(inp["L0_attn_hebbian"])
    w["mw1"] = np.ascontiguousarray(
        np.asarray(inp["L0_attn_m_w1"]).reshape(NCH, 128, 512))
    w["mw2"] = np.ascontiguousarray(
        np.asarray(inp["L0_attn_m_w2"]).reshape(4, 128, D))
    w["mb1"] = np.ascontiguousarray(np.asarray(inp["L0_attn_m_b1"]).reshape(4, 128).T)
    w["mb2"] = np.ascontiguousarray(np.asarray(inp["L0_attn_m_b2"]).reshape(NCH, 128).T)
    w["bmet"] = np.ascontiguousarray(np.asarray(inp["L0_attn_base_metric"]).T)
    w0 = np.asarray(inp["L0_ffn_weight"])
    w["w0T"] = np.ascontiguousarray(w0.T.reshape(NCH, 128, D))
    w["b0c"] = np.ascontiguousarray(np.asarray(inp["L0_ffn_bias"]).reshape(NCH, 128).T)
    wqkv = np.asarray(inp["L1_attn_W_qkv"])
    w["wqkvT"] = np.ascontiguousarray(wqkv.T)
    bqkv = np.asarray(inp["L1_attn_b_qkv"])
    w["bq_c"] = np.ascontiguousarray(bqkv[0:D].reshape(NCH, 128).T)
    w["bk_c"] = np.ascontiguousarray(bqkv[D:2 * D].reshape(NCH, 128).T)
    w["bkv_r"] = np.ascontiguousarray(bqkv[D:3 * D].reshape(1, 2 * D))
    w["gw"] = np.ascontiguousarray(np.asarray(inp["L1_ffn_gate_w"]).reshape(NCH, 128, NE))
    w["gb"] = np.ascontiguousarray(np.asarray(inp["L1_ffn_gate_b"]).reshape(NE, 1))
    w["w1"] = np.ascontiguousarray(np.asarray(inp["L1_ffn_w1"]).reshape(NE, NCH, 128, D))
    w["w2"] = np.ascontiguousarray(np.asarray(inp["L1_ffn_w2"]).reshape(NE, NCH, 128, D))
    w["w3"] = np.ascontiguousarray(np.asarray(inp["L1_ffn_w3"]).reshape(NE, NCH, 128, D))
    w["b1c"] = np.ascontiguousarray(
        np.asarray(inp["L1_ffn_b1"]).reshape(NE, NCH, 128).transpose(0, 2, 1))
    w["b2c"] = np.ascontiguousarray(
        np.asarray(inp["L1_ffn_b2"]).reshape(NE, NCH, 128).transpose(0, 2, 1))
    w["b3r"] = np.ascontiguousarray(np.asarray(inp["L1_ffn_b3"]).reshape(NE, 1, D))
    return w


def stage_x(inp, core):
    b, p = core // 4, core % 4
    x = np.asarray(inp["L0_attn_x"])[b]
    t0 = p * T
    if p == 0:
        halo = np.zeros((HALO, D), dtype=np.float32)
    else:
        halo = x[t0 - HALO: t0]
    sl = np.concatenate([halo, x[t0: t0 + T]], axis=0)
    fm = np.ascontiguousarray(sl.T.reshape(NCH, 128, TE))
    return fm


def stage_all(inp):
    shared = stage_weights(inp)
    consts = stage_consts()
    maps = []
    for core in range(8):
        m = {}
        m.update(shared)
        m.update(consts)
        m.update(stage_core_consts(core))
        m["x_fm"] = stage_x(inp, core)
        maps.append(m)
    return maps


def gather_out(results):
    full = np.zeros((B, TFULL, D), dtype=np.float32)
    for core, r in enumerate(results):
        b, p = core // 4, core % 4
        o = np.asarray(r["out"])
        full[b, p * T:(p + 1) * T] = o.reshape(D, T).T
    return full


_NC_CACHE = {}


def _build_nc():
    if "nc" not in _NC_CACHE:
        prog = Prog(debug={}, phases=99)
        _NC_CACHE["nc"] = prog.build()
    return _NC_CACHE["nc"]


def kernel(**inputs):
    nc = _build_nc()
    maps = stage_all(inputs)
    res = run_bass_kernel_spmd(nc, maps, core_ids=list(range(8)))
    return gather_out(res.results)


# version 06
# speedup vs baseline: 1.0213x; 1.0213x over previous
import numpy as np
import concourse.bass as bass
import concourse.mybir as mybir
from concourse.bass_utils import run_bass_kernel_spmd

F32 = mybir.dt.float32
I32 = mybir.dt.int32
BF16 = mybir.dt.bfloat16
AF = mybir.ActivationFunctionType
ALU = mybir.AluOpType
AX = mybir.AxisListType.X
AP = bass.AP

B, TFULL, D = 2, 8192, 1024
T = 2048
HALO = 31
TE = T + HALO
NCH = 8
NH = 8
HD = 128
NE = 8
CH = 512
NCHUNK = T // CH


def hilo(v):
    v = np.asarray(v, dtype=np.float64)
    hi = v.astype(np.float32)
    lo = (v - hi.astype(np.float64)).astype(np.float32)
    return hi, lo


class Prog:
    P9MODE = "full"

    def __init__(self, debug=(), phases=99):
        self.nc = bass.Bass("TRN2", target_bir_lowering=False)
        self.debug = dict(debug)
        self.phases = phases
        self._keep = []

    def sbuf(self, name, shape, dtype):
        cm = self.nc.sbuf_tensor(name, list(shape), dtype)
        t = cm.__enter__()
        self._keep.append(cm)
        return t

    def psum(self, name, shape, dtype=F32):
        cm = self.nc.psum_tensor(name, list(shape), dtype)
        t = cm.__enter__()
        self._keep.append(cm)
        return t

    def sem(self, name):
        cm = self.nc.semaphore(name)
        h = cm.__enter__()
        self._keep.append(cm)
        return h

    def dst(self, eng, sname, out, in_, **kw):
        nc = self.nc
        n0 = nc.n_instructions()
        eng.dma_start(out, in_, **kw).then_inc(getattr(self, sname), 16)
        cnt = 0
        for nm in nc.inst_names()[n0:]:
            inst = nc.find_inst_by_name(nm)
            if inst is not None and type(inst).__name__ == "InstDMACopy":
                cnt += 1
        assert cnt >= 1
        setattr(self, sname + "_n", getattr(self, sname + "_n") + 16 * cnt)
        return getattr(self, sname + "_n")

    class _Scope:
        def __init__(self, prog):
            self.prog = prog
            self.cms = []

        def sbuf(self, name, shape, dtype):
            cm = self.prog.nc.sbuf_tensor(name, list(shape), dtype)
            t = cm.__enter__()
            self.cms.append(cm)
            return t

        def __enter__(self):
            return self

        def __exit__(self, *a):
            for cm in reversed(self.cms):
                cm.__exit__(None, None, None)
            return False

    def scope(self):
        return Prog._Scope(self)

    def declare_io(self):
        nc = self.nc
        I = lambda n, s, dt=F32: nc.dram_tensor(n, list(s), dt, kind="ExternalInput")
        O = lambda n, s, dt=F32: nc.dram_tensor(n, list(s), dt, kind="ExternalOutput")
        N = lambda n, s, dt=F32: nc.dram_tensor(n, list(s), dt)
        self.x_in = I("x_fm", [NCH, 128, TE])
        self.encq = I("encq", [NH, NCH, 128, HD])
        self.encv = I("encv", [NCH, 128, NH * HD])
        self.dec = I("dec", [NCH, 128, D])
        self.hebw = I("heb", [NH, HD, HD])
        self.mw1 = I("mw1", [NCH, 128, 512])
        self.mw2 = I("mw2", [4, 128, D])
        self.mb1 = I("mb1", [128, 4])
        self.mb2 = I("mb2", [128, 8])
        self.bmet = I("bmet", [128, NH])
        self.w0T = I("w0T", [NCH, 128, D])
        self.b0c = I("b0c", [128, NCH])
        self.wqkvT = I("wqkvT", [D, 3 * D])
        self.bq_c = I("bq_c", [128, NCH])
        self.bk_c = I("bk_c", [128, NCH])
        self.bkv_r = I("bkv_r", [1, 2 * D])
        self.gwc = I("gw", [NCH, 128, NE])
        self.gbc = I("gb", [NE, 1])
        self.w1 = I("w1", [NE, NCH, 128, D])
        self.w2 = I("w2", [NE, NCH, 128, D])
        self.w3 = I("w3", [NE, NCH, 128, D])
        self.b1c = I("b1c", [NE, 128, NCH])
        self.b2c = I("b2c", [NE, 128, NCH])
        self.b3r = I("b3r", [NE, 1, D])
        self.id128 = I("id128", [128, 128], BF16)
        self.idf1 = I("idf1", [1, 1])
        self.ones_col = I("ones_col", [128, 1], BF16)
        self.maskbigA = I("maskbigA", [128, 128])
        self.maskbigB = I("maskbigB", [31, 128])
        self.tri_in = I("tri", [128, 128], BF16)
        self.dk_hi = I("dk_hi", [128, 16]); self.dk_lo = I("dk_lo", [128, 16])
        self.di_hi = I("di_hi", [128, 16]); self.di_lo = I("di_lo", [128, 16])
        self.drm_hi = I("drm_hi", [128, 4]); self.drm_lo = I("drm_lo", [128, 4])
        self.drm_mk = I("drm_mk", [128, 4])
        self.bsel_in = I("bsel", [8, 1])
        self.b2_in = I("b2", [2, 1])
        self.ccol_in = I("ccol", [128, 8])
        self.ones128_in = I("ones128", [128, 128], BF16)
        self.idf8_in = I("idf8", [8, 8])
        self.idb8_in = I("idb8", [8, 8], BF16)
        self.idf128_in = I("idf128", [128, 128])
        self.onesrf_in = I("ones_rf", [1, 128])
        self.bsel2_in = I("bsel2", [128, 2])
        self.strided_off = 31

        self.cc_sig_i = N("cc_sig_i", [2, D]); self.cc_sig_o = N("cc_sig_o", [2, D])
        self.cc_gm_i = N("cc_gm_i", [2, NE]); self.cc_gm_o = N("cc_gm_o", [2, NE])
        self.cc_gs_i = N("cc_gs_i", [2, NE]); self.cc_gs_o = N("cc_gs_o", [2, NE])
        self.cc_ut_i = N("cc_ut_i", [128, NCH * D], BF16)
        self.cc_ut_o = N("cc_ut_o", [4 * 128, NCH * D], BF16)
        self.cc_vp_i = N("cc_vp_i", [NE, D]); self.cc_vp_o = N("cc_vp_o", [NE, D])
        self.x1_park = N("x1_park", [NCH, 128, T])
        self.a1_park = N("a1_park", [16, 128, D], BF16)
        self.out = O("out", [NCH, 128, T])
        self.dbg = {}
        for name, (shape, dt) in self.debug.items():
            self.dbg[name] = O(name, list(shape), dt)

    def build(self):
        self.declare_io()
        nc = self.nc
        self.sD = self.sem("sD"); self.sD_n = 0
        self.sG = self.sem("sG"); self.sG_n = 0
        self.sP = self.sem("sP"); self.sP_n = 0
        self.sS = self.sem("sS"); self.sS_n = 0
        self.sV = self.sem("sV"); self.sV_n = 0
        self.sC = self.sem("sC"); self.sC_n = 0
        self.sO = self.sem("sO"); self.sO_n = 0
        self.sW = self.sem("sW"); self.sW_n = 0
        self.sX = self.sem("sX"); self.sX_n = 0
        self.sY = self.sem("sY"); self.sY_n = 0
        self.sA = self.sem("sA"); self.sA_n = 0
        self.sB = self.sem("sB"); self.sB_n = 0
        self.sE = self.sem("sE"); self.sE_n = 0
        self.sF = self.sem("sF"); self.sF_n = 0
        self.sZ = self.sem("sZ"); self.sZ_n = 0
        self.sZ2 = self.sem("sZ2"); self.sZ2_n = 0

        self.pb = [self.psum(f"pb{i}", [128, 512]) for i in range(6)]
        self.pt = self.psum("pt", [128, 512], BF16)
        self.pf = self.psum("pf", [128, 512], F32)

        self.xnsb = self.sbuf("xnsb", [128, NCH * TE], BF16)
        self.scr = self.sbuf("scr", [128, TE], F32)
        self.cstb = self.sbuf("cstb", [128, 400], BF16)
        self.cstf = self.sbuf("cstf", [128, 672], F32)

        self.emit()
        nc.finalize()
        return nc

    def ID(self):  return self.cstb[:, 0:128]
    def OC(self):  return self.cstb[:, 128:129]
    def ONES128(self): return self.cstb[:, 136:264]
    def IDB8(self): return self.cstb[0:8, 264:272]
    def TRI(self): return self.cstb[:, 272:400]
    def MBA(self): return self.cstf[:, 0:128]
    def MBB(self): return self.cstf[0:31, 128:256]
    def DKM(self): return self.cstf[:, 256:272]
    def DIC(self): return self.cstf[:, 272:288]
    def DRM(self): return self.cstf[:, 288:292]
    def BSEL(self): return self.cstf[0:8, 292:293]
    def B2(self):  return self.cstf[0:2, 293:294]
    def IDF(self): return self.cstf[0:1, 294:295]
    def IDF8(self): return self.cstf[0:8, 296:304]

    def CCOL(self, i, parts=128):
        return self.cstf[0:parts, 304 + i: 305 + i]

    def BSEL2(self, parts=128):
        return self.cstf[0:parts, 312:314]

    def IDF128(self):
        return self.cstf[:, 384:512]

    def IDF32(self):
        return self.cstf[0:32, 384:416]

    def IDF2(self):
        return self.cstf[0:2, 384:386]

    def ONESRF(self):
        return self.cstf[0:1, 512:640]

    def xc(self, c, lo=0, n=TE):
        return self.xsb[:, c * TE + lo: c * TE + lo + n]

    def xn(self, c, lo=0, n=TE):
        return self.xnsb[:, c * TE + lo: c * TE + lo + n]

    def dbg_write(self, s, name, src_ap):
        if name in self.dbg:
            self.dst(s, "sO", self.dbg[name].ap(), src_ap)
            s.wait_ge(self.sO, self.sO_n)

    def bcast_row(self, row_fn, nblk, dst_fn, banks=(4, 5)):
        nc = self.nc
        with nc.Block() as blk:
            p0 = self.sP_n
            v0 = self.sV_n

            @blk.tensor
            def _(t):
                for i in range(nblk):
                    bank = self.pb[banks[i % 2]]
                    if i >= 2:
                        t.wait_ge(self.sV, v0 + (i - 2) + 1)
                    t.matmul(bank[:, 0:row_fn(i).ap[-1][1]], self.ONESRF(),
                             row_fn(i), start=True, stop=True).then_inc(self.sP, 1)
            self.sP_n += nblk

            @blk.vector
            def _(v):
                for i in range(nblk):
                    bank = self.pb[banks[i % 2]]
                    n = row_fn(i).ap[-1][1]
                    v.wait_ge(self.sP, p0 + i + 1)
                    v.tensor_copy(dst_fn(i), bank[:, 0:n]).then_inc(self.sV, 1)
                    v.drain()
            self.sV_n += nblk

    def rowcol(self, row_ap_fn, n, out_tile):
        nc = self.nc
        with nc.Block() as blk:
            @blk.tensor
            def _(t):
                for i in range(n):
                    tr = t.transpose(self.pf[:, i:i + 1], row_ap_fn(i), self.IDF())
                tr.then_inc(self.sP, 1)
            self.sP_n += 1

            @blk.vector
            def _(v):
                v.wait_ge(self.sP, self.sP_n)
                v.tensor_copy(out_tile, self.pf[:, 0:n]); v.drain()

    def emit(self):
        ph = self.phases
        with self.scope() as sc0:
            self.xsb = sc0.sbuf("xsb", [128, NCH * TE], F32)
            self.p0_load()
            with self.scope() as bsc:
                self.mbc = bsc.sbuf("mbc01", [128, TE], F32)
                self.rbc = bsc.sbuf("rbc01", [128, TE], F32)
                if ph >= 1: self.p1_ln("0")
                if ph >= 2: self.p2_ffn0()
            if ph >= 3: self.p3_attn0()
            if ph >= 4:
                with self.scope() as bsc:
                    self.mbc = bsc.sbuf("mbc4", [128, TE], F32)
                    self.rbc = bsc.sbuf("rbc4", [128, TE], F32)
                    self.p4_ln1()
            if ph < 9:
                self.p_final(parked=(ph >= 4))
        if ph >= 5:
            with self.scope() as scG:
                self.wcolsb = scG.sbuf("wcolsb", [128, 16 * NE], BF16)
                self.indb = scG.sbuf("indb", [NE, T], BF16)
                if ph >= 6: self.p6_gate()
                self.p5_qkv()
                if ph >= 7: self.p7_attn1()
                if ph >= 8: self.p8_moe()
                if ph >= 9: self.p9_assemble()

    def p0_load(self):
        nc = self.nc
        with nc.Block() as blk:
            @blk.sync
            def _(s):
                def ld(dst, src_ap):
                    self.dst(s, "sD", dst, src_ap)
                ld(self.xsb[:, :], AP(self.x_in, 0, [[TE, 128], [128 * TE, NCH], [1, TE]]))
                ld(self.ID(), self.id128[:, :])
                ld(self.OC(), self.ones_col[:, :])
                ld(self.MBA(), self.maskbigA[:, :])
                ld(self.MBB(), self.maskbigB[:, :])
                ld(self.TRI(), self.tri_in[:, :])
                ld(self.BSEL(), self.bsel_in[:, :])
                ld(self.B2(), self.b2_in[:, :])
                ld(self.IDF(), self.idf1[:, :])
                ld(self.cstf[:, 304:312], self.ccol_in[:, :])
                ld(self.ONES128(), self.ones128_in[:, :])
                ld(self.cstf[:, 384:400], self.dk_hi[:, :])
                ld(self.cstf[:, 400:416], self.di_hi[:, :])
                ld(self.cstf[:, 416:432], self.dk_lo[:, :])
                ld(self.cstf[:, 432:448], self.di_lo[:, :])
                ld(self.cstf[:, 448:452], self.drm_hi[:, :])
                ld(self.cstf[:, 452:456], self.drm_lo[:, :])
                ld(self.cstf[:, 456:460], self.drm_mk[:, :])
                s.wait_ge(self.sD, self.sD_n)
        with nc.Block() as blk:
            @blk.vector
            def _(v):
                v.tensor_add(self.DKM(), self.cstf[:, 384:400], self.cstf[:, 416:432]); v.drain()
                v.tensor_add(self.DIC(), self.cstf[:, 400:416], self.cstf[:, 432:448]); v.drain()
                v.tensor_add(self.DRM(), self.cstf[:, 448:452], self.cstf[:, 452:456]); v.drain()
        with nc.Block() as blk:
            @blk.scalar
            def _(sc):
                sc.activation(self.DKM(), self.DKM(), AF.Exp); sc.drain()
                sc.activation(self.DIC(), self.DIC(), AF.Exp); sc.drain()
                sc.activation(self.DRM(), self.DRM(), AF.Exp); sc.drain()
        with nc.Block() as blk:
            @blk.vector
            def _(v):
                v.tensor_scalar(self.DKM(), self.DKM(), 0.03125, None, op0=ALU.mult); v.drain()
                v.tensor_mul(self.DRM(), self.DRM(), self.cstf[:, 456:460]); v.drain()
        with nc.Block() as blk:
            @blk.sync
            def _(s):
                self.dst(s, "sD", self.IDF8(), self.idf8_in[:, :])
                self.dst(s, "sD", self.IDB8(), self.idb8_in[:, :])
                self.dst(s, "sD", self.IDF128(), self.idf128_in[:, :])
                self.dst(s, "sD", self.ONESRF(), self.onesrf_in[:, :])
                self.dst(s, "sD", self.cstf[:, 312:314], self.bsel2_in[:, :])
                s.wait_ge(self.sD, self.sD_n)

    def p1_ln(self, tag, ntok=TE, lo=0):
        nc = self.nc
        nbl = (ntok + 511) // 512
        lsc = self.scope(); lsc.__enter__()
        self.scrb = lsc.sbuf(f"scrb{tag}", [128, 2 * TE], BF16)
        self.rows = lsc.sbuf(f"rows{tag}", [1, 2 * TE], F32)
        with nc.Block() as blk:
            @blk.vector
            def _(v):
                for c in range(NCH):
                    v.tensor_copy(self.xn(c, lo, ntok), self.xc(c, lo, ntok)); v.drain()
        with nc.Block() as blk:
            @blk.tensor
            def _(t):
                for c in range(NCH):
                    for bi in range(nbl):
                        n = min(512, ntok - bi * 512)
                        t.matmul(self.pb[bi][0:1, 0:n], self.OC(),
                                 self.xn(c, lo + bi * 512, n),
                                 start=(c == 0), stop=(c == NCH - 1))
        with nc.Block() as blk:
            @blk.vector
            def _(v):
                for bi in range(nbl):
                    n = min(512, ntok - bi * 512)
                    v.tensor_scalar(self.rows[0:1, bi * 512: bi * 512 + n],
                                    self.pb[bi][0:1, 0:n], 1.0 / 1024.0, None,
                                    op0=ALU.mult); v.drain()
        with nc.Block() as blk:
            s_base = self.sS_n
            p_base = self.sP_n

            @blk.scalar
            def _(sc):
                for c in range(NCH):
                    if c >= 2:
                        sc.wait_ge(self.sP, p_base + c - 1)
                    half = (c % 2) * TE
                    sc.activation(self.scrb[:, half: half + ntok],
                                  self.xn(c, lo, ntok), AF.Square
                                  ).then_inc(self.sS, 1); sc.drain()
            self.sS_n += NCH

            @blk.tensor
            def _(t):
                for c in range(NCH):
                    t.wait_ge(self.sS, s_base + c + 1)
                    half = (c % 2) * TE
                    for bi in range(nbl):
                        n = min(512, ntok - bi * 512)
                        mm = t.matmul(self.pb[bi][0:1, 0:n], self.OC(),
                                      self.scrb[:, half + bi * 512: half + bi * 512 + n],
                                      start=(c == 0), stop=(c == NCH - 1))
                    mm.then_inc(self.sP, 1)
            self.sP_n += NCH
        with nc.Block() as blk:
            @blk.vector
            def _(v):
                for bi in range(nbl):
                    n = min(512, ntok - bi * 512)
                    v.tensor_scalar(self.rows[0:1, TE + bi * 512: TE + bi * 512 + n],
                                    self.pb[bi][0:1, 0:n], 1.0 / 1024.0, None,
                                    op0=ALU.mult); v.drain()
                v.tensor_mul(self.scr[0:1, 0:ntok],
                             self.rows[0:1, 0:ntok], self.rows[0:1, 0:ntok]); v.drain()
                v.tensor_sub(self.rows[0:1, TE: TE + ntok],
                             self.rows[0:1, TE: TE + ntok],
                             self.scr[0:1, 0:ntok]); v.drain()
        with nc.Block() as blk:
            @blk.scalar
            def _(sc):
                sc.activation(self.rows[0:1, TE: TE + ntok],
                              self.rows[0:1, TE: TE + ntok], AF.Sqrt,
                              bias=self.CCOL(0, 1))
                sc.drain()
        with nc.Block() as blk:
            @blk.vector
            def _(v):
                v.reciprocal(self.rows[0:1, TE: TE + ntok],
                             self.rows[0:1, TE: TE + ntok]); v.drain()
        nb = (ntok + 511) // 512
        self.bcast_row(lambda i: self.rows[0:1, i * 512: i * 512 + min(512, ntok - i * 512)],
                       nb, lambda i: self.mbc[:, i * 512: i * 512 + min(512, ntok - i * 512)])
        self.bcast_row(lambda i: self.rows[0:1, TE + i * 512: TE + i * 512 + min(512, ntok - i * 512)],
                       nb, lambda i: self.rbc[:, i * 512: i * 512 + min(512, ntok - i * 512)])
        with nc.Block() as blk:
            @blk.vector
            def _(v):
                for c in range(NCH):
                    v.tensor_sub(self.scr[:, 0:ntok], self.xc(c, lo, ntok),
                                 self.mbc[:, 0:ntok]); v.drain()
                    v.tensor_mul(self.xn(c, lo, ntok), self.scr[:, 0:ntok],
                                 self.rbc[:, 0:ntok]); v.drain()
        with nc.Block() as blk:
            @blk.sync
            def _(s):
                self.dbg_write(s, f"dbg_xn{tag}",
                               AP(self.xnsb, 0, [[NCH * TE, 128], [1, NCH * TE]]))
        lsc.__exit__(None, None, None)

    def p2_ffn0(self):
        nc = self.nc
        sc_ = self.scope(); sc_.__enter__()
        wq = sc_.sbuf("wq", [128, NCH * D], BF16)
        xq = sc_.sbuf("xq", [128, NCH * T], BF16)
        wamx = sc_.sbuf("wamx", [128, D], F32)
        tmp = sc_.sbuf("tmp_p2", [128, 2048], F32)
        itmp = sc_.sbuf("itmp_p2", [128, 2048], I32)
        wraw = sc_.sbuf("wraw_p2", [128, D], F32)
        swc = sc_.sbuf("swc", [128, NCH], F32)
        b0s = sc_.sbuf("b0s", [128, NCH], F32)

        for c in range(NCH):
            with nc.Block() as blk:
                @blk.sync
                def _(s):
                    self.dst(s, "sD", wraw[:, :], self.w0T[c])
                    if c == 0:
                        self.dst(s, "sD", b0s[:, :], self.b0c[:, :])
                    s.wait_ge(self.sD, self.sD_n)
            with nc.Block() as blk:
                cc = c

                @blk.vector
                def _(v):
                    v.tensor_scalar(tmp[:, 0:D], wraw[:, :], -1.0, None,
                                    op0=ALU.mult); v.drain()
                    v.tensor_tensor(tmp[:, 0:D], tmp[:, 0:D], wraw[:, :],
                                    op=ALU.max); v.drain()
                    if cc == 0:
                        v.tensor_copy(wamx[:, :], tmp[:, 0:D])
                    else:
                        v.tensor_tensor(wamx[:, :], wamx[:, :], tmp[:, 0:D],
                                        op=ALU.max)
                    v.drain()
        for g2_ in range(2):
            with nc.Block() as blk:
                gg = g2_
                p0 = self.sP_n

                @blk.tensor
                def _(t):
                    for j in range(4):
                        ocx = gg * 4 + j
                        tr = t.transpose(self.pf[:, j * 128:(j + 1) * 128],
                                         wamx[:, ocx * 128:(ocx + 1) * 128],
                                         self.IDF128())
                    tr.then_inc(self.sP, 1)
                self.sP_n += 1

                @blk.vector
                def _(v):
                    v.wait_ge(self.sP, self.sP_n)
                    for j in range(4):
                        ocx = gg * 4 + j
                        v.tensor_reduce(swc[:, ocx:ocx + 1],
                                        self.pf[:, j * 128:(j + 1) * 128], AX,
                                        ALU.max); v.drain()
        with nc.Block() as blk:
            @blk.vector
            def _(v):
                v.tensor_scalar(swc[:, :], swc[:, :], 1e-8, None, op0=ALU.max)
                v.drain()
        for half in range(2):
            with nc.Block() as blk:
                hh = half
                p0 = self.sP_n

                @blk.tensor
                def _(t):
                    for ocx in range(4):
                        tr = t.transpose(self.pf[0:1, ocx * 128:(ocx + 1) * 128],
                                         swc[:, hh * 4 + ocx: hh * 4 + ocx + 1],
                                         self.IDF128())
                    tr.then_inc(self.sP, 1)
                self.sP_n += 1

                @blk.vector
                def _(v):
                    v.wait_ge(self.sP, self.sP_n)
                    v.tensor_copy(self.scr[0:1, hh * 512: hh * 512 + 512],
                                  self.pf[0:1, 0:512]); v.drain()
        with nc.Block() as blk:
            @blk.vector
            def _(v):
                v.reciprocal(self.scr[0:1, 0:D], self.scr[0:1, 0:D]); v.drain()
        self.bcast_row(lambda i: self.scr[0:1, i * 512:(i + 1) * 512], 2,
                       lambda i: wamx[:, i * 512:(i + 1) * 512])
        with nc.Block() as blk:
            @blk.vector
            def _(v):
                v.tensor_scalar(swc[:, :], swc[:, :], 1.0 / 127.0, None,
                                op0=ALU.mult); v.drain()
        for c in range(NCH):
            with nc.Block() as blk:
                @blk.sync
                def _(s):
                    self.dst(s, "sD", wraw[:, :], self.w0T[c])
                    s.wait_ge(self.sD, self.sD_n)
            with nc.Block() as blk:
                cc = c

                @blk.vector
                def _(v):
                    cs = slice(cc * D, (cc + 1) * D)
                    v.tensor_mul(self.scr[:, 0:D], wraw[:, :], wamx[:, :]); v.drain()
                    v.tensor_scalar(self.scr[:, 0:D], self.scr[:, 0:D],
                                    127.0, 0.5, op0=ALU.mult, op1=ALU.add); v.drain()
                    v.tensor_copy(itmp[:, 0:D], self.scr[:, 0:D]); v.drain()
                    v.tensor_copy(tmp[:, 0:D], itmp[:, 0:D]); v.drain()
                    v.tensor_tensor(self.scr[:, 0:D], tmp[:, 0:D],
                                    self.scr[:, 0:D], op=ALU.is_gt); v.drain()
                    v.tensor_sub(wq[:, cs], tmp[:, 0:D],
                                 self.scr[:, 0:D]); v.drain()

        with nc.Block() as blk:
            @blk.vector
            def _(v):
                v.tensor_scalar(self.scr[:, 0:T], self.xn(0, HALO, T), -1.0,
                                None, op0=ALU.mult); v.drain()
                v.tensor_tensor(self.scr[:, 0:T], self.scr[:, 0:T],
                                self.xn(0, HALO, T), op=ALU.max); v.drain()
                for c in range(1, NCH):
                    v.tensor_scalar(tmp[:, 0:T], self.xn(c, HALO, T), -1.0,
                                    None, op0=ALU.mult); v.drain()
                    v.tensor_tensor(tmp[:, 0:T], tmp[:, 0:T],
                                    self.xn(c, HALO, T), op=ALU.max); v.drain()
                    v.tensor_tensor(self.scr[:, 0:T], self.scr[:, 0:T],
                                    tmp[:, 0:T], op=ALU.max); v.drain()
        with nc.Block() as blk:
            p0 = self.sP_n
            v0 = self.sV_n

            @blk.tensor
            def _(t):
                for g4 in range(4):
                    if g4 >= 1:
                        t.wait_ge(self.sV, v0 + g4)
                    for j in range(4):
                        tc = g4 * 4 + j
                        tr = t.transpose(self.pf[:, j * 128:(j + 1) * 128],
                                         self.scr[:, tc * 128:(tc + 1) * 128],
                                         self.IDF128())
                    tr.then_inc(self.sP, 1)
            self.sP_n += 4

            @blk.vector
            def _(v):
                for g4 in range(4):
                    v.wait_ge(self.sP, p0 + g4 + 1)
                    for j in range(4):
                        tc = g4 * 4 + j
                        v.tensor_reduce(tmp[:, tc:tc + 1],
                                        self.pf[:, j * 128:(j + 1) * 128],
                                        AX, ALU.max); v.drain()
                    v.nop().then_inc(self.sV, 1)
            self.sV_n += 4
        for grp4 in range(4):
            with nc.Block() as blk:
                gg = grp4
                p0 = self.sP_n

                @blk.tensor
                def _(t):
                    for j in range(4):
                        tr = t.transpose(self.pf[0:1, j * 128:(j + 1) * 128],
                                         tmp[:, gg * 4 + j: gg * 4 + j + 1],
                                         self.IDF128())
                    tr.then_inc(self.sP, 1)
                self.sP_n += 1

                @blk.vector
                def _(v):
                    v.wait_ge(self.sP, self.sP_n)
                    v.tensor_copy(self.scr[0:1, gg * 512: gg * 512 + 512],
                                  self.pf[0:1, 0:512]); v.drain()
        with nc.Block() as blk:
            @blk.vector
            def _(v):
                v.tensor_scalar(self.scr[0:1, 0:T], self.scr[0:1, 0:T], 1e-8,
                                None, op0=ALU.max); v.drain()
                v.tensor_scalar(self.scr[1:2, 0:T] if False else
                                self.mbc[0:1, 0:T], self.scr[0:1, 0:T],
                                1.0 / 127.0, None, op0=ALU.mult); v.drain()
                v.reciprocal(self.scr[0:1, 0:T], self.scr[0:1, 0:T]); v.drain()
        self.bcast_row(lambda i: self.mbc[0:1, i * 512:(i + 1) * 512], 4,
                       lambda i: self.rbc[:, i * 512:(i + 1) * 512])
        self.bcast_row(lambda i: self.scr[0:1, i * 512:(i + 1) * 512], 4,
                       lambda i: self.mbc[:, i * 512:(i + 1) * 512])
        with nc.Block() as blk:
            @blk.vector
            def _(v):
                for c in range(NCH):
                    cs = slice(c * T, (c + 1) * T)
                    v.tensor_tensor(self.scr[:, 0:T], self.xn(c, HALO, T),
                                    self.mbc[:, 0:T], op=ALU.mult); v.drain()
                    v.tensor_scalar(self.scr[:, 0:T], self.scr[:, 0:T],
                                    127.0, 0.5, op0=ALU.mult, op1=ALU.add); v.drain()
                    v.tensor_copy(itmp[:, 0:T], self.scr[:, 0:T]); v.drain()
                    v.tensor_copy(tmp[:, 0:T], itmp[:, 0:T]); v.drain()
                    v.tensor_tensor(self.scr[:, 0:T], tmp[:, 0:T],
                                    self.scr[:, 0:T], op=ALU.is_gt); v.drain()
                    v.tensor_sub(xq[:, cs], tmp[:, 0:T],
                                 self.scr[:, 0:T]); v.drain()
        with nc.Block() as blk:
            @blk.sync
            def _(s):
                self.dbg_write(s, "dbg_xq", AP(xq, 0, [[NCH * T, 128], [1, NCH * T]]))
                self.dbg_write(s, "dbg_wq", AP(wq, 0, [[NCH * D, 128], [1, NCH * D]]))

        with nc.Block() as blk:
            @blk.vector
            def _(v):
                for oc in range(NCH):
                    v.tensor_scalar(self.xc(oc, HALO, T), self.xc(oc, HALO, T),
                                    b0s[:, oc:oc + 1], None, op0=ALU.add); v.drain()
        tiles = [(oc, tb) for oc in range(NCH) for tb in range(T // 512)]
        with nc.Block() as blk:
            p_base = self.sP_n
            v_base = self.sV_n

            @blk.tensor
            def _(t):
                for i, (oc, tb) in enumerate(tiles):
                    bank = self.pb[i % 4]
                    if i >= 4:
                        t.wait_ge(self.sV, v_base + (i - 4) + 1)
                    for dc in range(NCH):
                        mm = t.matmul(bank[:, 0:512],
                                      wq[:, dc * D + oc * 128: dc * D + (oc + 1) * 128],
                                      xq[:, dc * T + tb * 512: dc * T + (tb + 1) * 512],
                                      start=(dc == 0), stop=(dc == NCH - 1))
                    mm.then_inc(self.sP, 1)
            self.sP_n += len(tiles)

            @blk.vector
            def _(v):
                for i, (oc, tb) in enumerate(tiles):
                    bank = self.pb[i % 4]
                    v.wait_ge(self.sP, p_base + i + 1)
                    v.tensor_tensor(self.mbc[:, 0:512], bank[:, 0:512],
                                    self.rbc[:, tb * 512:(tb + 1) * 512],
                                    op=ALU.mult).then_inc(self.sV, 1); v.drain()
                    v.scalar_tensor_tensor(self.xc(oc, HALO + tb * 512, 512),
                                           self.mbc[:, 0:512], swc[:, oc:oc + 1],
                                           self.xc(oc, HALO + tb * 512, 512),
                                           op0=ALU.mult, op1=ALU.add); v.drain()
            self.sV_n += len(tiles)
        with nc.Block() as blk:
            @blk.sync
            def _(s):
                self.dbg_write(s, "dbg_xffn",
                               AP(self.xsb, 0, [[NCH * TE, 128], [1, NCH * TE]]))
        sc_.__exit__(None, None, None)

    def emit_qh(self, qh, enqh):
        nc = self.nc
        nbl = (TE + 511) // 512
        with nc.Block() as blk:
            p0 = self.sP_n
            v0 = self.sV_n

            @blk.tensor
            def _(t):
                for bi in range(nbl):
                    n = min(512, TE - bi * 512)
                    if bi >= 2:
                        t.wait_ge(self.sV, v0 + (bi - 2) + 1)
                    for dc in range(NCH):
                        mm = t.matmul(self.pb[bi % 2][:, 0:n],
                                      enqh[:, dc * HD: (dc + 1) * HD],
                                      self.xn(dc, bi * 512, n),
                                      start=(dc == 0), stop=(dc == NCH - 1))
                    mm.then_inc(self.sP, 1)
            self.sP_n += nbl

            @blk.vector
            def _(v):
                for bi in range(nbl):
                    n = min(512, TE - bi * 512)
                    v.wait_ge(self.sP, p0 + bi + 1)
                    v.tensor_scalar(qh[:, bi * 512: bi * 512 + n],
                                    self.pb[bi % 2][:, 0:n], 0.0, None,
                                    op0=ALU.max).then_inc(self.sV, 1); v.drain()
            self.sV_n += nbl

    def p3_attn0(self):
        nc = self.nc
        sc_ = self.scope(); sc_.__enter__()
        mcol = sc_.sbuf("mcol", [128, 28], F32)
        scA = self.scope(); scA.__enter__()
        qh = scA.sbuf("qhA", [128, TE], BF16)
        enqh = scA.sbuf("enqhA", [128, NCH * HD], BF16)
        sigp = scA.sbuf("sigp", [128, 12], F32)
        bmsb = scA.sbuf("bmsb", [128, 24], F32)
        ehb = scA.sbuf("ehbA", [128, 16], BF16)

        for h in range(NH):
            with nc.Block() as blk:
                hh = h

                @blk.gpsimd
                def _(g):
                    self.dst(g, "sG", enqh[:, :],
                             AP(self.encq, hh * NCH * 128 * HD,
                                [[HD, 128], [128 * HD, NCH], [1, HD]]))
                    g.wait_ge(self.sG, self.sG_n)
            self.emit_qh(qh, enqh)
            with nc.Block() as blk:
                hh = h

                @blk.vector
                def _(v):
                    v.tensor_reduce(sigp[:, 8:9], qh[:, HALO: HALO + T], AX,
                                    ALU.add); v.drain()
                    v.tensor_reduce(sigp[:, 9:10],
                                    AP(qh, HALO, [[TE, 128], [8, 256]]), AX,
                                    ALU.add); v.drain()
                    v.tensor_scalar(sigp[:, 8:9], sigp[:, 8:9], 1.0 / 8192.0,
                                    None, op0=ALU.mult); v.drain()
                    v.scalar_tensor_tensor(sigp[:, hh:hh + 1], sigp[:, 9:10],
                                           0.5 / 1024.0, sigp[:, 8:9],
                                           op0=ALU.mult, op1=ALU.add); v.drain()
        for grp4 in range(2):
            with nc.Block() as blk:
                gg = grp4
                p0 = self.sP_n

                @blk.tensor
                def _(t):
                    for j in range(4):
                        tr = t.transpose(self.pf[0:1, j * 128:(j + 1) * 128],
                                         sigp[:, gg * 4 + j: gg * 4 + j + 1],
                                         self.IDF128())
                    tr.then_inc(self.sP, 1)
                self.sP_n += 1

                @blk.vector
                def _(v):
                    v.wait_ge(self.sP, self.sP_n)
                    v.tensor_copy(self.scr[0:1, gg * 512: gg * 512 + 512],
                                  self.pf[0:1, 0:512]); v.drain()
        sig2 = scA.sbuf("sig2", [2, D], F32)
        with nc.Block() as blk:
            @blk.sync
            def _(s):
                self.dst(s, "sD", sig2[0:1, :], self.scr[0:1, 0:D])
                self.dst(s, "sD", sig2[1:2, :], self.scr[0:1, 0:D])
                s.wait_ge(self.sD, self.sD_n)
        with nc.Block() as blk:
            @blk.vector
            def _(v):
                v.tensor_scalar(sig2[:, :], sig2[:, :], self.B2(), None,
                                op0=ALU.mult); v.drain()
        with nc.Block() as blk:
            @blk.sync
            def _(s):
                self.dst(s, "sD", self.cc_sig_i[:, :], sig2[:, :])
                s.wait_ge(self.sD, self.sD_n)
        with nc.Block() as blk:
            @blk.gpsimd
            def _(g):
                self.sC_n += 1
                g.collective_compute(
                    "AllReduce", ALU.add, replica_groups=[list(range(8))],
                    ins=[self.cc_sig_i.ap().opt()], outs=[self.cc_sig_o.ap().opt()],
                ).then_inc(self.sC)
                g.wait_ge(self.sC, self.sC_n)
        with nc.Block() as blk:
            @blk.sync
            def _(s):
                self.dst(s, "sD", sig2[:, :], self.cc_sig_o[:, :])
                s.wait_ge(self.sD, self.sD_n)
        with nc.Block() as blk:
            p0 = self.sP_n

            @blk.tensor
            def _(t):
                for h in range(NH):
                    tr = t.transpose(self.pf[:, h * 2:(h + 1) * 2],
                                     sig2[:, h * 128:(h + 1) * 128], self.IDF2())
                tr.then_inc(self.sP, 1)
            self.sP_n += 1

            @blk.vector
            def _(v):
                v.wait_ge(self.sP, self.sP_n)
                for h in range(NH):
                    v.tensor_tensor(self.pf[:, h * 2:(h + 1) * 2],
                                    self.pf[:, h * 2:(h + 1) * 2],
                                    self.BSEL2(), op=ALU.mult); v.drain()
                    v.tensor_reduce(mcol[:, h:h + 1], self.pf[:, h * 2:(h + 1) * 2],
                                    AX, ALU.add); v.drain()

        with self.scope() as msc:
            mwsb = msc.sbuf("mwsb", [128, NCH * 512], BF16)
            mw2sb = msc.sbuf("mw2sb", [128, 4 * D], BF16)
            with nc.Block() as blk:
                @blk.gpsimd
                def _(g):
                    self.dst(g, "sG", mwsb[:, :],
                             AP(self.mw1, 0, [[512, 128], [128 * 512, NCH], [1, 512]]))
                    self.dst(g, "sG", mw2sb[:, :],
                             AP(self.mw2, 0, [[D, 128], [128 * D, 4], [1, D]]))
                    g.wait_ge(self.sG, self.sG_n)

                @blk.sync
                def _(s):
                    self.dst(s, "sD", bmsb[:, 0:8], self.bmet[:, :])
                    self.dst(s, "sD", bmsb[:, 8:12], self.mb1[:, :])
                    self.dst(s, "sD", bmsb[:, 12:20], self.mb2[:, :])
                    s.wait_ge(self.sD, self.sD_n)
            with nc.Block() as blk:
                @blk.vector
                def _(v):
                    v.tensor_copy(ehb[:, 0:8], mcol[:, 0:8]); v.drain()
            with nc.Block() as blk:
                @blk.tensor
                def _(t):
                    for m in range(4):
                        for dc in range(NCH):
                            mm = t.matmul(self.pf[:, m:m + 1],
                                          mwsb[:, dc * 512 + m * 128: dc * 512 + (m + 1) * 128],
                                          ehb[:, dc:dc + 1],
                                          start=(dc == 0), stop=(dc == NCH - 1))
                    mm.then_inc(self.sP, 1)
                self.sP_n += 1

                @blk.scalar
                def _(sc):
                    sc.wait_ge(self.sP, self.sP_n)
                    for m in range(4):
                        sc.activation(mcol[:, 8 + m:9 + m], self.pf[:, m:m + 1],
                                      AF.Gelu, bias=bmsb[:, 8 + m:9 + m]); sc.drain()
            with nc.Block() as blk:
                @blk.vector
                def _(v):
                    v.tensor_copy(ehb[:, 8:12], mcol[:, 8:12]); v.drain()
            with nc.Block() as blk:
                @blk.tensor
                def _(t):
                    for oc in range(NCH):
                        for m in range(4):
                            mm = t.matmul(self.pf[:, 8 + oc:9 + oc],
                                          mw2sb[:, m * D + oc * 128: m * D + (oc + 1) * 128],
                                          ehb[:, 8 + m:9 + m],
                                          start=(m == 0), stop=(m == 3))
                    mm.then_inc(self.sP, 1)
                self.sP_n += 1

                @blk.vector
                def _(v):
                    v.wait_ge(self.sP, self.sP_n)
                    v.tensor_tensor(mcol[:, 12:20], self.pf[:, 8:16],
                                    bmsb[:, 12:20], op=ALU.add); v.drain()
                    v.tensor_scalar(mcol[:, 12:20], mcol[:, 12:20], 0.1, None,
                                    op0=ALU.mult); v.drain()
                    v.tensor_tensor(mcol[:, 12:20], mcol[:, 12:20], bmsb[:, 0:8],
                                    op=ALU.add); v.drain()
            with nc.Block() as blk:
                @blk.scalar
                def _(sc):
                    sc.activation(mcol[:, 20:28], mcol[:, 12:20], AF.Abs); sc.drain()
                    sc.activation(mcol[:, 20:28], mcol[:, 20:28], AF.Exp,
                                  scale=self.CCOL(4)); sc.drain()
                    sc.activation(mcol[:, 20:28], mcol[:, 20:28], AF.Ln,
                                  bias=self.CCOL(3)); sc.drain()
            with nc.Block() as blk:
                @blk.vector
                def _(v):
                    v.tensor_scalar(mcol[:, 12:20], mcol[:, 12:20], 0.0, None,
                                    op0=ALU.max); v.drain()
                    v.tensor_tensor(mcol[:, 20:28], mcol[:, 20:28], mcol[:, 12:20],
                                    op=ALU.add); v.drain()
                    v.tensor_scalar(mcol[:, 20:28], mcol[:, 20:28], 1e-6, None,
                                    op0=ALU.add); v.drain()
        with nc.Block() as blk:
            @blk.sync
            def _(s):
                self.dbg_write(s, "dbg_sig", mcol[:, 0:8])
                self.dbg_write(s, "dbg_mdiag", mcol[:, 20:28])

        scA.__exit__(None, None, None)
        vtm = sc_.sbuf("vtm", [128, 17 * D], BF16)

        with self.scope() as vsc:
            encvsb = vsc.sbuf("encvsb", [128, NCH * D], BF16)
            with nc.Block() as blk:
                @blk.gpsimd
                def _(g):
                    self.dst(g, "sG", encvsb[:, :],
                             AP(self.encv, 0, [[D, 128], [128 * D, NCH], [1, D]]))
                    g.wait_ge(self.sG, self.sG_n)
            with nc.Block() as blk:
                p_base = self.sP_n
                v_base = self.sV_n

                @blk.tensor
                def _(t):
                    for tc in range(17):
                        ntok = 128 if tc < 16 else 31
                        bk = (tc % 2) * 2
                        if tc >= 2:
                            t.wait_ge(self.sV, v_base + (tc - 2) + 1)
                        for dc in range(NCH):
                            for half in range(2):
                                mm = t.matmul(
                                    self.pb[bk + half][0:ntok, 0:512],
                                    self.xn(dc, tc * 128, ntok),
                                    encvsb[:, dc * D + half * 512: dc * D + (half + 1) * 512],
                                    start=(dc == 0), stop=(dc == NCH - 1))
                        mm.then_inc(self.sP, 1)
                self.sP_n += 17

                @blk.vector
                def _(v):
                    for tc in range(17):
                        ntok = 128 if tc < 16 else 31
                        bk = (tc % 2) * 2
                        v.wait_ge(self.sP, p_base + tc + 1)
                        for half in range(2):
                            v.tensor_scalar(
                                vtm[0:ntok, tc * D + half * 512: tc * D + (half + 1) * 512],
                                self.pb[bk + half][0:ntok, 0:512], 0.0,
                                None, op0=ALU.max); v.drain()
                        v.nop().then_inc(self.sV, 1)
                self.sV_n += 17

        aall = sc_.sbuf("aall", [128, 4 * T], BF16)
        qh = sc_.sbuf("qh", [128, TE], BF16)
        mq = sc_.sbuf("mq", [128, TE], BF16)
        sqm = sc_.sbuf("sqm", [128, TE], BF16)
        m2q = sc_.sbuf("m2q", [128, TE], BF16)
        ehb = sc_.sbuf("ehb", [128, 2 * 256], BF16)
        enqh = sc_.sbuf("enqh", [128, NCH * HD], BF16)
        hebh = sc_.sbuf("hebh", [128, HD], BF16)
        rDbc = sc_.sbuf("rDbc", [128, T], BF16)
        rNbc = sc_.sbuf("rNbc", [128, T], BF16)
        ctxs = sc_.sbuf("ctxs", [128, T], BF16)
        hebs = sc_.sbuf("hebs", [128, T], BF16)
        decsb = sc_.sbuf("decsb", [128, 4 * D], BF16)

        with nc.Block() as blk:
            @blk.vector
            def _(v):
                v.memset(self.pb[2][:, 0:512], 0.0); v.drain()
                v.memset(self.pb[3][:, 0:512], 0.0); v.drain()

        for grp in range(2):
            for hi in range(4):
                h = grp * 4 + hi
                with nc.Block() as blk:
                    hh = h

                    @blk.gpsimd
                    def _(g):
                        self.dst(g, "sG", enqh[:, :],
                                 AP(self.encq, hh * NCH * 128 * HD,
                                    [[HD, 128], [128 * HD, NCH], [1, HD]]))
                        self.dst(g, "sG", hebh[:, :],
                                 AP(self.hebw, hh * HD * HD, [[HD, 128], [1, HD]]))
                        g.wait_ge(self.sG, self.sG_n)
                self.emit_qh(qh, enqh)
                with nc.Block() as blk:
                    hh = h

                    @blk.vector
                    def _(v):
                        v.tensor_scalar(mq[:, :], qh[:, :], mcol[:, 20 + hh:21 + hh],
                                        None, op0=ALU.mult); v.drain()
                        v.tensor_tensor(sqm[:, :], qh[:, :], mq[:, :],
                                        op=ALU.mult); v.drain()
                        v.tensor_scalar(m2q[:, :], mq[:, :], -2.0, None,
                                        op0=ALU.mult); v.drain()

                with nc.Block() as blk:
                    pA = self.sP_n; pB = self.sW_n
                    vR = self.sV_n; sE = self.sS_n
                    vS = self.sX_n; vD = self.sY_n
                    hh = h

                    @blk.tensor
                    def _(t):
                        for qc in range(16):
                            bank = self.pb[qc % 2]
                            if qc >= 2:
                                t.wait_ge(self.sV, vR + (qc - 2) + 1)
                            kA = qc * 128
                            kB = qc * 128 + 128
                            qq = HALO + qc * 128
                            ops = [
                                (bank[:, 0:128], sqm[:, kA:kA + 128], self.ONES128()),
                                (bank[0:31, 128:256], sqm[:, kB:kB + 31], self.ONES128()),
                                (bank[0:31, 128:256], self.ONES128()[:, 0:31],
                                 sqm[:, qq:qq + 128]),
                                (bank[0:31, 128:256], qh[:, kB:kB + 31],
                                 m2q[:, qq:qq + 128]),
                                (bank[:, 0:128], self.ONES128(), sqm[:, qq:qq + 128]),
                                (bank[:, 0:128], qh[:, kA:kA + 128], m2q[:, qq:qq + 128]),
                            ]
                            for i, (o, l, r) in enumerate(ops):
                                mm = t.matmul(o, l, r, start=(i == 0), stop=(i == 5))
                            mm.then_inc(self.sP, 1)
                            t.wait_ge(self.sS, sE + qc + 1)
                            if qc >= 2:
                                t.wait_ge(self.sX, vS + (qc - 2) + 1)
                            ebuf = ehb[:, (qc % 2) * 256:(qc % 2) * 256 + 256]
                            g4 = qc // 4
                            c0 = (qc % 4) * 128
                            if qc % 4 == 0 and g4 > 0:
                                t.wait_ge(self.sY, vD + g4)
                            t.matmul(self.pf[0:1, c0:c0 + 128], self.OC(),
                                     ebuf[:, 0:128], start=True, stop=False)
                            t.matmul(self.pf[0:1, c0:c0 + 128], self.OC()[0:31, :],
                                     ebuf[0:31, 128:256], start=False, stop=True)
                            cb = self.pb[4 + qc % 2]
                            t.matmul(cb[:, 0:128],
                                     vtm[:, qc * D + hh * 128: qc * D + (hh + 1) * 128],
                                     ebuf[:, 0:128], start=True, stop=False)
                            t.matmul(cb[:, 0:128],
                                     vtm[0:31, (qc + 1) * D + hh * 128: (qc + 1) * D + (hh + 1) * 128],
                                     ebuf[0:31, 128:256], start=False, stop=True)
                            mm = t.matmul(cb[:, 256:384], hebh[:, :],
                                          qh[:, qq:qq + 128], start=True, stop=True)
                            mm.then_inc(self.sW, 1)
                    self.sP_n += 16
                    self.sW_n += 16

                    @blk.vector
                    def _(v):
                        for qc in range(16):
                            bank = self.pb[qc % 2]
                            ebank = self.pb[2 + qc % 2]
                            v.wait_ge(self.sP, pA + qc + 1)
                            if qc >= 2:
                                v.wait_ge(self.sS, sE + (qc - 2) + 1)
                            v.scalar_tensor_tensor(ebank[:, 0:128], bank[:, 0:128],
                                                   0.0, self.MBA(), op0=ALU.max,
                                                   op1=ALU.add); v.drain()
                            v.scalar_tensor_tensor(ebank[0:31, 128:256],
                                                   bank[0:31, 128:256], 0.0,
                                                   self.MBB(), op0=ALU.max,
                                                   op1=ALU.add).then_inc(self.sV, 1)
                            v.drain()
                            v.wait_ge(self.sW, pB + qc + 1)
                            cb = self.pb[4 + qc % 2]
                            v.tensor_copy(ctxs[:, qc * 128:(qc + 1) * 128],
                                          cb[:, 0:128]); v.drain()
                            v.tensor_copy(hebs[:, qc * 128:(qc + 1) * 128],
                                          cb[:, 256:384]).then_inc(self.sX, 1)
                            v.drain()
                            if qc % 4 == 3:
                                v.tensor_copy(self.scr[0:1, (qc // 4) * 512:(qc // 4) * 512 + 512],
                                              self.pf[0:1, 0:512]
                                              ).then_inc(self.sY, 1); v.drain()
                    self.sV_n += 16
                    self.sX_n += 16
                    self.sY_n += 4

                    @blk.scalar
                    def _(sc):
                        for qc in range(16):
                            ebank = self.pb[2 + qc % 2]
                            sc.wait_ge(self.sV, vR + qc + 1)
                            if qc >= 2:
                                sc.wait_ge(self.sW, pB + (qc - 2) + 1)
                            sc.activation(ebank[:, 0:256], ebank[:, 0:256], AF.Sqrt,
                                          bias=self.CCOL(1)); sc.drain()
                            sc.activation(ehb[:, (qc % 2) * 256:(qc % 2) * 256 + 256],
                                          ebank[:, 0:256],
                                          AF.Exp, scale=self.CCOL(4)
                                          ).then_inc(self.sS, 1); sc.drain()
                    self.sS_n += 16

                with nc.Block() as blk:
                    @blk.vector
                    def _(v):
                        v.tensor_scalar(self.scr[0:1, 0:T], self.scr[0:1, 0:T],
                                        1e-8, None, op0=ALU.add); v.drain()
                        v.reciprocal(self.scr[0:1, 0:T], self.scr[0:1, 0:T]); v.drain()
                self.bcast_row(lambda i: self.scr[0:1, i * 512:(i + 1) * 512], 4,
                               lambda i: rDbc[:, i * 512:(i + 1) * 512])
                with nc.Block() as blk:
                    @blk.vector
                    def _(v):
                        v.tensor_tensor(mq[:, HALO:HALO + T], qh[:, HALO:HALO + T],
                                        qh[:, HALO:HALO + T], op=ALU.mult); v.drain()
                with nc.Block() as blk:
                    pN = self.sP_n
                    vN = self.sV_n

                    @blk.tensor
                    def _(t):
                        for g4 in range(4):
                            if g4 > 0:
                                t.wait_ge(self.sV, vN + g4)
                            mm = t.matmul(self.pb[5][0:1, 0:512], self.OC(),
                                          mq[:, HALO + g4 * 512: HALO + (g4 + 1) * 512],
                                          start=True, stop=True)
                            mm.then_inc(self.sP, 1)
                    self.sP_n += 4

                    @blk.vector
                    def _(v):
                        for g4 in range(4):
                            v.wait_ge(self.sP, pN + g4 + 1)
                            v.tensor_copy(self.scr[0:1, g4 * 512:(g4 + 1) * 512],
                                          self.pb[5][0:1, 0:512]
                                          ).then_inc(self.sV, 1); v.drain()
                    self.sV_n += 4
                with nc.Block() as blk:
                    @blk.scalar
                    def _(sc):
                        sc.activation(self.scr[0:1, 0:T], self.scr[0:1, 0:T],
                                      AF.Sqrt); sc.drain()
                with nc.Block() as blk:
                    @blk.vector
                    def _(v):
                        v.tensor_scalar(self.scr[0:1, 0:T], self.scr[0:1, 0:T],
                                        1e-12, None, op0=ALU.max); v.drain()
                        v.reciprocal(self.scr[0:1, 0:T], self.scr[0:1, 0:T]); v.drain()
                self.bcast_row(lambda i: self.scr[0:1, i * 512:(i + 1) * 512], 4,
                               lambda i: rNbc[:, i * 512:(i + 1) * 512])
                with nc.Block() as blk:
                    hh = hi

                    @blk.vector
                    def _(v):
                        v.tensor_tensor(ctxs[:, :], ctxs[:, :], rDbc[:, :],
                                        op=ALU.mult); v.drain()
                        v.tensor_tensor(hebs[:, :], hebs[:, :], rNbc[:, :],
                                        op=ALU.mult); v.drain()
                        v.tensor_tensor(ctxs[:, :], ctxs[:, :], hebs[:, :],
                                        op=ALU.add); v.drain()
                        v.tensor_tensor(aall[:, hh * T:(hh + 1) * T], ctxs[:, :],
                                        qh[:, HALO:HALO + T], op=ALU.mult); v.drain()

            with nc.Block() as blk:
                gg = grp

                @blk.gpsimd
                def _(g):
                    self.dst(g, "sG", decsb[:, :],
                             AP(self.dec, gg * 4 * 128 * D,
                                [[D, 128], [128 * D, 4], [1, D]]))
                    g.wait_ge(self.sG, self.sG_n)
            tiles = [(oc, tb) for oc in range(NCH) for tb in range(T // 512)]
            with nc.Block() as blk:
                p_base = self.sP_n
                v_base = self.sV_n

                @blk.tensor
                def _(t):
                    for i, (oc, tb) in enumerate(tiles):
                        bank = self.pb[i % 4]
                        if i >= 4:
                            t.wait_ge(self.sV, v_base + (i - 4) + 1)
                        for hc in range(4):
                            mm = t.matmul(bank[:, 0:512],
                                          decsb[:, hc * D + oc * 128: hc * D + (oc + 1) * 128],
                                          aall[:, hc * T + tb * 512: hc * T + (tb + 1) * 512],
                                          start=(hc == 0), stop=(hc == 3))
                        mm.then_inc(self.sP, 1)
                self.sP_n += len(tiles)

                @blk.vector
                def _(v):
                    for i, (oc, tb) in enumerate(tiles):
                        bank = self.pb[i % 4]
                        v.wait_ge(self.sP, p_base + i + 1)
                        v.tensor_add(self.xc(oc, HALO + tb * 512, 512),
                                     self.xc(oc, HALO + tb * 512, 512),
                                     bank[:, 0:512]).then_inc(self.sV, 1); v.drain()
                self.sV_n += len(tiles)
            with nc.Block() as blk:
                @blk.vector
                def _(v):
                    v.memset(self.pb[2][:, 0:512], 0.0); v.drain()
                    v.memset(self.pb[3][:, 0:512], 0.0); v.drain()
        with nc.Block() as blk:
            @blk.sync
            def _(s):
                self.dbg_write(s, "dbg_x1",
                               AP(self.xsb, 0, [[NCH * TE, 128], [1, NCH * TE]]))
        sc_.__exit__(None, None, None)

    def p4_ln1(self):
        nc = self.nc
        self.p1_ln("1", ntok=T, lo=HALO)
        with nc.Block() as blk:
            @blk.sync
            def _(s):
                self.dst(s, "sD",
                         AP(self.x1_park, 0, [[T, 128], [128 * T, NCH], [1, T]]),
                         AP(self.xsb, HALO, [[NCH * TE, 128], [TE, NCH], [1, T]]))
                s.wait_ge(self.sD, self.sD_n)

    def p6_gate(self):
        nc = self.nc
        sc_ = self.scope(); sc_.__enter__()
        glog = sc_.sbuf("glog", [8, T], F32)
        pm = sc_.sbuf("pm", [8, T], F32)
        gwsb = sc_.sbuf("gwsb", [128, NCH * NE], BF16)
        gcol = sc_.sbuf("gcol", [8, 8], F32)
        g2 = sc_.sbuf("g2", [2, NE], F32)
        wtb = sc_.sbuf("wtb", [8, T], BF16)
        with nc.Block() as blk:
            @blk.gpsimd
            def _(g):
                self.dst(g, "sG", gwsb[:, :],
                         AP(self.gwc, 0, [[NE, 128], [128 * NE, NCH], [1, NE]]))
                g.wait_ge(self.sG, self.sG_n)

            @blk.sync
            def _(s):
                self.dst(s, "sD", gcol[:, 0:1], self.gbc[:, :])
                s.wait_ge(self.sD, self.sD_n)
        with nc.Block() as blk:
            @blk.tensor
            def _(t):
                for tb in range(4):
                    for dc in range(NCH):
                        mm = t.matmul(self.pb[tb][0:8, 0:512],
                                      gwsb[:, dc * NE:(dc + 1) * NE],
                                      self.xn(dc, HALO + tb * 512, 512),
                                      start=(dc == 0), stop=(dc == NCH - 1))
                mm.then_inc(self.sP, 1)
            self.sP_n += 1

            @blk.vector
            def _(v):
                v.wait_ge(self.sP, self.sP_n)
                for tb in range(4):
                    v.tensor_scalar(glog[:, tb * 512:(tb + 1) * 512],
                                    self.pb[tb][0:8, 0:512], gcol[:, 0:1], None,
                                    op0=ALU.add); v.drain()

        def stat_allreduce(src_i, cc_i, cc_o, is_max, out_i):
            with nc.Block() as blk:
                @blk.tensor
                def _(t):
                    t.transpose(self.pf[0:1, 0:8], gcol[:, src_i:src_i + 1],
                                self.IDF8()).then_inc(self.sP, 1)
                self.sP_n += 1

                @blk.vector
                def _(v):
                    v.wait_ge(self.sP, self.sP_n)
                    v.tensor_copy(self.scr[0:1, 0:8], self.pf[0:1, 0:8]); v.drain()
            with nc.Block() as blk:
                @blk.sync
                def _(s):
                    self.dst(s, "sD", g2[0:1, :], self.scr[0:1, 0:8])
                    self.dst(s, "sD", g2[1:2, :], self.scr[0:1, 0:8])
                    s.wait_ge(self.sD, self.sD_n)
            with nc.Block() as blk:
                @blk.vector
                def _(v):
                    v.tensor_scalar(g2[:, :], g2[:, :], self.B2(), None,
                                    op0=ALU.mult); v.drain()
                    if is_max:
                        v.tensor_scalar(self.scr[0:2, 8:9], self.B2(), -1.0,
                                        3.0e38, op0=ALU.add, op1=ALU.mult); v.drain()
                        v.tensor_scalar(g2[:, :], g2[:, :], self.scr[0:2, 8:9],
                                        None, op0=ALU.add); v.drain()
            with nc.Block() as blk:
                @blk.sync
                def _(s):
                    self.dst(s, "sD", cc_i[:, :], g2[:, :])
                    s.wait_ge(self.sD, self.sD_n)
            with nc.Block() as blk:
                @blk.gpsimd
                def _(g):
                    self.sC_n += 1
                    g.collective_compute(
                        "AllReduce", ALU.max if is_max else ALU.add,
                        replica_groups=[list(range(8))],
                        ins=[cc_i.ap().opt()], outs=[cc_o.ap().opt()],
                    ).then_inc(self.sC)
                    g.wait_ge(self.sC, self.sC_n)
            with nc.Block() as blk:
                @blk.sync
                def _(s):
                    self.dst(s, "sD", g2[:, :], cc_o[:, :])
                    s.wait_ge(self.sD, self.sD_n)
            with nc.Block() as blk:
                @blk.tensor
                def _(t):
                    t.transpose(self.pf[0:8, 0:2], g2[:, :],
                                self.IDF2()).then_inc(self.sP, 1)
                self.sP_n += 1

                @blk.vector
                def _(v):
                    v.wait_ge(self.sP, self.sP_n)
                    v.tensor_tensor(self.pf[0:8, 0:2], self.pf[0:8, 0:2],
                                    self.BSEL2(8), op=ALU.mult); v.drain()
                    v.tensor_reduce(gcol[:, out_i:out_i + 1], self.pf[0:8, 0:2],
                                    AX, ALU.add); v.drain()

        with nc.Block() as blk:
            @blk.vector
            def _(v):
                v.tensor_reduce(gcol[:, 1:2], glog[:, :], AX, ALU.max); v.drain()
        stat_allreduce(1, self.cc_gm_i, self.cc_gm_o, True, 2)
        with nc.Block() as blk:
            @blk.vector
            def _(v):
                v.tensor_scalar(gcol[:, 3:4], gcol[:, 2:3], -1.0, None,
                                op0=ALU.mult); v.drain()
        with nc.Block() as blk:
            @blk.scalar
            def _(sc):
                sc.activation(glog[:, :], glog[:, :], AF.Exp,
                              bias=gcol[0:8, 3:4]); sc.drain()
        with nc.Block() as blk:
            @blk.vector
            def _(v):
                v.tensor_reduce(gcol[:, 4:5], glog[:, :], AX, ALU.add); v.drain()
        stat_allreduce(4, self.cc_gs_i, self.cc_gs_o, False, 5)
        with nc.Block() as blk:
            @blk.vector
            def _(v):
                v.reciprocal(gcol[:, 6:7], gcol[:, 5:6]); v.drain()
                v.tensor_scalar(glog[:, :], glog[:, :], gcol[:, 6:7], None,
                                op0=ALU.mult); v.drain()
        ptm = sc_.sbuf("ptm", [128, 16 * NE], F32)
        m12 = sc_.sbuf("m12", [128, 2 * 16], F32)
        with nc.Block() as blk:
            p0 = self.sP_n
            v0 = self.sV_n

            @blk.tensor
            def _(t):
                for g4 in range(4):
                    if g4 >= 1:
                        t.wait_ge(self.sV, v0 + g4)
                    for j in range(4):
                        tc = g4 * 4 + j
                        tr = t.transpose(self.pf[0:128, j * 8: j * 8 + 8],
                                         glog[:, tc * 128:(tc + 1) * 128],
                                         self.IDF8())
                    tr.then_inc(self.sP, 1)
            self.sP_n += 4

            @blk.vector
            def _(v):
                for g4 in range(4):
                    v.wait_ge(self.sP, p0 + g4 + 1)
                    for j in range(4):
                        tc = g4 * 4 + j
                        v.tensor_copy(ptm[:, tc * 8:(tc + 1) * 8],
                                      self.pf[0:128, j * 8: j * 8 + 8]); v.drain()
                    v.nop().then_inc(self.sV, 1)
            self.sV_n += 4
        with nc.Block() as blk:
            @blk.vector
            def _(v):
                for tc in range(16):
                    sl_ = ptm[:, tc * 8:(tc + 1) * 8]
                    v.tensor_reduce(m12[:, tc:tc + 1], sl_, AX, ALU.max); v.drain()
                    v.tensor_scalar(self.scr[:, 0:8], sl_, m12[:, tc:tc + 1],
                                    None, op0=ALU.is_ge); v.drain()
                    v.scalar_tensor_tensor(self.scr[:, 0:8], self.scr[:, 0:8],
                                           -1.0e9, sl_, op0=ALU.mult,
                                           op1=ALU.add); v.drain()
                    v.tensor_reduce(m12[:, 16 + tc: 17 + tc], self.scr[:, 0:8],
                                    AX, ALU.max); v.drain()
                    v.tensor_scalar(self.scr[:, 0:8], sl_,
                                    m12[:, 16 + tc: 17 + tc], None,
                                    op0=ALU.is_ge); v.drain()
                    v.tensor_tensor(self.scr[:, 8:16], sl_, self.scr[:, 0:8],
                                    op=ALU.mult); v.drain()
                    v.tensor_copy(self.wcolsb[:, tc * 8:(tc + 1) * 8],
                                  self.scr[:, 8:16]); v.drain()
                    v.tensor_copy(ptm[:, tc * 8:(tc + 1) * 8],
                                  self.scr[:, 0:8]); v.drain()
        with nc.Block() as blk:
            p0 = self.sP_n
            v0 = self.sV_n

            @blk.tensor
            def _(t):
                for g4 in range(4):
                    if g4 >= 1:
                        t.wait_ge(self.sV, v0 + g4)
                    for j in range(4):
                        tc = g4 * 4 + j
                        tr = t.transpose(self.pf[0:8, j * 128: j * 128 + 128],
                                         ptm[:, tc * 8:(tc + 1) * 8], self.IDF128())
                    tr.then_inc(self.sP, 1)
            self.sP_n += 4

            @blk.vector
            def _(v):
                for g4 in range(4):
                    v.wait_ge(self.sP, p0 + g4 + 1)
                    for j in range(4):
                        tc = g4 * 4 + j
                        v.tensor_copy(self.indb[:, tc * 128:(tc + 1) * 128],
                                      self.pf[0:8, j * 128: j * 128 + 128]); v.drain()
                    v.nop().then_inc(self.sV, 1)
            self.sV_n += 4
        with nc.Block() as blk:
            @blk.sync
            def _(s):
                self.dbg_write(s, "dbg_wtok", self.wcolsb[:, :])
                self.dbg_write(s, "dbg_ind", self.indb[:, :])
        sc_.__exit__(None, None, None)

    def p5_qkv(self):
        nc = self.nc
        scL = self.scope(); scL.__enter__()
        self.scL = scL
        q1 = scL.sbuf("q1", [128, NCH * T], BF16)
        vtm1 = scL.sbuf("vtm1", [128, 16 * D], BF16)
        a1b = scL.sbuf("a1b", [128, D], BF16)
        ssb = scL.sbuf("ssb", [128, 10 * 128], BF16)
        ktc = scL.sbuf("ktc", [128, 3 * 512], BF16)
        slb = ktc
        bqk = scL.sbuf("bqk", [128, 16], F32)
        st32 = scL.sbuf("st32", [128, NCH * D], F32)
        st16 = scL.sbuf("st16", [128, NCH * D], BF16)
        kfm = scL.sbuf("kfm", [128, NCH * T], BF16)
        wdb = scL.sbuf("wdb", [128, NCH * 512], BF16)
        self._l1 = dict(q1=q1, vtm1=vtm1, st32=st32, st16=st16, a1b=a1b,
                        slb=slb, ssb=ssb, ktc=ktc, kfm=kfm)

        with nc.Block() as blk:
            @blk.sync
            def _(s):
                self.dst(s, "sD", bqk[:, 0:8], self.bq_c[:, :])
                self.dst(s, "sD", bqk[:, 8:16], self.bk_c[:, :])
                s.wait_ge(self.sD, self.sD_n)

            @blk.sync
            def _(s2):
                self.dst(s2, "sD", self.scr[0:1, 0:D],
                         AP(self.bkv_r, D, [[1, 1], [1, D]]))
                s2.wait_ge(self.sD, self.sD_n)
        self.bcast_row(lambda i: self.scr[0:1, i * 512:(i + 1) * 512], 2,
                       lambda i: self.scr[:, i * 512:(i + 1) * 512])

        with nc.Block() as blk:
            p_base = self.sP_n
            v_base = self.sV_n
            gmarks = []

            @blk.gpsimd
            def _(g):
                for f in range(16):
                    if f >= 2:
                        g.wait_ge(self.sP, p_base + 4 * (f - 1))
                    gmarks.append(self.dst(
                        g, ["sA", "sB"][f % 2],
                        wdb[:, (f % 2) * 1024:(f % 2) * 1024 + 1024],
                        AP(self.wqkvT, f * 128,
                           [[3 * D, 128], [128 * 3 * D, NCH], [1, 128]])))

            @blk.tensor
            def _(t):
                for f in range(16):
                    t.wait_ge([self.sA, self.sB][f % 2], gmarks[f])
                    if f >= 1:
                        t.wait_ge(self.sV, v_base + 4 * f)
                    for tb in range(4):
                        for dc in range(NCH):
                            mm = t.matmul(
                                self.pb[tb][:, 0:512],
                                wdb[:, (f % 2) * 1024 + dc * 128:(f % 2) * 1024 + (dc + 1) * 128],
                                self.xn(dc, HALO + tb * 512, 512),
                                start=(dc == 0), stop=(dc == NCH - 1))
                        mm.then_inc(self.sP, 1)
            self.sP_n += 64

            @blk.vector
            def _(v):
                for f in range(16):
                    dst = q1 if f < 8 else kfm
                    fc = f % 8
                    for tb in range(4):
                        v.wait_ge(self.sP, p_base + 4 * f + tb + 1)
                        v.tensor_scalar(dst[:, fc * T + tb * 512: fc * T + (tb + 1) * 512],
                                        self.pb[tb][:, 0:512], bqk[:, f:f + 1],
                                        None, op0=ALU.add).then_inc(self.sV, 1)
                        v.drain()
            self.sV_n += 64

        for half in range(2):
            with nc.Block() as blk:
                hh = half

                @blk.gpsimd
                def _(g):
                    self.dst(g, "sG", wdb[:, :],
                             AP(self.wqkvT, 2 * D + hh * 512,
                                [[3 * D, 128], [128 * 3 * D, NCH], [1, 512]]))
                    g.wait_ge(self.sG, self.sG_n)
            with nc.Block() as blk:
                p_base = self.sP_n
                v_base = self.sV_n
                hh = half

                @blk.tensor
                def _(t):
                    for tc in range(16):
                        bank = self.pb[tc % 4]
                        if tc >= 4:
                            t.wait_ge(self.sV, v_base + (tc - 4) + 1)
                        for dc in range(NCH):
                            mm = t.matmul(bank[:, 0:512],
                                          self.xn(dc, HALO + tc * 128, 128),
                                          wdb[:, dc * 512:(dc + 1) * 512],
                                          start=(dc == 0), stop=(dc == NCH - 1))
                        mm.then_inc(self.sP, 1)
                self.sP_n += 16

                @blk.vector
                def _(v):
                    for tc in range(16):
                        bank = self.pb[tc % 4]
                        v.wait_ge(self.sP, p_base + tc + 1)
                        v.tensor_tensor(vtm1[:, tc * D + hh * 512: tc * D + hh * 512 + 512],
                                        bank[:, 0:512],
                                        self.scr[:, hh * 512: hh * 512 + 512],
                                        op=ALU.add).then_inc(self.sV, 1); v.drain()
                self.sV_n += 16

    def p7_attn1(self):
        nc = self.nc
        L = self._l1
        q1, vtm1, st32, st16 = L["q1"], L["vtm1"], L["st32"], L["st16"]
        a1b, slb, ssb, ktc, kfm = L["a1b"], L["slb"], L["ssb"], L["ktc"], L["kfm"]

        with nc.Block() as blk:
            @blk.vector
            def _(v):
                for dc in range(NCH):
                    v.memset(st32[:, dc * D:(dc + 1) * D], 0.0); v.drain()
                    v.memset(st16[:, dc * D:(dc + 1) * D], 0.0); v.drain()

        for c in range(NCHUNK):
            pairs = [(p_, kb) for p_ in range(4) for kb in range(p_ + 1)]
            with nc.Block() as blk:
                p0 = self.sP_n
                v0 = self.sV_n
                cc = c

                @blk.tensor
                def _(t):
                    for i, (p_, kb) in enumerate(pairs):
                        bank = self.pb[i % 2]
                        if i >= 2:
                            t.wait_ge(self.sV, v0 + (i - 2) + 1)
                        kt = cc * 4 + kb
                        qt = cc * 4 + p_
                        for dc in range(NCH):
                            mm = t.matmul(
                                bank[:, 0:128],
                                kfm[:, dc * T + kt * 128:dc * T + (kt + 1) * 128],
                                q1[:, dc * T + qt * 128:dc * T + (qt + 1) * 128],
                                start=(dc == 0), stop=(dc == NCH - 1))
                        mm.then_inc(self.sP, 1)
                self.sP_n += len(pairs)

                @blk.vector
                def _(v):
                    for i, (p_, kb) in enumerate(pairs):
                        bank = self.pb[i % 2]
                        kt = cc * 4 + kb
                        v.wait_ge(self.sP, p0 + i + 1)
                        if p_ == kb:
                            v.scalar_tensor_tensor(ssb[:, i * 128:(i + 1) * 128],
                                                   bank[:, 0:128],
                                                   self.DKM()[:, kt:kt + 1],
                                                   self.TRI(), op0=ALU.mult,
                                                   op1=ALU.mult
                                                   ).then_inc(self.sV, 1); v.drain()
                        else:
                            v.tensor_scalar(ssb[:, i * 128:(i + 1) * 128],
                                            bank[:, 0:128],
                                            self.DKM()[:, kt:kt + 1], None,
                                            op0=ALU.mult).then_inc(self.sV, 1)
                            v.drain()
                self.sV_n += len(pairs)
            with nc.Block() as blk:
                p0 = self.sP_n
                v0 = self.sV_n
                d0 = self.sD_n
                cc = c

                @blk.tensor
                def _(t):
                    for p_ in range(4):
                        qt = cc * 4 + p_
                        bk = (p_ % 2) * 2
                        if p_ >= 2:
                            t.wait_ge(self.sV, v0 + (p_ - 2) + 1)
                        for kb in range(p_ + 1):
                            i = p_ * (p_ + 1) // 2 + kb
                            kt = cc * 4 + kb
                            for hf in range(2):
                                t.matmul(self.pb[bk + hf][:, 0:512],
                                         ssb[:, i * 128:(i + 1) * 128],
                                         vtm1[:, kt * D + hf * 512: kt * D + (hf + 1) * 512],
                                         start=(kb == 0), stop=False)
                        for dc in range(NCH):
                            for hf in range(2):
                                mm = t.matmul(
                                    self.pb[bk + hf][:, 0:512],
                                    q1[:, dc * T + qt * 128:dc * T + (qt + 1) * 128],
                                    st16[:, dc * D + hf * 512:dc * D + (hf + 1) * 512],
                                    start=False, stop=(dc == NCH - 1))
                        mm.then_inc(self.sP, 1)
                self.sP_n += 4

                dmarks = []

                @blk.sync
                def _(s):
                    for p_ in range(4):
                        qt = cc * 4 + p_
                        s.wait_ge(self.sV, v0 + p_ + 1)
                        dmarks.append(self.dst(
                            s, "sD",
                            AP(self.a1_park, qt * 128 * D, [[D, 128], [1, D]]),
                            a1b[:, :]))
                    s.wait_ge(self.sD, self.sD_n)

                @blk.vector
                def _(v):
                    for p_ in range(4):
                        qt = cc * 4 + p_
                        bk = (p_ % 2) * 2
                        v.wait_ge(self.sP, p0 + p_ + 1)
                        if p_ >= 1:
                            v.wait_ge(self.sD, dmarks[p_ - 1])
                        for hf in range(2):
                            v.tensor_scalar(a1b[:, hf * 512:(hf + 1) * 512],
                                            self.pb[bk + hf][:, 0:512],
                                            self.DIC()[:, qt:qt + 1], None,
                                            op0=ALU.mult); v.drain()
                        v.nop().then_inc(self.sV, 1)
                self.sV_n += 4
            with nc.Block() as blk:
                pT = self.sP_n
                pU = self.sW_n
                vK = self.sV_n
                vS = self.sX_n
                cc = c

                @blk.tensor
                def _(t):
                    for dc in range(NCH):
                        for tsc in range(4):
                            kt = cc * 4 + tsc
                            tr = t.transpose(
                                self.pt[:, tsc * 128:(tsc + 1) * 128],
                                kfm[:, dc * T + kt * 128:dc * T + (kt + 1) * 128],
                                self.ID())
                        tr.then_inc(self.sP, 1)
                        t.wait_ge(self.sV, vK + dc + 1)
                        bk = (dc % 2) * 2
                        if dc >= 2:
                            t.wait_ge(self.sX, vS + (dc - 2) + 1)
                        for tsc in range(4):
                            for hf in range(2):
                                mm = t.matmul(
                                    self.pb[bk + hf][:, 0:512],
                                    ktc[:, tsc * 128:(tsc + 1) * 128],
                                    vtm1[:, (cc * 4 + tsc) * D + hf * 512: (cc * 4 + tsc) * D + (hf + 1) * 512],
                                    start=(tsc == 0), stop=(tsc == 3))
                        mm.then_inc(self.sW, 1)
                self.sP_n += NCH
                self.sW_n += NCH

                @blk.vector
                def _(v):
                    for dc in range(NCH):
                        v.wait_ge(self.sP, pT + dc + 1)
                        for tsc in range(4):
                            kt = cc * 4 + tsc
                            v.tensor_scalar(ktc[:, tsc * 128:(tsc + 1) * 128],
                                            self.pt[:, tsc * 128:(tsc + 1) * 128],
                                            self.DKM()[:, kt:kt + 1], None,
                                            op0=ALU.mult); v.drain()
                        v.nop().then_inc(self.sV, 1)
                        v.wait_ge(self.sW, pU + dc + 1)
                        bk = (dc % 2) * 2
                        for hf in range(2):
                            hs = slice(dc * D + hf * 512, dc * D + (hf + 1) * 512)
                            v.tensor_tensor(st32[:, hs], st32[:, hs],
                                            self.pb[bk + hf][:, 0:512],
                                            op=ALU.add); v.drain()
                            v.tensor_copy(st16[:, hs], st32[:, hs]); v.drain()
                        v.nop().then_inc(self.sX, 1)
                self.sV_n += NCH
                self.sX_n += NCH

        with nc.Block() as blk:
            @blk.sync
            def _(s):
                self.dst(s, "sD", self.cc_ut_i.ap(), st16[:, :])
                s.wait_ge(self.sD, self.sD_n)
        with nc.Block() as blk:
            @blk.gpsimd
            def _(g):
                self.sC_n += 1
                g.collective_compute(
                    "AllGather", ALU.bypass,
                    replica_groups=[[0, 1, 2, 3], [4, 5, 6, 7]],
                    ins=[self.cc_ut_i.ap().opt()], outs=[self.cc_ut_o.ap().opt()],
                ).then_inc(self.sC)
                g.wait_ge(self.sC, self.sC_n)
        for dc in range(NCH):
            for hf in range(2):
                with nc.Block() as blk:
                    dd, hh = dc, hf

                    @blk.sync
                    def _(s):
                        for qs in range(3):
                            self.dst(s, "sD", slb[:, qs * 512:(qs + 1) * 512],
                                     AP(self.cc_ut_o,
                                        qs * 128 * NCH * D + dd * D + hh * 512,
                                        [[NCH * D, 128], [1, 512]]))
                        s.wait_ge(self.sD, self.sD_n)
                with nc.Block() as blk:
                    dd, hh = dc, hf

                    @blk.vector
                    def _(v):
                        hs = slice(dd * D + hh * 512, dd * D + (hh + 1) * 512)
                        v.tensor_scalar(self.scr[:, 0:512], slb[:, 0:512],
                                        self.DRM()[:, 0:1], None,
                                        op0=ALU.mult); v.drain()
                        v.scalar_tensor_tensor(self.scr[:, 0:512],
                                               slb[:, 512:1024],
                                               self.DRM()[:, 1:2],
                                               self.scr[:, 0:512],
                                               op0=ALU.mult, op1=ALU.add); v.drain()
                        v.scalar_tensor_tensor(self.scr[:, 0:512],
                                               slb[:, 1024:1536],
                                               self.DRM()[:, 2:3],
                                               self.scr[:, 0:512],
                                               op0=ALU.mult, op1=ALU.add); v.drain()
                        v.tensor_copy(st16[:, hs], self.scr[:, 0:512]); v.drain()
        with nc.Block() as blk:
            p0 = self.sP_n
            v0 = self.sV_n
            g0 = self.sG_n

            @blk.tensor
            def _(t):
                for tc in range(16):
                    bk = (tc % 2) * 2
                    if tc >= 2:
                        t.wait_ge(self.sV, v0 + (tc - 2) + 1)
                    for dc in range(NCH):
                        for hf in range(2):
                            mm = t.matmul(
                                self.pb[bk + hf][:, 0:512],
                                q1[:, dc * T + tc * 128:dc * T + (tc + 1) * 128],
                                st16[:, dc * D + hf * 512:dc * D + (hf + 1) * 512],
                                start=(dc == 0), stop=(dc == NCH - 1))
                    mm.then_inc(self.sP, 1)
            self.sP_n += 16

            rmarks = []

            @blk.gpsimd
            def _(g):
                for tc in range(16):
                    g.wait_ge(self.sV, v0 + tc + 1)
                    rmarks.append(self.dst(
                        g, "sG",
                        AP(self.a1_park, tc * 128 * D, [[D, 128], [1, D]]),
                        a1b[:, :], accum_op=ALU.add))
                g.wait_ge(self.sG, self.sG_n)

            @blk.vector
            def _(v):
                for tc in range(16):
                    bk = (tc % 2) * 2
                    v.wait_ge(self.sP, p0 + tc + 1)
                    if tc >= 1:
                        v.wait_ge(self.sG, rmarks[tc - 1])
                    for hf in range(2):
                        v.tensor_scalar(a1b[:, hf * 512:(hf + 1) * 512],
                                        self.pb[bk + hf][:, 0:512],
                                        self.DIC()[:, tc:tc + 1], None,
                                        op0=ALU.mult); v.drain()
                    v.nop().then_inc(self.sV, 1)
            self.sV_n += 16
        with nc.Block() as blk:
            @blk.sync
            def _(s):
                self.dbg_write(s, "dbg_a1", self.a1_park.ap())
        self.scL.__exit__(None, None, None)

    def p8_moe(self):
        nc = self.nc
        scM = self.scope(); scM.__enter__()
        h1 = scM.sbuf("h1", [128, NCH * T], BF16)
        h2 = scM.sbuf("h2", [128, NCH * T], BF16)
        ytm = scM.sbuf("ytm", [128, 16 * D], BF16)
        vps = scM.sbuf("vps", [1, 2 * D], F32)
        bcl = scM.sbuf("bcl", [128, 4 * NCH], F32)
        b3s = scM.sbuf("b3s", [1, 2 * D], BF16)
        wsb = scM.sbuf("wsb", [128, 2 * NCH * D], BF16)

        steps = [(e, l) for e in range(NE) for l in range(3)]
        NS = len(steps)
        sp_base = [0] * (NS + 1)
        sp_end = [0] * NS
        vp_pmark = [0] * NE
        ss_base = [0] * NS
        ss_end = [0] * NS
        for si, (e, l) in enumerate(steps):
            sp_base[si] = self.sP_n
            self.sP_n += 32 if l < 2 else 16
            sp_end[si] = self.sP_n
            if l == 2:
                self.sP_n += 1
                vp_pmark[e] = self.sP_n
            ss_base[si] = self.sS_n
            self.sS_n += 32 if l < 2 else 16
            ss_end[si] = self.sS_n
        sp_base[NS] = self.sP_n
        vcopy_mark = [self.sV_n + e + 1 for e in range(NE)]
        self.sV_n += NE
        bmark = [0] * NE
        gmark = [0] * NE
        vpsdma_mark = [0] * NE
        wmarks = {}

        def b3slot(e):
            return b3s[:, (e % 2) * D:(e % 2) * D + D]

        def load_w(g, which, e, slot, si):
            wmarks[si] = (slot, self.dst(
                g, ["sA", "sB"][slot],
                wsb[:, slot * NCH * D:(slot + 1) * NCH * D],
                AP(which, e * NCH * 128 * D,
                   [[D, 128], [128 * D, NCH], [1, D]])))

        with nc.Block() as blk:
            @blk.gpsimd
            def _(g):
                load_w(g, self.w1, 0, 0, 0)
                for si, (e, l) in enumerate(steps):
                    if l == 2:
                        gmark[e] = self.dst(
                            g, "sG", b3slot(e),
                            AP(self.b3r, e * D, [[1, 1], [1, D]]))
                    if si + 1 < NS:
                        ne_, nl_ = steps[si + 1]
                        if si >= 1:
                            g.wait_ge(self.sP, sp_end[si - 1])
                        wt = [self.w1, self.w2, self.w3][nl_]
                        load_w(g, wt, ne_, (si + 1) % 2, si + 1)

            @blk.sync
            def _(s):
                for e in range(NE):
                    if e >= 2:
                        s.wait_ge(self.sS, ss_end[3 * (e - 2) + 1])
                    bo = (e % 2) * 16
                    self.dst(s, "sD", bcl[:, bo:bo + 8],
                             AP(self.b1c, e * 128 * NCH, [[NCH, 128], [1, NCH]]))
                    bmark[e] = self.dst(s, "sD", bcl[:, bo + 8:bo + 16],
                                        AP(self.b2c, e * 128 * NCH,
                                           [[NCH, 128], [1, NCH]]))
                    s.wait_ge(self.sV, vcopy_mark[e])
                    vpsdma_mark[e] = self.dst(
                        s, "sW", AP(self.cc_vp_i, e * D, [[1, 1], [1, D]]),
                        vps[0:1, (e % 2) * D:(e % 2) * D + D])
                    s.wait_ge(self.sW, vpsdma_mark[e])

            @blk.tensor
            def _(t):
                for si, (e, l) in enumerate(steps):
                    wslot, wmark = wmarks[si]
                    t.wait_ge([self.sA, self.sB][wslot], wmark)
                    if si >= 1:
                        t.wait_ge(self.sS, ss_end[si - 1])
                    if l < 2:
                        src = self.xnsb if l == 0 else h1
                        soff = HALO if l == 0 else 0
                        swid = TE if l == 0 else T
                        for i in range(32):
                            oc, tb = i // 4, i % 4
                            bank = self.pb[i % 4]
                            if i >= 4:
                                t.wait_ge(self.sS, ss_base[si] + (i - 4) + 1)
                            for dc in range(NCH):
                                mm = t.matmul(
                                    bank[:, 0:512],
                                    wsb[:, wslot * NCH * D + dc * D + oc * 128: wslot * NCH * D + dc * D + (oc + 1) * 128],
                                    src[:, dc * swid + soff + tb * 512: dc * swid + soff + tb * 512 + 512],
                                    start=(dc == 0), stop=(dc == NCH - 1))
                            mm.then_inc(self.sP, 1)
                    else:
                        t.wait_ge(self.sG, gmark[e])
                        for tc in range(16):
                            bk = (tc % 2) * 2
                            if tc >= 2:
                                t.wait_ge(self.sS, ss_base[si] + (tc - 2) + 1)
                            for oc in range(NCH):
                                for hf in range(2):
                                    t.matmul(
                                        self.pb[bk + hf][:, 0:512],
                                        h2[:, oc * T + tc * 128:oc * T + (tc + 1) * 128],
                                        wsb[:, wslot * NCH * D + oc * D + hf * 512: wslot * NCH * D + oc * D + hf * 512 + 512],
                                        start=(oc == 0), stop=False)
                            for hf in range(2):
                                mm = t.matmul(self.pb[bk + hf][:, 0:512],
                                              self.ONES128()[0:1, :],
                                              AP(b3s, (e % 2) * D + hf * 512,
                                                 [[2 * D, 1], [1, 512]]),
                                              start=False, stop=True)
                            mm.then_inc(self.sP, 1)
                        if e >= 1:
                            t.wait_ge(self.sV, vcopy_mark[e - 1])
                        for hf in range(2):
                            bnk = self.pf if hf == 0 else self.pb[4]
                            for tc in range(16):
                                t.wait_ge(self.sS, ss_base[si] + tc + 1)
                                mm = t.matmul(
                                    bnk[0:1, 0:512],
                                    self.wcolsb[:, tc * 8 + e: tc * 8 + e + 1],
                                    ytm[:, tc * D + hf * 512: tc * D + hf * 512 + 512],
                                    start=(tc == 0), stop=(tc == 15))
                        mm.then_inc(self.sP, 1)

            @blk.scalar
            def _(sc):
                for si, (e, l) in enumerate(steps):
                    if l < 2:
                        if l == 0:
                            sc.wait_ge(self.sD, bmark[e])
                        dst = h1 if l == 0 else h2
                        bo = (e % 2) * 16 + (0 if l == 0 else 8)
                        for i in range(32):
                            oc, tb = i // 4, i % 4
                            bank = self.pb[i % 4]
                            sc.wait_ge(self.sP, sp_base[si] + i + 1)
                            sc.activation(
                                dst[:, oc * T + tb * 512: oc * T + tb * 512 + 512],
                                bank[:, 0:512], AF.Relu,
                                bias=bcl[:, bo + oc:bo + oc + 1]
                            ).then_inc(self.sS, 1)
                    else:
                        for tc in range(16):
                            bk = (tc % 2) * 2
                            sc.wait_ge(self.sP, sp_base[si] + tc + 1)
                            sc.activation(
                                ytm[:, tc * D: tc * D + 512],
                                self.pb[bk][:, 0:512], AF.Relu,
                                bias=self.CCOL(5))
                            sc.activation(
                                ytm[:, tc * D + 512: tc * D + 1024],
                                self.pb[bk + 1][:, 0:512], AF.Relu,
                                bias=self.CCOL(5)).then_inc(self.sS, 1)

            @blk.vector
            def _(v):
                for e in range(NE):
                    v.wait_ge(self.sP, vp_pmark[e])
                    if e >= 2:
                        v.wait_ge(self.sW, vpsdma_mark[e - 2])
                    vo = (e % 2) * D
                    v.tensor_copy(vps[0:1, vo:vo + 512],
                                  self.pf[0:1, 0:512]); v.drain()
                    v.tensor_copy(vps[0:1, vo + 512:vo + 1024],
                                  self.pb[4][0:1, 0:512]).then_inc(self.sV, 1)
                    v.drain()

        with nc.Block() as blk:
            @blk.gpsimd
            def _(g):
                g.wait_ge(self.sW, vpsdma_mark[NE - 1])
                self.sC_n += 1
                g.collective_compute(
                    "AllReduce", ALU.add, replica_groups=[list(range(8))],
                    ins=[self.cc_vp_i.ap().opt()], outs=[self.cc_vp_o.ap().opt()],
                ).then_inc(self.sC)
                g.wait_ge(self.sC, self.sC_n)
        with nc.Block() as blk:
            @blk.sync
            def _(s):
                self.dbg_write(s, "dbg_v", self.cc_vp_o.ap())
                self.dbg_write(s, "dbg_vp", self.cc_vp_i.ap())
        scM.__exit__(None, None, None)

    def p9_assemble(self):
        nc = self.nc
        sc_ = self.scope(); sc_.__enter__()
        vsb = sc_.sbuf("vsb", [8, D], BF16)
        x1t = sc_.sbuf("x1t", [128, 2 * 512], F32)
        a1t = sc_.sbuf("a1t", [128, 2 * 512], BF16)
        ot = sc_.sbuf("ot", [128, 2 * 512], F32)
        with nc.Block() as blk:
            @blk.gpsimd
            def _(g):
                self.dst(g, "sG", vsb[:, :], self.cc_vp_o.ap())
                g.wait_ge(self.sG, self.sG_n)
        tiles = [(oc, tb) for oc in range(NCH) for tb in range(4)]
        with nc.Block() as blk:
            p0 = self.sP_n
            v0 = self.sV_n
            o0 = self.sO_n
            lmarks = []
            omarks = []

            @blk.sync
            def _(s):
                for i, (oc, tb) in enumerate(tiles):
                    sl_ = ["sE", "sF"][i % 2]
                    if i >= 2:
                        s.wait_ge(self.sV, v0 + (i - 2) + 1)
                    self.dst(s, sl_, x1t[:, (i % 2) * 512:(i % 2) * 512 + 512],
                             AP(self.x1_park, oc * 128 * T + tb * 512,
                                [[T, 128], [1, 512]]))
                    lmarks.append(self.dst(
                        s, sl_,
                        AP(a1t, (i % 2) * 512, [[2 * 512, 128], [128, 4], [1, 128]]),
                        AP(self.a1_park, tb * 4 * 128 * D + oc * 128,
                           [[D, 128], [128 * D, 4], [1, 128]])))

            @blk.tensor
            def _(t):
                for i, (oc, tb) in enumerate(tiles):
                    bk = i % 2
                    if i >= 1:
                        t.wait_ge(self.sV, v0 + i)
                    t.wait_ge([self.sE, self.sF][i % 2], lmarks[i])
                    t.matmul(self.pb[bk][:, 0:512],
                             vsb[:, oc * 128:(oc + 1) * 128],
                             self.indb[:, tb * 512:(tb + 1) * 512],
                             start=True, stop=True)
                    for j in range(4):
                        tr = t.transpose(
                            self.pt[:, j * 128:(j + 1) * 128],
                            a1t[:, (i % 2) * 512 + j * 128:(i % 2) * 512 + (j + 1) * 128],
                            self.ID())
                    tr.then_inc(self.sP, 1)
            self.sP_n += len(tiles)

            @blk.scalar
            def _(sc):
                for i, (oc, tb) in enumerate(tiles):
                    bk = i % 2
                    sc.wait_ge(self.sV, v0 + i + 1)
                    omarks.append(self.dst(
                        sc, ["sZ", "sZ2"][i % 2],
                        AP(self.out, oc * 128 * T + tb * 512, [[T, 128], [1, 512]]),
                        ot[:, bk * 512:bk * 512 + 512]))
                sc.wait_ge(self.sZ, self.sZ_n)
                sc.wait_ge(self.sZ2, self.sZ2_n)

            @blk.vector
            def _(v):
                for i, (oc, tb) in enumerate(tiles):
                    bk = i % 2
                    v.wait_ge(self.sP, p0 + i + 1)
                    if i >= 2:
                        v.wait_ge([self.sZ, self.sZ2][i % 2], omarks[i - 2])
                    if Prog.P9MODE == "full":
                        v.scalar_tensor_tensor(ot[:, bk * 512:bk * 512 + 512],
                                               self.pb[bk][:, 0:512], 0.0,
                                               x1t[:, (i % 2) * 512:(i % 2) * 512 + 512],
                                               op0=ALU.add, op1=ALU.add); v.drain()
                        v.tensor_tensor(ot[:, bk * 512:bk * 512 + 512],
                                        ot[:, bk * 512:bk * 512 + 512],
                                        self.pt[:, 0:512], op=ALU.add
                                        ).then_inc(self.sV, 1); v.drain()
                    elif Prog.P9MODE == "a1":
                        v.tensor_copy(ot[:, bk * 512:bk * 512 + 512],
                                      self.pt[:, 0:512]).then_inc(self.sV, 1); v.drain()
                    elif Prog.P9MODE == "moe":
                        v.tensor_copy(ot[:, bk * 512:bk * 512 + 512],
                                      self.pb[bk][:, 0:512]).then_inc(self.sV, 1); v.drain()
                    else:
                        v.tensor_copy(ot[:, bk * 512:bk * 512 + 512],
                                      x1t[:, (i % 2) * 512:(i % 2) * 512 + 512]
                                      ).then_inc(self.sV, 1); v.drain()
            self.sV_n += len(tiles)
        sc_.__exit__(None, None, None)

    def p_final(self, parked=False):
        nc = self.nc
        with nc.Block() as blk:
            pk = parked

            @blk.sync
            def _(s):
                if pk:
                    self.dst(s, "sO", self.out.ap(), self.x1_park.ap())
                else:
                    self.dst(s, "sO",
                             AP(self.out, 0, [[T, 128], [128 * T, NCH], [1, T]]),
                             AP(self.xsb, HALO, [[NCH * TE, 128], [TE, NCH], [1, T]]))
                s.wait_ge(self.sO, self.sO_n)




import ml_dtypes

LN_ALPHA = np.log(np.float64(0.99))


def stage_consts():
    import ml_dtypes
    c = {}
    c["id128"] = np.eye(128, dtype=ml_dtypes.bfloat16)
    c["idf1"] = np.ones((1, 1), dtype=np.float32)
    cc = np.zeros((128, 8), dtype=np.float32)
    cc[:, 0] = 1e-5; cc[:, 1] = 1e-8; cc[:, 2] = 0.5; cc[:, 3] = 1.0; cc[:, 4] = -1.0
    c["ccol"] = cc
    c["ones_col"] = np.ones((128, 1), dtype=ml_dtypes.bfloat16)
    c["ones128"] = np.ones((128, 128), dtype=ml_dtypes.bfloat16)
    c["idf8"] = np.eye(8, dtype=np.float32)
    c["idb8"] = np.eye(8, dtype=ml_dtypes.bfloat16)
    c["idf128"] = np.eye(128, dtype=np.float32)
    c["ones_rf"] = np.ones((1, 128), dtype=np.float32)
    kk, qq = np.indices((128, 128))
    band = (kk - qq >= 0) & (kk - qq <= 31)
    c["maskbigA"] = np.where(band, 0.0, 1e4).astype(np.float32)
    kk2, qq2 = np.indices((31, 128))
    band2 = ((kk2 + 128) - qq2 >= 0) & ((kk2 + 128) - qq2 <= 31)
    c["maskbigB"] = np.where(band2, 0.0, 1e4).astype(np.float32)
    kk3, qq3 = np.indices((128, 128))
    c["tri"] = (qq3 >= kk3).astype(ml_dtypes.bfloat16)
    p_, tc_ = np.indices((128, 16))
    jl = (tc_ * 128 + p_).astype(np.float64)
    hi, lo = hilo(-jl * LN_ALPHA)
    c["dk_hi"], c["dk_lo"] = hi, lo
    hi, lo = hilo(jl * LN_ALPHA)
    c["di_hi"], c["di_lo"] = hi, lo
    return c


def stage_core_consts(core):
    b, p = core // 4, core % 4
    out = {}
    delta = np.zeros((128, 4), dtype=np.float64)
    mask = np.zeros((128, 4), dtype=np.float32)
    for q in range(4):
        d = p - q
        if d >= 1:
            delta[:, q] = 2048.0 * d * LN_ALPHA
            mask[:, q] = 1.0
    hi, lo = hilo(delta)
    out["drm_hi"], out["drm_lo"], out["drm_mk"] = hi, lo, mask
    bs = np.zeros((8, 1), dtype=np.float32)
    bs[4 * b: 4 * b + 4, 0] = 1.0
    out["bsel"] = bs
    b2 = np.zeros((2, 1), dtype=np.float32)
    b2[b, 0] = 1.0
    out["b2"] = b2
    bs2 = np.zeros((128, 2), dtype=np.float32)
    bs2[:, b] = 1.0
    out["bsel2"] = bs2
    return out


def stage_weights(inp):
    w = {}
    eq = np.ascontiguousarray(inp["L0_attn_enc_q"])
    w["encq"] = np.ascontiguousarray(eq.reshape(NH, NCH, 128, HD))
    ev = np.ascontiguousarray(inp["L0_attn_enc_v"])
    w["encv"] = np.ascontiguousarray(
        ev.transpose(1, 0, 2).reshape(NCH, 128, NH * HD))
    w["dec"] = np.ascontiguousarray(
        np.asarray(inp["L0_attn_decoder"]).reshape(NCH, 128, D))
    w["heb"] = np.ascontiguousarray(inp["L0_attn_hebbian"])
    w["mw1"] = np.ascontiguousarray(
        np.asarray(inp["L0_attn_m_w1"]).reshape(NCH, 128, 512))
    w["mw2"] = np.ascontiguousarray(
        np.asarray(inp["L0_attn_m_w2"]).reshape(4, 128, D))
    w["mb1"] = np.ascontiguousarray(np.asarray(inp["L0_attn_m_b1"]).reshape(4, 128).T)
    w["mb2"] = np.ascontiguousarray(np.asarray(inp["L0_attn_m_b2"]).reshape(NCH, 128).T)
    w["bmet"] = np.ascontiguousarray(np.asarray(inp["L0_attn_base_metric"]).T)
    w0 = np.asarray(inp["L0_ffn_weight"])
    w["w0T"] = np.ascontiguousarray(w0.T.reshape(NCH, 128, D))
    w["b0c"] = np.ascontiguousarray(np.asarray(inp["L0_ffn_bias"]).reshape(NCH, 128).T)
    wqkv = np.asarray(inp["L1_attn_W_qkv"])
    w["wqkvT"] = np.ascontiguousarray(wqkv.T)
    bqkv = np.asarray(inp["L1_attn_b_qkv"])
    w["bq_c"] = np.ascontiguousarray(bqkv[0:D].reshape(NCH, 128).T)
    w["bk_c"] = np.ascontiguousarray(bqkv[D:2 * D].reshape(NCH, 128).T)
    w["bkv_r"] = np.ascontiguousarray(bqkv[D:3 * D].reshape(1, 2 * D))
    w["gw"] = np.ascontiguousarray(np.asarray(inp["L1_ffn_gate_w"]).reshape(NCH, 128, NE))
    w["gb"] = np.ascontiguousarray(np.asarray(inp["L1_ffn_gate_b"]).reshape(NE, 1))
    w["w1"] = np.ascontiguousarray(np.asarray(inp["L1_ffn_w1"]).reshape(NE, NCH, 128, D))
    w["w2"] = np.ascontiguousarray(np.asarray(inp["L1_ffn_w2"]).reshape(NE, NCH, 128, D))
    w["w3"] = np.ascontiguousarray(np.asarray(inp["L1_ffn_w3"]).reshape(NE, NCH, 128, D))
    w["b1c"] = np.ascontiguousarray(
        np.asarray(inp["L1_ffn_b1"]).reshape(NE, NCH, 128).transpose(0, 2, 1))
    w["b2c"] = np.ascontiguousarray(
        np.asarray(inp["L1_ffn_b2"]).reshape(NE, NCH, 128).transpose(0, 2, 1))
    w["b3r"] = np.ascontiguousarray(np.asarray(inp["L1_ffn_b3"]).reshape(NE, 1, D))
    return w


def stage_x(inp, core):
    b, p = core // 4, core % 4
    x = np.asarray(inp["L0_attn_x"])[b]
    t0 = p * T
    if p == 0:
        halo = np.zeros((HALO, D), dtype=np.float32)
    else:
        halo = x[t0 - HALO: t0]
    sl = np.concatenate([halo, x[t0: t0 + T]], axis=0)
    fm = np.ascontiguousarray(sl.T.reshape(NCH, 128, TE))
    return fm


def stage_all(inp):
    shared = stage_weights(inp)
    consts = stage_consts()
    maps = []
    for core in range(8):
        m = {}
        m.update(shared)
        m.update(consts)
        m.update(stage_core_consts(core))
        m["x_fm"] = stage_x(inp, core)
        maps.append(m)
    return maps


def gather_out(results):
    full = np.zeros((B, TFULL, D), dtype=np.float32)
    for core, r in enumerate(results):
        b, p = core // 4, core % 4
        o = np.asarray(r["out"])
        full[b, p * T:(p + 1) * T] = o.reshape(D, T).T
    return full


_NC_CACHE = {}


def _build_nc():
    if "nc" not in _NC_CACHE:
        prog = Prog(debug={}, phases=99)
        _NC_CACHE["nc"] = prog.build()
    return _NC_CACHE["nc"]


def kernel(**inputs):
    nc = _build_nc()
    maps = stage_all(inputs)
    res = run_bass_kernel_spmd(nc, maps, core_ids=list(range(8)))
    return gather_out(res.results)
